# version 01
# baseline (speedup 1.0000x reference)
import numpy as np
from contextlib import ExitStack

import concourse.bass as bass
import concourse.mybir as mybir
from concourse.bass_utils import run_bass_kernel_spmd

F32 = mybir.dt.float32
BF16 = mybir.dt.bfloat16
AF = mybir.ActivationFunctionType
ALU = mybir.AluOpType
AXX = mybir.AxisListType.X

P = 128
D = 2048
T = 2048
NK = D // P
NTILE = T // P
PROJ = 4640
CONV = 2560
NCCH = CONV // P
HEADS = 32
HD = 64
STATE = 256
CHUNK = 64
EPS = 1e-5
E_RT = 8
I_RT = 1408
I_SH = 2816
KSTACK = E_RT * I_RT + I_SH
NKST = KSTACK // P
INV_D = 1.0 / D

ENGINES = ("tensor", "vector", "scalar", "gpsimd", "sync")
SEMKEYS = ("pe", "dve", "act", "pool", "qs", "qa", "qg")
DMA_SEM = {"sync": "qs", "scalar": "qa", "gpsimd": "qg"}


class Prog:

    def __init__(self):
        self.ops = {e: [] for e in ENGINES}
        self.nid = 0
        self.used = set()
        self.last = {}

    def _new_id(self):
        self.nid += 1
        return self.nid

    def _op(self, eng, build, waits, semkey, is_dma):
        opid = self._new_id()
        wl = []
        for t in waits:
            if t is None:
                continue
            sk, oid = t
            self.used.add((sk, oid))
            wl.append((sk, oid))
        if is_dma:
            self.used.add((semkey, opid))
        self.ops[eng].append((wl, build, semkey, opid, is_dma))
        self.last[semkey] = (semkey, opid)
        return (semkey, opid)

    def fence(self, keys=("pe", "dve", "act", "pool")):
        return [self.last[k] for k in keys if k in self.last]

    def fence_all(self):
        return [self.last[k] for k in SEMKEYS if k in self.last]

    def t(self, build, waits=()):
        return self._op("tensor", build, waits, "pe", False)

    def v(self, build, waits=()):
        return self._op("vector", build, waits, "dve", False)

    def s(self, build, waits=()):
        return self._op("scalar", build, waits, "act", False)

    def g(self, build, waits=()):
        return self._op("gpsimd", build, waits, "pool", False)

    def dma(self, eng, build, waits=()):
        return self._op(eng, build, waits, DMA_SEM[eng], True)

    def finalize(self, nc, block, sems):
        marked = {k: [] for k in SEMKEYS}
        for eng in ENGINES:
            for (wl, build, sk, opid, is_dma) in self.ops[eng]:
                if (sk, opid) in self.used:
                    marked[sk].append(opid)
        rank = {}
        for sk in SEMKEYS:
            for i, oid in enumerate(sorted(marked[sk])):
                rank[(sk, oid)] = i + 1

        def section(eng):
            lastwait = {}

            def body(h):
                for (wl, build, sk, opid, is_dma) in self.ops[eng]:
                    for (wsk, woid) in wl:
                        step = 16 if wsk in ("qs", "qa", "qg") else 1
                        val = rank[(wsk, woid)] * step
                        if lastwait.get(wsk, 0) < val:
                            h.wait_ge(sems[wsk], val)
                            lastwait[wsk] = val
                    ins = build(h)
                    if (sk, opid) in self.used:
                        step = 16 if sk in ("qs", "qa", "qg") else 1
                        ins.then_inc(sems[sk], step)
            return body

        for eng in ENGINES:
            if self.ops[eng]:
                getattr(block, eng)(section(eng))


def apx(base_ap, free_dims, extra_off=0):
    return bass.AP(base_ap.tensor, base_ap.offset + extra_off,
                   [list(base_ap.ap[0])] + [list(d) for d in free_dims])


def build_program(debug=False):
    nc = bass.Bass("TRN2", target_bir_lowering=False)
    pr = Prog()

    def din(name, shape, dt=F32):
        return nc.dram_tensor(name, shape, dt, kind="ExternalInput").ap()

    x = din("x", [T, D])
    w_in = din("w_in", [D, PROJ])
    conv_w = din("conv_w", [CONV, 4])
    conv_b2 = din("conv_b2", [NCCH, P])
    dt_bias = din("dt_bias", [HEADS, 1])
    a_log = din("a_log", [HEADS, 1])
    d_skip = din("d_skip", [1, HEADS])
    normw2 = din("normw2", [P, NK])
    gnormw2 = din("gnormw2", [P, NK])
    w_out = din("w_out", [D, D])
    wo_w = din("wo_w", [D, D])
    gate_w = din("gate_w", [E_RT, D])
    w1 = din("w1", [E_RT, D, I_RT])
    w3 = din("w3", [E_RT, D, I_RT])
    w2 = din("w2", [E_RT, I_RT, D])
    sw1 = din("sw1", [D, I_SH])
    sw3 = din("sw3", [D, I_SH])
    sw2 = din("sw2", [I_SH, D])
    iden_f = din("iden_f", [P, P])
    tri_cum = din("tri_cum", [P, P])
    ut_strict = din("ut_strict", [P, P])
    e63 = din("e63", [P, P])
    e127 = din("e127", [P, P])
    ones1 = din("ones1", [1, P])
    esel = din("esel", [HEADS, HEADS * P])

    z_dram = nc.dram_tensor("z_dram", [T, D], F32, kind="Internal").ap()
    xbc_dram = nc.dram_tensor("xbc_dram", [CONV, T], F32, kind="Internal").ap()
    convq_dram = nc.dram_tensor("convq_dram", [4, CONV, 512], BF16, kind="Internal").ap()
    actT_dram = nc.dram_tensor("actT_dram", [KSTACK, T], BF16, kind="Internal").ap()
    ffn_dram = nc.dram_tensor("ffn_dram", [T, D], F32, kind="Internal").ap()
    srs_dram = nc.dram_tensor("srs_dram", [T], F32, kind="Internal").ap()
    m_dram = nc.dram_tensor("m_dram", [P, P], F32, kind="Internal").ap()
    u_dram = nc.dram_tensor("u_dram", [D, T], BF16, kind="Internal").ap()
    out = nc.dram_tensor("out", [T, D], F32, kind="ExternalOutput").ap()
    if debug:
        dbg_z = nc.dram_tensor("dbg_z", [T, D], F32, kind="ExternalOutput").ap()
        dbg_xbc = nc.dram_tensor("dbg_xbc", [CONV, T], F32, kind="ExternalOutput").ap()
        dbg_dt = nc.dram_tensor("dbg_dt", [HEADS, T], F32, kind="ExternalOutput").ap()
        dbg_xnf = nc.dram_tensor("dbg_xnf", [P, NK * T], F32, kind="ExternalOutput").ap()
        dbg_srs = nc.dram_tensor("dbg_srs", [P, NTILE], F32, kind="ExternalOutput").ap()
        dbg_stat = nc.dram_tensor("dbg_stat", [P, 16], F32, kind="ExternalOutput").ap()
        dbg_ffn = nc.dram_tensor("dbg_ffn", [T, D], F32, kind="ExternalOutput").ap()
        dbg_gfm = nc.dram_tensor("dbg_gfm", [P, NK * T], F32, kind="ExternalOutput").ap()
        dbg_m = nc.dram_tensor("dbg_m", [P, P], F32, kind="ExternalOutput").ap()
        dbg_cq = nc.dram_tensor("dbg_cq", [4 * CONV, 512], F32, kind="ExternalOutput").ap()
        dbg_dtv = nc.dram_tensor("dbg_dtv", [HEADS, 2 * T], F32, kind="ExternalOutput").ap()
        dbg_acr = nc.dram_tensor("dbg_acr", [P, P], F32, kind="ExternalOutput").ap()
        dbg_ssm = nc.dram_tensor("dbg_ssm", [P, 160], F32, kind="ExternalOutput").ap()
        dbg_yfin = nc.dram_tensor("dbg_yfin", [P, T], F32, kind="ExternalOutput").ap()
        dbg_xtw = nc.dram_tensor("dbg_xtw", [P, T], F32, kind="ExternalOutput").ap()
        dbg_S = nc.dram_tensor("dbg_S", [P, 2 * T], F32, kind="ExternalOutput").ap()
        dbg_act = nc.dram_tensor("dbg_act", [KSTACK, T], F32, kind="ExternalOutput").ap()
        dbg_zs = nc.dram_tensor("dbg_zs", [P, T], F32, kind="ExternalOutput").ap()
        dbg_gt = nc.dram_tensor("dbg_gt", [P, T], F32, kind="ExternalOutput").ap()
        dbg_gst = nc.dram_tensor("dbg_gst", [P, 4], F32, kind="ExternalOutput").ap()
        dbg_yg = nc.dram_tensor("dbg_yg", [P, T], F32, kind="ExternalOutput").ap()

    es = ExitStack()

    def mkSB(stack):
        return lambda n, s, d=F32: stack.enter_context(nc.sbuf_tensor(n, s, d))

    SB = mkSB(es)
    PSUM = lambda n, s, d=F32: es.enter_context(nc.psum_tensor(n, s, d))

    pbig = PSUM("pbig", [P, 2048])
    pdub = PSUM("pdub", [P, 1024])
    pbf = PSUM("pbf", [P, 1024], BF16)
    pmix = PSUM("pmix", [P, 512])

    sb_iden_f = SB("c_idenf", [P, P])
    sb_iden_b = SB("c_idenb", [P, P], BF16)
    sb_tri = SB("c_tri", [P, P])
    sb_ut = SB("c_ut", [P, P])
    sb_e63 = SB("c_e63", [P, P])
    sb_e127 = SB("c_e127", [P, P])
    sb_ones = SB("c_ones", [1, P])
    sb_normw = SB("c_normw", [P, NK])
    sb_gnormw = SB("c_gnormw", [P, NK])
    srs = SB("srs", [P, NTILE])
    stat = SB("stat", [P, 16])
    cpool = SB("cpool", [P, 4])
    dt_fm = SB("dt_fm", [HEADS, T])

    sems = {}
    for sk in SEMKEYS:
        sems[sk] = es.enter_context(nc.semaphore("sem_" + sk))

    c_tok = []
    for dst, src in [(sb_iden_f, iden_f), (sb_tri, tri_cum),
                     (sb_ut, ut_strict), (sb_e63, e63), (sb_e127, e127),
                     (sb_ones, ones1), (sb_normw, normw2), (sb_gnormw, gnormw2)]:
        c_tok.append(pr.dma("sync", lambda h, d=dst, s=src: h.dma_start(out=d[:], in_=s)))
    c_tok.append(pr.dma("gpsimd", lambda h: h.dma_start(out=sb_iden_b[:], in_=iden_f)))
    tok_consts = c_tok[-2]
    tok_consts_b = c_tok[-1]
    pr.v(lambda h: h.memset(cpool[:, 0:1], EPS))
    pr.v(lambda h: h.memset(cpool[:, 2:3], -1e30))
    tok_cpool = pr.v(lambda h: h.memset(cpool[:, 1:2], INV_D))
    c_eps = cpool[:, 0:1]
    c_invd = cpool[:, 1:2]
    c_n30 = cpool[:, 2:3]

    s_xnf = ExitStack()
    xnf = mkSB(s_xnf)("xnf", [P, NK * T], BF16)

    s12 = ExitStack()
    SB12 = mkSB(s12)
    junk2k = SB12("junk2k", [P, 2048])
    xt = [SB12("xt0", [P, D]), SB12("xt1", [P, D])]
    xnt = [SB12("xnt0", [P, D], BF16), SB12("xnt1", [P, D], BF16)]

    tok_xt_free = [None, None]
    tok_xnt_free = [None, None]
    tok_pbf_free = None
    tok_srs_last = None

    for ti in range(NTILE):
        pa = ti % 2
        t_dmax = pr.dma(
            "sync", lambda h, pa=pa, ti=ti: h.dma_start(
                out=xt[pa][:], in_=x[ti * P:(ti + 1) * P, :]),
            waits=[tok_xt_free[pa]])
        w0 = [t_dmax]
        t_sum = pr.v(lambda h, pa=pa: h.reduce_sum(stat[:, 0:1], xt[pa][:], axis=AXX),
                     waits=w0)
        t_sq = pr.s(lambda h, pa=pa: h.activation(
            junk2k[:], xt[pa][:], AF.Square, accum_out=stat[:, 1:2]), waits=w0)
        t_mean = pr.v(lambda h: h.tensor_scalar_mul(stat[:, 2:3], stat[:, 0:1], INV_D),
                      waits=[t_sum])
        t_m2 = pr.v(lambda h: h.tensor_mul(stat[:, 3:4], stat[:, 2:3], stat[:, 2:3]),
                    waits=[t_mean])
        t_v0 = pr.v(lambda h: h.tensor_scalar_mul(stat[:, 4:5], stat[:, 1:2], INV_D),
                    waits=[t_sq])
        t_var = pr.v(lambda h: h.tensor_sub(stat[:, 4:5], stat[:, 4:5], stat[:, 3:4]),
                     waits=[t_v0, t_m2])
        t_lnv = pr.s(lambda h: h.activation(stat[:, 5:6], stat[:, 4:5], AF.Ln,
                                            bias=c_eps),
                     waits=[t_var, tok_cpool])
        t_rstd = pr.s(lambda h: h.activation(stat[:, 6:7], stat[:, 5:6], AF.Exp,
                                             scale=-0.5), waits=[t_lnv])
        t_negmb = pr.v(lambda h: h.scalar_tensor_tensor(
            stat[:, 7:8], stat[:, 2:3], -1.0, stat[:, 6:7], ALU.mult, ALU.mult),
            waits=[t_rstd, t_mean])
        t_msq = pr.s(lambda h, pa=pa: h.activation(
            junk2k[:], xt[pa][:], AF.Square, bias=stat[:, 7:8], scale=stat[:, 6:7],
            accum_out=stat[:, 8:9]), waits=[t_negmb])
        t_msqm = pr.v(lambda h: h.tensor_scalar_mul(stat[:, 9:10], stat[:, 8:9], INV_D),
                      waits=[t_msq])
        t_l2 = pr.s(lambda h: h.activation(stat[:, 10:11], stat[:, 9:10], AF.Ln,
                                           bias=c_eps), waits=[t_msqm])
        t_rs2 = pr.s(lambda h, ti=ti: h.activation(
            srs[:, ti:ti + 1], stat[:, 10:11], AF.Exp, scale=-0.5), waits=[t_l2])
        tok_srs_last = t_rs2
        t_comb = pr.v(lambda h: h.tensor_copy(stat[:, 12:13], stat[:, 6:7]),
                      waits=[t_rstd])
        t_negc = pr.v(lambda h: h.scalar_tensor_tensor(
            stat[:, 13:14], stat[:, 2:3], -1.0, stat[:, 12:13], ALU.mult, ALU.mult),
            waits=[t_comb])
        tok_xt_free[pa] = pr.v(lambda h, pa=pa: h.tensor_scalar(
            xnt[pa][:], xt[pa][:], stat[:, 12:13], stat[:, 13:14],
            ALU.mult, ALU.add),
            waits=[t_negc, tok_xnt_free[pa]])
        for grp in range(2):
            tts = []
            for j in range(8):
                k = grp * 8 + j
                tts.append(pr.t(lambda h, pa=pa, k=k, j=j: h.transpose(
                    pbf[:, j * P:(j + 1) * P], xnt[pa][:, k * P:(k + 1) * P],
                    sb_iden_b[:]),
                    waits=[tok_xt_free[pa], tok_pbf_free, tok_consts,
                           tok_consts_b]))
            off = (grp * 8) * T + ti * P
            tok_pbf_free = pr.v(lambda h, off=off: h.tensor_copy(
                apx(xnf[:], [[T, 8], [1, P]], extra_off=off),
                apx(pbf[:], [[P, 8], [1, P]])), waits=[tts[-1]])
            if grp == 1:
                tok_xnt_free[pa] = tok_pbf_free

    wst32 = SB12("wst32", [P, NK * 512])
    wstbf = [SB12("wstbf0", [P, NK * 512], BF16), SB12("wstbf1", [P, NK * 512], BF16)]
    ostage = [SB12("ostage%d" % i, [P, 512]) for i in range(4)]
    srsT = SB12("srsT", [NTILE, P])
    rs2_b = SB12("rs2_b", [P, 2048])
    srs_row = SB12("srs_row", [1, 2048])

    w_in_r = w_in.rearrange("(k p) c -> p k c", p=P)

    t_tr = pr.t(lambda h: h.transpose(pmix[0:NTILE, 0:P], srs[:], sb_iden_f[:]),
                waits=[tok_srs_last, tok_consts])
    t_srs_ev = pr.v(lambda h: h.tensor_copy(srsT[:], pmix[0:NTILE, 0:P]), waits=[t_tr])
    t_srs_out = pr.dma("scalar", lambda h: h.dma_start(
        out=srs_dram.rearrange("(a b) -> a b", a=NTILE), in_=srsT[:]),
        waits=[t_srs_ev])
    t_srs_in = pr.dma("sync", lambda h: h.dma_start(
        out=srs_row[:], in_=srs_dram.rearrange("(a b) -> a b", a=1)),
        waits=[t_srs_out])
    k1s = []
    for j in range(4):
        k1s.append(pr.t(lambda h, j=j: h.matmul(
            pbig[:, j * 512:(j + 1) * 512], sb_ones[:],
            srs_row[:, j * 512:(j + 1) * 512], start=True, stop=True),
            waits=[t_srs_in]))
    t_rs2b = pr.v(lambda h: h.tensor_copy(rs2_b[:], pbig[:]), waits=[k1s[-1]])

    blocks = [(c0, 512, "z") for c0 in range(0, 2048, 512)] \
        + [(c0, 512, "xbc") for c0 in range(2048, 4608, 512)] \
        + [(4608, 32, "dt")]

    tok_cast = [None, None]
    tok_w32_free = None
    tok_ost_free = [None] * 4
    tok_psB = [None, None]
    ost_i = [0]

    def next_ost():
        i = ost_i[0] % 4
        ost_i[0] += 1
        return i

    for bi, (c0, C, kind) in enumerate(blocks):
        pa = bi % 2
        t_w32 = pr.dma("sync", lambda h, c0=c0, C=C: h.dma_start(
            out=apx(wst32[:], [[512, NK], [1, C]]),
            in_=w_in_r[:, :, c0:c0 + C]), waits=[tok_w32_free])
        casts = []
        for k in range(NK):
            casts.append(pr.v(lambda h, k=k, C=C, pa=pa: h.tensor_scalar_mul(
                wstbf[pa][:, k * C:(k + 1) * C], wst32[:, k * 512:k * 512 + C],
                sb_normw[:, k:k + 1]),
                waits=[t_w32, tok_cast[pa], tok_consts]))
        tok_w32_free = casts[-1]
        tok_cast[pa] = None
        last_use = None

        if kind == "z":
            for M in range(NTILE):
                pp = M % 2
                ps = pdub[:, pp * 512:(pp + 1) * 512]
                mms = []
                for k in range(NK):
                    mms.append(pr.t(lambda h, k=k, M=M, pa=pa, C=C, ps=ps:
                                    h.matmul(ps, xnf[:, k * T + M * P:k * T + (M + 1) * P],
                                             wstbf[pa][:, k * C:(k + 1) * C],
                                             start=(k == 0), stop=(k == NK - 1)),
                                    waits=[casts[-1], tok_psB[pp]]))
                last_use = mms[-1]
                oi = next_ost()
                t_ev = pr.v(lambda h, oi=oi, ps=ps, M=M: h.tensor_scalar_mul(
                    ostage[oi][:], ps, srs[:, M:M + 1]),
                    waits=[mms[-1], tok_ost_free[oi], t_rs2b])
                tok_psB[pp] = t_ev
                tok_ost_free[oi] = pr.dma(
                    "scalar", lambda h, oi=oi, M=M, c0=c0: h.dma_start(
                        out=z_dram[M * P:(M + 1) * P, c0:c0 + 512], in_=ostage[oi][:]),
                    waits=[t_ev])
        elif kind == "xbc":
            for PC in range(4):
                for TN in range(4):
                    pp = (PC * 4 + TN) % 2
                    ps = pdub[:, pp * 512:(pp + 1) * 512]
                    mms = []
                    for k in range(NK):
                        mms.append(pr.t(lambda h, k=k, PC=PC, TN=TN, pa=pa, C=C, ps=ps:
                                        h.matmul(ps,
                                                 wstbf[pa][:, k * C + PC * P:k * C + (PC + 1) * P],
                                                 xnf[:, k * T + TN * 512:k * T + (TN + 1) * 512],
                                                 start=(k == 0), stop=(k == NK - 1)),
                                        waits=[casts[-1], tok_psB[pp]]))
                    last_use = mms[-1]
                    oi = next_ost()
                    t_ev = pr.v(lambda h, oi=oi, ps=ps, TN=TN: h.tensor_tensor(
                        ostage[oi][:], ps, rs2_b[:, TN * 512:(TN + 1) * 512],
                        ALU.mult),
                        waits=[mms[-1], tok_ost_free[oi], t_rs2b])
                    tok_psB[pp] = t_ev
                    r0 = c0 - 2048 + PC * P
                    tok_ost_free[oi] = pr.dma(
                        "scalar", lambda h, oi=oi, r0=r0, TN=TN: h.dma_start(
                            out=xbc_dram[r0:r0 + P, TN * 512:(TN + 1) * 512],
                            in_=ostage[oi][:]),
                        waits=[t_ev])
        else:
            for TN in range(4):
                pp = TN % 2
                ps = pdub[0:HEADS, pp * 512:(pp + 1) * 512]
                mms = []
                for k in range(NK):
                    mms.append(pr.t(lambda h, k=k, TN=TN, pa=pa, C=C, ps=ps:
                                    h.matmul(ps,
                                             wstbf[pa][:, k * C:k * C + HEADS],
                                             xnf[:, k * T + TN * 512:k * T + (TN + 1) * 512],
                                             start=(k == 0), stop=(k == NK - 1)),
                                    waits=[casts[-1], tok_psB[pp]]))
                last_use = mms[-1]
                tok_psB[pp] = pr.v(lambda h, TN=TN, ps=ps: h.tensor_tensor(
                    dt_fm[:, TN * 512:(TN + 1) * 512], ps,
                    rs2_b[0:HEADS, TN * 512:(TN + 1) * 512], ALU.mult),
                    waits=[mms[-1], t_rs2b])
        tok_cast[pa] = last_use

    s12.close()
    barrier23 = [t for t in tok_ost_free if t is not None] + pr.fence_all()

    s3 = ExitStack()
    SB3 = mkSB(s3)
    gwf = SB3("gwf", [E_RT, D])
    gwT32 = SB3("gwT32", [P, NTILE * E_RT])
    gwhi = SB3("gwhi", [P, NTILE * E_RT], BF16)
    gwlo = SB3("gwlo", [P, NTILE * E_RT], BF16)
    glog = SB3("glog", [P, E_RT])
    ge1 = SB3("ge1", [P, E_RT])
    ge2 = SB3("ge2", [P, E_RT])
    glm = SB3("glm", [P, E_RT])
    gsc = SB3("gsc", [P, 8])
    m_tm = SB3("m_tm", [P, NTILE * E_RT])
    mT = SB3("mT", [P, P])
    me_b = SB3("me_b", [P, T])
    me_row = SB3("me_row", [1, T])
    w1blk = [SB3("w1blk0", [P, NK * 512], BF16), SB3("w1blk1", [P, NK * 512], BF16)]
    w3blk = [SB3("w3blk0", [P, NK * 512], BF16), SB3("w3blk1", [P, NK * 512], BF16)]
    astage = [SB3("ast%d" % i, [P, 512], BF16) for i in range(4)]
    shtile = [SB3("sht%d" % i, [P, 512]) for i in range(2)]
    a1tile = [SB3("a1t%d" % i, [P, 512]) for i in range(2)]

    t_gw = pr.dma("sync", lambda h: h.dma_start(out=gwf[:], in_=gate_w),
                  waits=barrier23)
    gts = []
    for k in range(NK):
        gts.append(pr.t(lambda h, k=k: h.transpose(
            pmix[:, k * 8:(k + 1) * 8], gwf[0:E_RT, k * P:(k + 1) * P],
            sb_iden_f[0:E_RT, 0:E_RT]), waits=[t_gw, t_srs_ev]))
    t_gwT = pr.v(lambda h: h.tensor_copy(gwT32[:], pmix[:, 0:NTILE * E_RT]),
                 waits=[gts[-1]])
    t_ghi = pr.v(lambda h: h.tensor_copy(gwhi[:], gwT32[:]), waits=[t_gwT])
    t_glo = pr.v(lambda h: h.tensor_tensor(gwlo[:], gwT32[:], gwhi[:],
                                           ALU.subtract), waits=[t_ghi])

    t_pm_free = t_gwT
    tok_m_last = None
    for M in range(NTILE):
        mms = []
        for k in range(NK):
            mms.append(pr.t(lambda h, k=k, M=M: h.matmul(
                pmix[:, 0:E_RT], xnf[:, k * T + M * P:k * T + (M + 1) * P],
                gwhi[:, k * 8:(k + 1) * 8], start=(k == 0), stop=False),
                waits=[t_glo, t_pm_free]))
        for k in range(NK):
            mms.append(pr.t(lambda h, k=k, M=M: h.matmul(
                pmix[:, 0:E_RT], xnf[:, k * T + M * P:k * T + (M + 1) * P],
                gwlo[:, k * 8:(k + 1) * 8], start=False, stop=(k == NK - 1)),
                waits=[t_glo]))
        t_lg = pr.v(lambda h: h.tensor_copy(glog[:], pmix[:, 0:E_RT]),
                    waits=[mms[-1], tok_m_last])
        t_pm_free = t_lg
        t_mx1 = pr.v(lambda h: h.reduce_max(gsc[:, 0:1], glog[:], axis=AXX),
                     waits=[t_lg])
        t_e1 = pr.v(lambda h: h.tensor_tensor(
            ge1[:], glog[:], apx(gsc[:, 0:1], [[0, E_RT]]), ALU.is_equal),
            waits=[t_mx1])
        t_l30 = pr.v(lambda h: h.tensor_scalar_mul(glm[:], ge1[:], c_n30),
                     waits=[t_e1, tok_cpool])
        t_lm = pr.v(lambda h: h.tensor_add(glm[:], glm[:], glog[:]), waits=[t_l30])
        t_mx2 = pr.v(lambda h: h.reduce_max(gsc[:, 1:2], glm[:], axis=AXX),
                     waits=[t_lm])
        t_e2 = pr.v(lambda h: h.tensor_tensor(
            ge2[:], glm[:], apx(gsc[:, 1:2], [[0, E_RT]]), ALU.is_equal),
            waits=[t_mx2])
        t_d = pr.v(lambda h: h.tensor_sub(gsc[:, 2:3], gsc[:, 1:2], gsc[:, 0:1]),
                   waits=[t_mx2])
        t_ed = pr.s(lambda h: h.activation(gsc[:, 3:4], gsc[:, 2:3], AF.Exp),
                    waits=[t_d])
        t_den = pr.v(lambda h: h.tensor_scalar_add(gsc[:, 4:5], gsc[:, 3:4], 1.0),
                     waits=[t_ed])
        t_r = pr.v(lambda h: h.reciprocal(gsc[:, 5:6], gsc[:, 4:5]), waits=[t_den])
        t_w2v = pr.v(lambda h: h.tensor_mul(gsc[:, 6:7], gsc[:, 3:4], gsc[:, 5:6]),
                     waits=[t_r])
        t_m1 = pr.v(lambda h: h.tensor_scalar_mul(ge1[:], ge1[:], gsc[:, 5:6]),
                    waits=[t_r, t_e1])
        tok_m_last = pr.v(lambda h, M=M: h.scalar_tensor_tensor(
            m_tm[:, M * 8:(M + 1) * 8], ge2[:], gsc[:, 6:7], ge1[:],
            ALU.mult, ALU.add), waits=[t_w2v, t_m1])

    t_mtr = pr.t(lambda h: h.transpose(pmix[:, 0:P], m_tm[:], sb_iden_f[:]),
                 waits=[tok_m_last, t_pm_free])
    t_mT = pr.v(lambda h: h.tensor_copy(mT[:], pmix[:, 0:P]), waits=[t_mtr])
    t_mout = pr.dma("scalar", lambda h: h.dma_start(out=m_dram, in_=mT[:]),
                    waits=[t_mT])

    w1r = w1.rearrange("e (k p) i -> e p k i", p=P)
    w3r = w3.rearrange("e (k p) i -> e p k i", p=P)
    sw1r = sw1.rearrange("(k p) i -> p k i", p=P)
    sw3r = sw3.rearrange("(k p) i -> p k i", p=P)

    tok_wblk = [None, None]
    tok_hps = [None, None]
    tok_gps = [None, None]
    tok_sh = [None, None]
    tok_a1 = [None, None]
    tok_ast = [None] * 4
    tok_meb = None
    t_meb_frees = [t_rs2b]
    it_i = [0]
    ast_i = [0]
    blk_i = [0]

    ph_slot = [(pdub[:], 0), (pbig[:], 0)]
    pg_slot = [(pdub[:], 512), (pbig[:], 512)]

    for ei in range(9):
        routed = ei < E_RT
        row0 = ei * I_RT if routed else E_RT * I_RT
        if routed:
            t_mer = pr.dma("sync", lambda h, ei=ei: h.dma_start(
                out=apx(me_row[:], [[P, NTILE], [1, P]]),
                in_=bass.AP(m_dram.tensor, ei * P,
                            [[1, 1], [E_RT * P, NTILE], [1, P]])),
                waits=[t_mout, tok_meb])
            kk = []
            for j in range(4):
                kk.append(pr.t(lambda h, j=j: h.matmul(
                    pbig[:, j * 512:(j + 1) * 512], sb_ones[:],
                    me_row[:, j * 512:(j + 1) * 512], start=True, stop=True),
                    waits=[t_mer, tok_hps[1], tok_gps[1]] + t_meb_frees))
            tok_meb = pr.v(lambda h: h.tensor_copy(me_b[:], pbig[:]), waits=[kk[-1]])
            t_meb_frees = [tok_meb]
        if routed:
            nclist = [(0, 512), (512, 512), (1024, 384)]
        else:
            nclist = [(c, min(512, I_SH - c)) for c in range(0, I_SH, 512)]
        for (c0, C) in nclist:
            pa = blk_i[0] % 2
            blk_i[0] += 1
            if routed:
                src1 = w1r[ei, :, :, c0:c0 + C]
                src3 = w3r[ei, :, :, c0:c0 + C]
            else:
                src1 = sw1r[:, :, c0:c0 + C]
                src3 = sw3r[:, :, c0:c0 + C]
            first2 = blk_i[0] <= 2
            t_s1 = pr.dma("gpsimd", lambda h, pa=pa, src1=src1, C=C: h.dma_start(
                out=apx(w1blk[pa][:], [[512, NK], [1, C]]), in_=src1),
                waits=[tok_wblk[pa]] + (barrier23 if first2 else []))
            t_s3 = pr.dma("gpsimd", lambda h, pa=pa, src3=src3, C=C: h.dma_start(
                out=apx(w3blk[pa][:], [[512, NK], [1, C]]), in_=src3),
                waits=[tok_wblk[pa]] + (barrier23 if first2 else []))
            last_mm = None
            for ic in range(C // P):
                for tn in range(4):
                    sp = it_i[0] % 2
                    it_i[0] += 1
                    pht, phc = ph_slot[sp]
                    pgt, pgc = pg_slot[sp]
                    hmms = []
                    gmms = []
                    for k in range(NK):
                        hmms.append(pr.t(
                            lambda h, k=k, ic=ic, tn=tn, pa=pa, pht=pht, phc=phc:
                            h.matmul(pht[:, phc:phc + 512],
                                     w1blk[pa][:, k * 512 + ic * P:k * 512 + (ic + 1) * P],
                                     xnf[:, k * T + tn * 512:k * T + (tn + 1) * 512],
                                     start=(k == 0), stop=(k == NK - 1)),
                            waits=[t_s1, tok_hps[sp]]
                            + ([t_meb_frees[0]] if routed else [])))
                    for k in range(NK):
                        gmms.append(pr.t(
                            lambda h, k=k, ic=ic, tn=tn, pa=pa, pgt=pgt, pgc=pgc:
                            h.matmul(pgt[:, pgc:pgc + 512],
                                     w3blk[pa][:, k * 512 + ic * P:k * 512 + (ic + 1) * P],
                                     xnf[:, k * T + tn * 512:k * T + (tn + 1) * 512],
                                     start=(k == 0), stop=(k == NK - 1)),
                            waits=[t_s3, tok_gps[sp]]))
                    last_mm = gmms[-1]
                    t_sh = pr.s(lambda h, sp=sp, pht=pht, phc=phc: h.activation(
                        shtile[sp][:], pht[:, phc:phc + 512], AF.Silu),
                        waits=[hmms[-1], tok_sh[sp]])
                    tok_hps[sp] = t_sh
                    ai = ast_i[0] % 4
                    ast_i[0] += 1
                    if routed:
                        t_a1 = pr.v(lambda h, sp=sp, pgt=pgt, pgc=pgc: h.tensor_mul(
                            a1tile[sp][:], shtile[sp][:], pgt[:, pgc:pgc + 512]),
                            waits=[t_sh, gmms[-1], tok_a1[sp]])
                        tok_gps[sp] = t_a1
                        tok_sh[sp] = t_a1
                        t_a2 = pr.v(lambda h, sp=sp, ai=ai, tn=tn: h.tensor_mul(
                            astage[ai][:], a1tile[sp][:],
                            me_b[:, tn * 512:(tn + 1) * 512]),
                            waits=[t_a1, tok_meb, tok_ast[ai]])
                        tok_a1[sp] = t_a2
                    else:
                        t_a2 = pr.v(lambda h, sp=sp, ai=ai, pgt=pgt, pgc=pgc:
                                    h.tensor_mul(astage[ai][:], shtile[sp][:],
                                                 pgt[:, pgc:pgc + 512]),
                                    waits=[t_sh, gmms[-1], tok_ast[ai]])
                        tok_gps[sp] = t_a2
                        tok_sh[sp] = t_a2
                    r0 = row0 + c0 + ic * P
                    tok_ast[ai] = pr.dma(
                        "scalar", lambda h, ai=ai, r0=r0, tn=tn: h.dma_start(
                            out=actT_dram[r0:r0 + P, tn * 512:(tn + 1) * 512],
                            in_=astage[ai][:]), waits=[t_a2])
            tok_wblk[pa] = last_mm

    barrier3b = [tok_ast[i] for i in range(4)] + [tok_meb] + pr.fence_all()
    if debug:
        t_dbg_xnf = pr.dma("gpsimd", lambda h: h.dma_start(out=dbg_xnf, in_=xnf[:]),
                           waits=pr.fence())
        barrier3b.append(t_dbg_xnf)
    s3.close()
    s_xnf.close()

    s4 = ExitStack()
    SB4 = mkSB(s4)
    y_t = SB4("y_t", [P, NTILE * 1024])
    atst = SB4("atst", [P, 16 * 1024], BF16)
    w2st = [SB4("w2st0", [P, 16 * 1024], BF16), SB4("w2st1", [P, 16 * 1024], BF16)]

    w2r = w2.rearrange("e (c p) n -> e p c n", p=P)
    sw2r = sw2.rearrange("(c p) n -> p c n", p=P)

    def w2_runs(cc0, ncc):
        runs = []
        cc = cc0
        while cc < cc0 + ncc:
            if cc < 88:
                e = cc // 11
                cstart = cc % 11
                ln = min(11 - cstart, cc0 + ncc - cc)
                runs.append(("w2", e, cstart, ln, cc - cc0))
            else:
                cstart = cc - 88
                ln = cc0 + ncc - cc
                runs.append(("sw2", 0, cstart, ln, cc - cc0))
            cc += ln
        return runs

    tok_w2st = [barrier3b, barrier3b]
    tok_at = [None, None]
    tok_yps = [None, None, None]
    tok_yrow = {}
    kgs = [(kg * 16, min(16, NKST - kg * 16)) for kg in range(7)]
    actT_r = actT_dram.rearrange("(c p) t -> p c t", p=P)

    for dh in range(2):
        tok_ylast = {}
        for gi, (cc0, ncc) in enumerate(kgs):
            wpa = gi % 2
            stoks = []
            for (kind, e, cstart, ln, boff) in w2_runs(cc0, ncc):
                if kind == "w2":
                    src = w2r[e, :, cstart:cstart + ln, dh * 1024:(dh + 1) * 1024]
                else:
                    src = sw2r[:, cstart:cstart + ln, dh * 1024:(dh + 1) * 1024]
                stoks.append(pr.dma(
                    "gpsimd", lambda h, wpa=wpa, src=src, ln=ln, boff=boff:
                    h.dma_start(out=apx(w2st[wpa][:], [[1024, ln], [1, 1024]],
                                        extra_off=boff * 1024), in_=src),
                    waits=tok_w2st[wpa]))
            for thalf in range(2):
                athalf_toks = []
                for ah in range(2):
                    nc_h = min(8, ncc - ah * 8)
                    if nc_h <= 0:
                        athalf_toks.append(athalf_toks[0])
                        continue
                    athalf_toks.append(pr.dma(
                        "sync", lambda h, cc0=cc0, ah=ah, nc_h=nc_h, thalf=thalf:
                        h.dma_start(
                            out=apx(atst[:], [[1024, nc_h], [1, 1024]],
                                    extra_off=ah * 8 * 1024),
                            in_=actT_r[:, cc0 + ah * 8:cc0 + ah * 8 + nc_h,
                                       thalf * 1024:(thalf + 1) * 1024]),
                        waits=[tok_at[ah]]
                        + (barrier3b if (dh == 0 and gi == 0) else [])))
                for M in range(8):
                    Mg = thalf * 8 + M
                    psl = (gi * 16 + thalf * 8 + M) % 3
                    pps = [pdub[:, 0:1024], pbig[:, 0:1024], pbig[:, 1024:2048]][psl]
                    mms = []
                    for ci in range(ncc):
                        lhs_w = [athalf_toks[ci // 8]]
                        for hh2 in range(2):
                            mms.append(pr.t(
                                lambda h, ci=ci, M=M, wpa=wpa, pps=pps, hh2=hh2:
                                h.matmul(pps[:, hh2 * 512:(hh2 + 1) * 512],
                                         atst[:, ci * 1024 + M * P:ci * 1024 + (M + 1) * P],
                                         w2st[wpa][:, ci * 1024 + hh2 * 512:ci * 1024 + (hh2 + 1) * 512],
                                         start=(ci == 0), stop=(ci == ncc - 1)),
                                waits=lhs_w + stoks + [tok_yps[psl]]))
                    if gi == 0:
                        t_y = pr.v(lambda h, Mg=Mg, pps=pps: h.tensor_copy(
                            y_t[:, Mg * 1024:(Mg + 1) * 1024], pps),
                            waits=[mms[-1], tok_yrow.get(Mg)])
                    else:
                        t_y = pr.v(lambda h, Mg=Mg, pps=pps: h.tensor_add(
                            y_t[:, Mg * 1024:(Mg + 1) * 1024],
                            y_t[:, Mg * 1024:(Mg + 1) * 1024], pps),
                            waits=[mms[-1], tok_ylast[Mg]])
                    tok_ylast[Mg] = t_y
                    tok_yps[psl] = t_y
                    if ncc > 8:
                        tok_at[0] = mms[2 * 8 - 1]
                        tok_at[1] = mms[-1]
                    else:
                        tok_at[0] = mms[-1]
            tok_w2st[wpa] = [mms[-1]]
        for Mg in range(NTILE):
            tok_yrow[Mg] = pr.dma(
                "scalar", lambda h, Mg=Mg, dh=dh: h.dma_start(
                    out=ffn_dram[Mg * P:(Mg + 1) * P, dh * 1024:(dh + 1) * 1024],
                    in_=y_t[:, Mg * 1024:(Mg + 1) * 1024]),
                waits=[tok_ylast[Mg]])

    barrier4 = [tok_yrow[Mg] for Mg in range(NTILE)] + pr.fence_all()
    s4.close()

    s5 = ExitStack()
    SB5 = mkSB(s5)
    g_fm = SB5("g_fm", [P, NK * T], BF16)
    dt3 = SB5("dt3", [HEADS, 2 * T])
    esel_b = SB5("esel_b", [HEADS, HEADS * P], BF16)
    dskb = SB5("dskb", [P, HEADS])
    dtb_sb = SB5("dtb_sb", [HEADS, 1])
    alog_sb = SB5("alog_sb", [HEADS, 2])
    dskrow = SB5("dskrow", [1, HEADS])

    t_esel = pr.dma("gpsimd", lambda h: h.dma_start(out=esel_b[:], in_=esel),
                    waits=barrier4)
    t_dtb = pr.dma("sync", lambda h: h.dma_start(out=dtb_sb[:], in_=dt_bias),
                   waits=barrier4)
    t_alog = pr.dma("sync", lambda h: h.dma_start(out=alog_sb[:, 0:1], in_=a_log))
    t_dskr = pr.dma("sync", lambda h: h.dma_start(out=dskrow[:], in_=d_skip))

    t_dskm = pr.t(lambda h: h.matmul(pmix[:, 0:HEADS], sb_ones[:], dskrow[:],
                                     start=True, stop=True),
                  waits=[t_dskr, t_mT] + barrier4)
    t_dskb = pr.v(lambda h: h.tensor_copy(dskb[:], pmix[:, 0:HEADS]), waits=[t_dskm])

    t_na1 = pr.s(lambda h: h.activation(alog_sb[:, 1:2], alog_sb[:, 0:1], AF.Exp),
                 waits=[t_alog] + barrier4)
    t_na = pr.v(lambda h: h.tensor_scalar_mul(alog_sb[:, 1:2], alog_sb[:, 1:2], -1.0),
                waits=[t_na1])

    t_p1 = pr.v(lambda h: h.tensor_scalar_add(dt3[:, T:2 * T], dt_fm[:], dtb_sb[:]),
                waits=[t_dtb] + barrier4)
    t_p2 = pr.s(lambda h: h.activation(dt3[:, 0:T], dt3[:, T:2 * T], AF.Abs),
                waits=[t_p1])
    t_p3 = pr.s(lambda h: h.activation(dt3[:, 0:T], dt3[:, 0:T],
                                       AF.Exp, scale=-1.0), waits=[t_p2])
    t_p4 = pr.s(lambda h: h.activation(dt3[:, 0:T], dt3[:, 0:T],
                                       AF.Ln, bias=1.0), waits=[t_p3])
    t_p5 = pr.v(lambda h: h.tensor_scalar_max(dt3[:, T:2 * T], dt3[:, T:2 * T], 0.0),
                waits=[t_p2])
    t_dtv = pr.v(lambda h: h.tensor_add(dt3[:, T:2 * T], dt3[:, T:2 * T],
                                        dt3[:, 0:T]), waits=[t_p4, t_p5])
    t_adt = pr.v(lambda h: h.tensor_scalar_mul(dt3[:, 0:T], dt3[:, T:2 * T],
                                               alog_sb[:, 1:2]),
                 waits=[t_dtv, t_na])
    t_lndt = t_adt

    s5a = ExitStack()
    SB5A = mkSB(s5a)
    cxin = [SB5A("cxin0", [P, 2051]), SB5A("cxin1", [P, 2051])]
    ubuf = SB5A("ubuf", [P, T])
    cxo = [SB5A("cxo0", [P, T], BF16), SB5A("cxo1", [P, T], BF16)]
    convw_sb = SB5A("convw_sb", [P, NCCH * 4])
    convb_t = SB5A("convb_t", [NCCH, P])
    convb_sb = SB5A("convb_sb", [P, NCCH])

    t_cw = pr.dma("sync", lambda h: h.dma_start(
        out=convw_sb.ap().rearrange("p (c k) -> p c k", k=4),
        in_=conv_w.rearrange("(c p) k -> p c k", p=P)), waits=barrier4)
    t_cb1 = pr.dma("sync", lambda h: h.dma_start(out=convb_t[:], in_=conv_b2),
                   waits=barrier4)
    t_cb2 = pr.t(lambda h: h.transpose(pmix[:, 32:32 + NCCH], convb_t[:],
                                       sb_iden_f[0:NCCH, 0:NCCH]),
                 waits=[t_cb1, t_dskb])
    t_cb = pr.v(lambda h: h.tensor_copy(convb_sb[:], pmix[:, 32:32 + NCCH]),
                waits=[t_cb2])
    tz0 = pr.v(lambda h: h.memset(cxin[0][:, 0:3], 0.0), waits=barrier4)
    tz1 = pr.v(lambda h: h.memset(cxin[1][:, 0:3], 0.0), waits=barrier4)

    tok_cxin = [tz0, tz1]
    tok_cxo = [None, None]
    tok_u = None
    qd = []
    for cc in range(NCCH):
        pa = cc % 2
        t_ld = pr.dma("sync", lambda h, cc=cc, pa=pa: h.dma_start(
            out=cxin[pa][:, 3:2051], in_=xbc_dram[cc * P:(cc + 1) * P, :]),
            waits=[tok_cxin[pa]])
        tu = pr.v(lambda h, cc=cc, pa=pa: h.tensor_scalar_mul(
            ubuf[:], cxin[pa][:, 0:2048], convw_sb[:, cc * 4:cc * 4 + 1]),
            waits=[t_ld, t_cw, tok_u])
        for kk2 in range(1, 4):
            tu = pr.v(lambda h, cc=cc, pa=pa, kk2=kk2: h.scalar_tensor_tensor(
                ubuf[:], cxin[pa][:, kk2:2048 + kk2],
                convw_sb[:, cc * 4 + kk2:cc * 4 + kk2 + 1], ubuf[:],
                ALU.mult, ALU.add), waits=[tu])
        tok_cxin[pa] = tu
        t_si = pr.s(lambda h, cc=cc, pa=pa: h.activation(
            cxo[pa][:], ubuf[:], AF.Silu, bias=convb_sb[:, cc:cc + 1]),
            waits=[tu, t_cb, tok_cxo[pa]])
        tok_u = t_si
        qd = []
        for q in range(4):
            qd.append(pr.dma("scalar", lambda h, cc=cc, pa=pa, q=q: h.dma_start(
                out=convq_dram[q, cc * P:(cc + 1) * P, :],
                in_=cxo[pa][:, q * 512:(q + 1) * 512]), waits=[t_si]))
        tok_cxo[pa] = qd[-1]

    barrier4b = [qd[-1], t_lndt, t_adt, t_dskb, t_esel] + pr.fence_all()
    s5a.close()

    s5b = ExitStack()
    SB5B = mkSB(s5b)
    cq = SB5B("cq", [P, NCCH * 512], BF16)
    S_buf = [SB5B("S0", [P, 2 * T]), SB5B("S1", [P, 2 * T])]
    prev_bf = SB5B("prev_bf", [P, 2 * T], BF16)
    zt = SB5B("zt", [P, T])
    dtt = SB5B("dtt", [P, 96])
    acr = SB5B("acr", [P, P])
    ssm = SB5B("ssm", [P, 160])
    ahi = SB5B("ahi", [HEADS, P], BF16)
    alo = SB5B("alo", [HEADS, P], BF16)
    cbm = SB5B("cbm", [P, P], BF16)
    xs_tm = SB5B("xs_tm", [P, T], BF16)
    btm = SB5B("btm", [P, 2 * P], BF16)
    xtw = SB5B("xtw", [P, T], BF16)
    relb = [SB5B("relb0", [P, P]), SB5B("relb1", [P, P])]
    mpb = [SB5B("mpb0", [P, P], BF16), SB5B("mpb1", [P, P], BF16)]
    ybuf = SB5B("ybuf", [P, 1024])
    ybuf2 = SB5B("ybuf2", [P, 1024])
    yfin = SB5B("yfin", [P, T])
    tbuf = [SB5B("tbuf0", [P, 512]), SB5B("tbuf1", [P, 512])]
    gst = SB5B("gst", [P, 4])
    gtbf = SB5B("gtbf", [P, T], BF16)

    t_s0 = pr.v(lambda h: h.memset(S_buf[0][:], 0.0), waits=barrier4b)
    scur = 0
    tok_S = [t_s0, None]
    tok_Sread = [[], []]
    tok_tbuf = [None, None]
    tok_ybuf2 = None
    tok_zsb = None
    tok_cq = None
    tok_zt = t_s0
    tok_prev = None
    tok_pdub_free = None
    tok_pbig_free = None
    tok_pmix_free = t_cb
    tok_gtr_done = None

    for q in range(4):
        t_cq = pr.dma("sync", lambda h, q=q: h.dma_start(
            out=cq.ap().rearrange("p (c t) -> p c t", t=512),
            in_=convq_dram[q].rearrange("(c p) t -> p c t", p=P)),
            waits=[tok_cq] + (barrier4b if q == 0 else []))
        for ppq in range(4):
            pi = q * 4 + ppq
            t0 = pi * P
            pf = pr.fence()
            tok_zt = pr.dma("sync", lambda h, t0=t0: h.dma_start(
                out=zt[:], in_=z_dram[t0:t0 + P, :]), waits=[tok_zt])

            ta = []
            for j in range(2):
                ta.append(pr.t(lambda h, j=j, t0=t0: h.transpose(
                    pmix[:, j * 32:(j + 1) * 32], dt3[:, j * T + t0:j * T + t0 + P],
                    sb_iden_f[0:32, 0:32]),
                    waits=[t_adt, t_lndt, tok_pmix_free] + pf))
            t_dtt0 = pr.v(lambda h: h.tensor_copy(dtt[:, 0:64], pmix[:, 0:64]),
                          waits=[ta[-1]] + pf)
            t_dtt = pr.s(lambda h: h.activation(dtt[:, 64:96], dtt[:, 32:64],
                                                AF.Ln), waits=[t_dtt0] + pf)
            tb = []
            for j, cst in enumerate((sb_tri, sb_ut, sb_e63, sb_e127)):
                tb.append(pr.t(lambda h, j=j, cst=cst: h.matmul(
                    pdub[:, j * 32:(j + 1) * 32], cst[:], dtt[:, 0:32],
                    start=True, stop=True),
                    waits=[t_dtt0, tok_pdub_free]))
            t_acr = pr.v(lambda h: h.tensor_copy(acr[:], pdub[:, 0:P]),
                         waits=[tb[-1]] + pf)
            t_eacs = pr.s(lambda h: h.activation(ssm[:, 0:32], acr[:, 0:32], AF.Exp),
                          waits=[t_acr] + pf)
            t_wdec = pr.s(lambda h: h.activation(ssm[:, 32:64], acr[:, 32:64],
                                                 AF.Exp), waits=[t_acr] + pf)
            t_w128 = pr.v(lambda h: h.tensor_mul(ssm[:, 64:96], ssm[:, 32:64],
                                                 dtt[:, 32:64]),
                          waits=[t_wdec, t_dtt0])
            t_cdb1 = pr.s(lambda h: h.activation(ssm[:, 96:128], acr[:, 64:96],
                                                 AF.Exp), waits=[t_acr] + pf)
            t_cdb2 = pr.s(lambda h: h.activation(ssm[:, 128:160], acr[:, 96:128],
                                                 AF.Exp), waits=[t_acr] + pf)
            t_afm = pr.t(lambda h: h.transpose(pmix[0:32, 0:P], acr[:, 0:32],
                                               sb_iden_f[:]),
                         waits=[t_acr, t_dtt0])
            t_ahi = pr.v(lambda h: h.tensor_copy(ahi[:], pmix[0:32, 0:P]),
                         waits=[t_afm] + pf)
            t_alo = pr.v(lambda h: h.tensor_tensor(alo[:], pmix[0:32, 0:P], ahi[:],
                                                   ALU.subtract), waits=[t_ahi])
            tok_pmix_free = t_alo
            te1 = pr.t(lambda h, ppq=ppq: h.matmul(
                pdub[:, 128:256], cq[:, 16 * 512 + ppq * P:16 * 512 + (ppq + 1) * P],
                cq[:, 18 * 512 + ppq * P:18 * 512 + (ppq + 1) * P],
                start=True, stop=False), waits=[t_cq, t_acr])
            te2 = pr.t(lambda h, ppq=ppq: h.matmul(
                pdub[:, 128:256], cq[:, 17 * 512 + ppq * P:17 * 512 + (ppq + 1) * P],
                cq[:, 19 * 512 + ppq * P:19 * 512 + (ppq + 1) * P],
                start=False, stop=True), waits=[te1])
            t_cbm = pr.v(lambda h: h.tensor_mul(cbm[:], pdub[:, 128:256], sb_tri[:]),
                         waits=[te2] + pf)
            ev_toks = []
            for grp in range(3):
                kl = [grp * 8 + j for j in range(8)] if grp < 2 else [16, 17]
                tts = []
                for j, k in enumerate(kl):
                    tts.append(pr.t(lambda h, j=j, k=k, ppq=ppq: h.transpose(
                        pbf[:, j * P:(j + 1) * P],
                        cq[:, k * 512 + ppq * P:k * 512 + (ppq + 1) * P],
                        sb_iden_b[:]),
                        waits=[t_cq, tok_pbf_free]))
                if grp < 2:
                    tok_pbf_free = pr.v(lambda h, grp=grp: h.tensor_copy(
                        xs_tm[:, grp * 1024:(grp + 1) * 1024], pbf[:, 0:1024]),
                        waits=[tts[-1]] + pf)
                else:
                    tok_pbf_free = pr.v(lambda h: h.tensor_copy(
                        btm[:], pbf[:, 0:256]), waits=[tts[-1]] + pf)
                ev_toks.append(tok_pbf_free)
            t_xs, t_xs2, t_btm = ev_toks
            t_xtw = pr.v(lambda h: h.tensor_tensor(
                apx(xtw[:], [[64, 32], [1, 64]]),
                apx(xs_tm[:], [[64, 32], [1, 64]]),
                apx(ssm[:, 64:96], [[1, 32], [0, 64]]), ALU.mult),
                waits=[t_xs, t_xs2, t_w128] + pf)
            tc0 = []
            for ncn in range(2):
                tc0.append(pr.s(lambda h, ncn=ncn, scur=scur: h.activation(
                    prev_bf[:, ncn * T:(ncn + 1) * T],
                    S_buf[scur][:, ncn * T:(ncn + 1) * T], AF.Copy),
                    waits=[tok_S[scur], tok_prev]))
            t_prev_w = tc0[-1]
            tok_Sread[scur].append(t_prev_w)
            t_ydiag = []
            tok_slot = [None] * 4
            tok_rel = [None, None]
            tok_lm = [None, None]
            tok_mp = [None, None]
            for hh in range(HEADS):
                sl = hh % 4
                bp = pdub[:, 512 + sl * P:512 + (sl + 1) * P]
                b1 = pr.t(lambda h, hh=hh, bp=bp: h.matmul(
                    bp, esel_b[:, hh * P:(hh + 1) * P], ahi[:],
                    start=True, stop=False),
                    waits=[t_alo, tok_slot[sl], t_esel])
                b2 = pr.t(lambda h, hh=hh, bp=bp: h.matmul(
                    bp, esel_b[:, hh * P:(hh + 1) * P], alo[:],
                    start=False, stop=True), waits=[b1])
                pb2 = hh % 2
                t_rel = pr.s(lambda h, hh=hh, bp=bp, pb2=pb2: h.activation(
                    relb[pb2][:], bp, AF.Relu, scale=-1.0,
                    bias=acr[:, hh:hh + 1]), waits=[b2, tok_rel[pb2]])
                tok_slot[sl] = t_rel
                t_lmx = pr.s(lambda h, hh=hh, pb2=pb2: h.activation(
                    relb[pb2][:], relb[pb2][:], AF.Exp, scale=-1.0,
                    bias=dtt[:, 64 + hh:64 + hh + 1]), waits=[t_rel, tok_lm[pb2]])
                t_mp = pr.g(lambda h, pb2=pb2: h.tensor_mul(
                    mpb[pb2][:], relb[pb2][:], cbm[:]),
                    waits=[t_lmx, t_cbm, tok_mp[pb2]])
                tok_lm[pb2] = t_mp
                tok_rel[pb2] = t_mp
                t_yd = pr.t(lambda h, hh=hh, pb2=pb2: h.matmul(
                    pbig[:, hh * HD:(hh + 1) * HD], mpb[pb2][:],
                    xs_tm[:, hh * HD:(hh + 1) * HD], start=True, stop=True),
                    waits=[t_mp, t_xs, t_xs2, tok_pbig_free])
                tok_mp[pb2] = t_yd
                t_ydiag.append(t_yd)
            t_asm = []
            sA, sB = scur, 1 - scur
            for c in range(2):
                src_b = sA if c == 0 else sB
                dst_b = sB if c == 0 else sA
                Lc = slice(c * 64, (c + 1) * 64)
                cdb_off = 96 + c * 32
                if c == 1:
                    tc1 = []
                    for ncn in range(2):
                        tc1.append(pr.s(lambda h, ncn=ncn, src_b=src_b: h.activation(
                            prev_bf[:, ncn * T:(ncn + 1) * T],
                            S_buf[src_b][:, ncn * T:(ncn + 1) * T], AF.Copy),
                            waits=[tok_S[src_b], tok_prev]))
                    t_prev_w = tc1[-1]
                    tok_Sread[src_b].append(t_prev_w)
                for r in range(2):
                    yo = []
                    for jj in range(2):
                        for ncn in range(2):
                            yo.append(pr.t(
                                lambda h, c=c, r=r, jj=jj, ncn=ncn, ppq=ppq, Lc=Lc:
                                h.matmul(
                                    pdub[Lc, jj * 512:(jj + 1) * 512],
                                    cq[:, (18 + ncn) * 512 + ppq * P + c * 64:
                                        (18 + ncn) * 512 + ppq * P + c * 64 + 64],
                                    prev_bf[:, ncn * T + r * 1024 + jj * 512:
                                            ncn * T + r * 1024 + (jj + 1) * 512],
                                    start=(ncn == 0), stop=(ncn == 1)),
                                waits=[t_prev_w, t_cq, tok_pdub_free,
                                       tok_slot[0], tok_slot[1], tok_slot[2],
                                       tok_slot[3]]))
                    tok_prev = yo[-1]
                    a1 = pr.v(lambda h, Lc=Lc, r=r: h.tensor_tensor(
                        apx(ybuf[Lc, :], [[64, 16], [1, 64]]),
                        apx(pdub[Lc, 0:1024], [[64, 16], [1, 64]]),
                        apx(ssm[Lc, r * 16:r * 16 + 16], [[1, 16], [0, 64]]),
                        ALU.mult), waits=[yo[-1], t_eacs])
                    a2 = pr.g(lambda h, Lc=Lc, r=r: h.tensor_tensor(
                        apx(ybuf2[Lc, :], [[64, 16], [1, 64]]),
                        apx(xs_tm[Lc, r * 1024:(r + 1) * 1024], [[64, 16], [1, 64]]),
                        apx(dskb[Lc, r * 16:r * 16 + 16], [[1, 16], [0, 64]]),
                        ALU.mult), waits=[t_xs, t_xs2, t_dskb, tok_ybuf2])
                    a3 = pr.v(lambda h, Lc=Lc: h.tensor_add(
                        ybuf[Lc, :], ybuf[Lc, :], ybuf2[Lc, :]), waits=[a1, a2])
                    tok_ybuf2 = a3
                    a4 = pr.v(lambda h, Lc=Lc, r=r: h.tensor_add(
                        yfin[Lc, r * 1024:(r + 1) * 1024], ybuf[Lc, :],
                        pbig[Lc, r * 1024:(r + 1) * 1024]),
                        waits=[a3, t_ydiag[-1]])
                    t_asm.append(a4)
                    tok_pdub_free = a4
                for ncn in range(2):
                    for jj in range(4):
                        psl2 = jj % 2
                        pst = pdub[:, psl2 * 512:(psl2 + 1) * 512]
                        t_st = pr.t(lambda h, c=c, ncn=ncn, jj=jj, pst=pst:
                                    h.matmul(
                                        pst,
                                        btm[c * 64:(c + 1) * 64, ncn * P:(ncn + 1) * P],
                                        xtw[c * 64:(c + 1) * 64, jj * 512:(jj + 1) * 512],
                                        start=True, stop=True),
                                    waits=[t_btm, t_xtw, tok_pdub_free])
                        t_tb = pr.g(lambda h, ncn=ncn, jj=jj, src_b=src_b,
                                    cdb_off=cdb_off:
                                    h.tensor_tensor(
                                        apx(tbuf[jj % 2][:], [[64, 8], [1, 64]]),
                                        apx(S_buf[src_b][:, ncn * T + jj * 512:ncn * T + (jj + 1) * 512],
                                            [[64, 8], [1, 64]]),
                                        apx(ssm[:, cdb_off + jj * 8:cdb_off + jj * 8 + 8],
                                            [[1, 8], [0, 64]]),
                                        ALU.mult),
                                    waits=[tok_S[src_b], t_cdb1, t_cdb2,
                                           tok_tbuf[jj % 2]])
                        tok_Sread[src_b].append(t_tb)
                        t_up = pr.v(lambda h, ncn=ncn, jj=jj, dst_b=dst_b, pst=pst:
                                    h.tensor_add(
                                        S_buf[dst_b][:, ncn * T + jj * 512:ncn * T + (jj + 1) * 512],
                                        tbuf[jj % 2][:], pst),
                                    waits=[t_st, t_tb] + tok_Sread[dst_b])
                        tok_pdub_free = t_up
                        tok_tbuf[jj % 2] = t_up
                        tok_S[dst_b] = t_up
                tok_Sread[dst_b] = []
            dump_toks = []
            if debug and pi == 0:
                dump_toks.append(pr.dma("scalar", lambda h: h.dma_start(
                    out=dbg_yfin, in_=yfin[:]), waits=t_asm))
                dump_toks.append(pr.dma("scalar", lambda h: h.dma_start(
                    out=dbg_acr, in_=acr[:]), waits=[t_acr] + t_asm))
                dump_toks.append(pr.dma("scalar", lambda h: h.dma_start(
                    out=dbg_ssm, in_=ssm[:]), waits=[t_w128, t_cdb2] + t_asm))
                dump_toks.append(pr.dma("gpsimd", lambda h: h.dma_start(
                    out=dbg_xtw, in_=xtw[:]), waits=[t_xtw] + t_asm))
                dump_toks.append(pr.dma("scalar", lambda h: h.dma_start(
                    out=dbg_S, in_=S_buf[sA][:]), waits=[tok_S[sA]] + t_asm))
            t_zs = pr.s(lambda h: h.activation(zt[:], zt[:], AF.Silu),
                        waits=[tok_zt, tok_zsb])
            if debug and pi == 0:
                dump_toks.append(pr.dma("scalar", lambda h: h.dma_start(
                    out=dbg_zs, in_=zt[:]), waits=[t_zs]))
            t_yg = pr.v(lambda h: h.tensor_mul(yfin[:], yfin[:], zt[:]),
                        waits=[t_zs] + t_asm + dump_toks)
            if debug and pi == 0:
                t_dyg = pr.dma("scalar", lambda h: h.dma_start(
                    out=dbg_yg, in_=yfin[:]), waits=[t_yg])
                dump_toks.append(t_dyg)
            t_gq = pr.s(lambda h: h.activation(zt[:], yfin[:], AF.Square,
                                               accum_out=gst[:, 0:1]),
                        waits=[t_yg] + dump_toks)
            tok_zsb = t_gq
            t_gl = pr.s(lambda h: h.activation(gst[:, 1:2], gst[:, 0:1], AF.Ln,
                                               bias=c_eps, scale=c_invd),
                        waits=[t_gq])
            t_gr = pr.s(lambda h: h.activation(gst[:, 2:3], gst[:, 1:2], AF.Exp,
                                               scale=-0.5), waits=[t_gl])
            t_gbf = pr.v(lambda h: h.tensor_scalar_mul(gtbf[:], yfin[:],
                                                       gst[:, 2:3]),
                         waits=[t_gr, t_yg] + pf)
            if debug and pi == 0:
                t_dgt = pr.dma("gpsimd", lambda h: h.dma_start(
                    out=dbg_gt, in_=gtbf[:]), waits=[t_gbf])
                t_dgs = pr.dma("scalar", lambda h: h.dma_start(
                    out=dbg_gst, in_=gst[:]), waits=[t_gr])
                t_gbf = pr.v(lambda h: h.engine_nop(), waits=[t_gbf, t_dgt, t_dgs])
            for grp in range(2):
                tts = []
                for j in range(8):
                    k = grp * 8 + j
                    tts.append(pr.t(lambda h, j=j, k=k: h.transpose(
                        pbf[:, j * P:(j + 1) * P], gtbf[:, k * P:(k + 1) * P],
                        sb_iden_b[:]), waits=[t_gbf, tok_pbf_free]))
                off = (grp * 8) * T + t0
                tok_pbf_free = pr.v(lambda h, off=off: h.tensor_copy(
                    apx(g_fm[:], [[T, 8], [1, P]], extra_off=off),
                    apx(pbf[:], [[P, 8], [1, P]])), waits=[tts[-1]])
            tok_gtr_done = tok_pbf_free
            tok_pbig_free = t_asm[-1]
            tok_zt = t_gq
        tok_cq = tok_gtr_done

    s5b.close()
    s5c = ExitStack()
    SB5C = mkSB(s5c)
    w32st = SB5C("w32st", [P, NK * 512])
    wfull = SB5C("wfull", [P, NK * 2048], BF16)
    ustage = [SB5C("ust0", [P, 512], BF16), SB5C("ust1", [P, 512], BF16)]

    w_out_r = w_out.rearrange("(k p) c -> p k c", p=P)
    t_w32_free2 = None
    tok_wf = []
    for b in range(4):
        t_wd = pr.dma("sync", lambda h, b=b: h.dma_start(
            out=apx(w32st[:], [[512, NK], [1, 512]]),
            in_=w_out_r[:, :, b * 512:(b + 1) * 512]),
            waits=[t_w32_free2, tok_gtr_done] + (pr.fence() if b == 0 else []))
        casts2 = []
        for k in range(NK):
            casts2.append(pr.v(lambda h, b=b, k=k: h.tensor_scalar_mul(
                wfull[:, k * 2048 + b * 512:k * 2048 + (b + 1) * 512],
                w32st[:, k * 512:(k + 1) * 512], sb_gnormw[:, k:k + 1]),
                waits=[t_wd, tok_consts]))
        t_w32_free2 = casts2[-1]
        tok_wf.append(casts2[-1])

    tok_ups = [tok_pdub_free, tok_pbig_free]
    tok_ust = [None, None]
    ui = [0]
    for uc in range(NK):
        for tn in range(4):
            sp = (uc * 4 + tn) % 2
            pps = [pdub[:, 0:512], pbig[:, 0:512]][sp]
            mms = []
            for k in range(NK):
                mms.append(pr.t(lambda h, uc=uc, tn=tn, k=k, pps=pps: h.matmul(
                    pps, wfull[:, k * 2048 + uc * P:k * 2048 + (uc + 1) * P],
                    g_fm[:, k * T + tn * 512:k * T + (tn + 1) * 512],
                    start=(k == 0), stop=(k == NK - 1)),
                    waits=tok_wf + [tok_ups[sp], tok_gtr_done]))
            usl = ui[0] % 2
            ui[0] += 1
            t_ue = pr.s(lambda h, usl=usl, pps=pps: h.activation(
                ustage[usl][:], pps, AF.Copy), waits=[mms[-1], tok_ust[usl]])
            tok_ups[sp] = t_ue
            tok_ust[usl] = pr.dma(
                "scalar", lambda h, usl=usl, uc=uc, tn=tn: h.dma_start(
                    out=u_dram[uc * P:(uc + 1) * P, tn * 512:(tn + 1) * 512],
                    in_=ustage[usl][:]), waits=[t_ue])

    barrier5 = [tok_ust[0], tok_ust[1]] + pr.fence_all()
    if debug:
        t_dbg_g = pr.dma("gpsimd", lambda h: h.dma_start(out=dbg_gfm, in_=g_fm[:]),
                         waits=[tok_gtr_done] + pr.fence())
        barrier5.append(t_dbg_g)
        t_dbg_dtv = pr.dma("scalar", lambda h: h.dma_start(out=dbg_dtv, in_=dt3[:]),
                           waits=[t_adt] + pr.fence())
        barrier5.append(t_dbg_dtv)
    s5c.close()
    s5.close()

    s6 = ExitStack()
    SB6 = mkSB(s6)
    ublock = SB6("ublock", [P, NK * 1024], BF16)
    wobf = SB6("wobf", [P, NK * 2048], BF16)
    xtile = SB6("xtile", [P, D])
    ftile = SB6("ftile", [P, D])
    ostg = [SB6("ostg%d" % i, [P, 512]) for i in range(4)]

    wo_r = wo_w.rearrange("(k p) c -> p k c", p=P)
    t_wo = pr.dma("gpsimd", lambda h: h.dma_start(
        out=apx(wobf[:], [[2048, NK], [1, 2048]]), in_=wo_r), waits=barrier5)

    tok_ub = None
    tok_xf = None
    tok_og = [None] * 4
    tok_vps = [None, None, None]
    oi2 = [0]
    t_lastout = None
    u_r = u_dram.rearrange("(k p) t -> p k t", p=P)
    for thalf in range(2):
        tok_ub = pr.dma("sync", lambda h, thalf=thalf: h.dma_start(
            out=apx(ublock[:], [[1024, NK], [1, 1024]]),
            in_=u_r[:, :, thalf * 1024:(thalf + 1) * 1024]),
            waits=[tok_ub] + barrier5 + ([tok_vps[0], tok_vps[1], tok_vps[2]]
                                         if thalf else []))
        for M in range(8):
            Mg = thalf * 8 + M
            tok_xf = pr.dma("sync", lambda h, Mg=Mg: h.dma_start(
                out=xtile[:], in_=x[Mg * P:(Mg + 1) * P, :]),
                waits=[tok_xf] + (barrier5 if Mg == 0 else []))
            tok_xf = pr.dma("sync", lambda h, Mg=Mg: h.dma_start(
                out=ftile[:], in_=ffn_dram[Mg * P:(Mg + 1) * P, :]),
                waits=[tok_xf])
            last_o2 = None
            for vn in range(4):
                sp = (M * 4 + vn) % 3
                pps = [pdub[:, 0:512], pbig[:, 0:512], pbig[:, 512:1024]][sp]
                mms = []
                for k in range(NK):
                    mms.append(pr.t(lambda h, M=M, vn=vn, k=k, pps=pps: h.matmul(
                        pps, ublock[:, k * 1024 + M * P:k * 1024 + (M + 1) * P],
                        wobf[:, k * 2048 + vn * 512:k * 2048 + (vn + 1) * 512],
                        start=(k == 0), stop=(k == NK - 1)),
                        waits=[t_wo, tok_ub, tok_vps[sp]]))
                oi = oi2[0] % 4
                oi2[0] += 1
                o1 = pr.v(lambda h, oi=oi, vn=vn, pps=pps: h.tensor_add(
                    ostg[oi][:], pps, xtile[:, vn * 512:(vn + 1) * 512]),
                    waits=[mms[-1], tok_xf, tok_og[oi]])
                o2 = pr.v(lambda h, oi=oi, vn=vn: h.tensor_add(
                    ostg[oi][:], ostg[oi][:], ftile[:, vn * 512:(vn + 1) * 512]),
                    waits=[o1])
                tok_vps[sp] = o2
                last_o2 = o2
                t_lastout = pr.dma(
                    "scalar", lambda h, oi=oi, Mg=Mg, vn=vn: h.dma_start(
                        out=out[Mg * P:(Mg + 1) * P, vn * 512:(vn + 1) * 512],
                        in_=ostg[oi][:]), waits=[o2])
                tok_og[oi] = t_lastout
            tok_xf = last_o2

    tout = t_lastout
    if debug:
        tout = pr.dma("scalar", lambda h: h.dma_start(out=dbg_z, in_=z_dram),
                      waits=[tout])
        tout = pr.dma("scalar", lambda h: h.dma_start(out=dbg_xbc, in_=xbc_dram),
                      waits=[tout])
        tout = pr.dma("scalar", lambda h: h.dma_start(out=dbg_ffn, in_=ffn_dram),
                      waits=[tout])
        tout = pr.dma("scalar", lambda h: h.dma_start(out=dbg_dt, in_=dt_fm[:]),
                      waits=[tout])
        tout = pr.dma("scalar", lambda h: h.dma_start(out=dbg_srs, in_=srs[:]),
                      waits=[tout])
        tout = pr.dma("scalar", lambda h: h.dma_start(out=dbg_stat, in_=stat[:]),
                      waits=[tout])
        tout = pr.dma("scalar", lambda h: h.dma_start(out=dbg_m, in_=m_dram),
                      waits=[tout])
        tout = pr.dma("gpsimd", lambda h: h.dma_start(
            out=dbg_cq, in_=convq_dram.rearrange("q c t -> (q c) t")), waits=[tout])
        tout = pr.dma("gpsimd", lambda h: h.dma_start(out=dbg_act, in_=actT_dram),
                      waits=[tout])

    pr._op("sync", lambda h: h.nop(), [tout], "qs", False)

    s6.close()
    with es, nc.Block() as block:
        pr.finalize(nc, block, sems)
    return nc


def make_constants():
    idx = np.arange(P)
    tri = ((idx[:, None] <= idx[None, :]) &
           (idx[:, None] // CHUNK == idx[None, :] // CHUNK)).astype(np.float32)
    ut = ((idx[:, None] > idx[None, :]) &
          (idx[:, None] // CHUNK == idx[None, :] // CHUNK)).astype(np.float32)
    e63a = np.zeros((P, P), np.float32)
    e63a[0:64, :] = 1.0
    e127a = np.zeros((P, P), np.float32)
    e127a[64:128, :] = 1.0
    esl = np.zeros((HEADS, HEADS * P), np.float32)
    for hh in range(HEADS):
        esl[hh, hh * P:(hh + 1) * P] = 1.0
    return {
        "iden_f": np.eye(P, dtype=np.float32),
        "tri_cum": tri, "ut_strict": ut, "e63": e63a, "e127": e127a,
        "ones1": np.ones((1, P), np.float32),
        "esel": esl,
    }


def make_in_map(core, inputs):
    m = {
        "x": np.ascontiguousarray(inputs["L0_attn_x"][core]),
        "w_in": inputs["L0_attn_w_in"],
        "conv_w": inputs["L0_attn_conv_w"],
        "conv_b2": np.ascontiguousarray(inputs["L0_attn_conv_b"].reshape(NCCH, P)),
        "dt_bias": np.ascontiguousarray(inputs["L0_attn_dt_bias"].reshape(HEADS, 1)),
        "a_log": np.ascontiguousarray(inputs["L0_attn_A_log"].reshape(HEADS, 1)),
        "d_skip": np.ascontiguousarray(inputs["L0_attn_D_skip"].reshape(1, HEADS)),
        "normw2": np.ascontiguousarray(inputs["L0_attn_norm_w"].reshape(NK, P).T),
        "gnormw2": np.ascontiguousarray(inputs["L0_attn_gnorm_w"].reshape(NK, P).T),
        "w_out": inputs["L0_attn_w_out"],
        "wo_w": inputs["L0_attn_wo_w"],
        "gate_w": inputs["L0_ffn_gate_w"],
        "w1": inputs["L0_ffn_w1"],
        "w3": inputs["L0_ffn_w3"],
        "w2": inputs["L0_ffn_w2"],
        "sw1": inputs["L0_ffn_sw1"],
        "sw3": inputs["L0_ffn_sw3"],
        "sw2": inputs["L0_ffn_sw2"],
    }
    m.update(make_constants())
    return m


def kernel(**inputs):
    inputs = {k: np.asarray(v) for k, v in inputs.items()}
    nc = build_program()
    in_maps = [make_in_map(b, inputs) for b in range(8)]
    res = run_bass_kernel_spmd(nc, in_maps, core_ids=list(range(8)))
    return np.stack([np.asarray(res.results[b]["out"]) for b in range(8)], axis=0)


# version 02
# speedup vs baseline: 1.0381x; 1.0381x over previous
import numpy as np
from contextlib import ExitStack

import concourse.bass as bass
import concourse.mybir as mybir
from concourse.bass_utils import run_bass_kernel_spmd

F32 = mybir.dt.float32
BF16 = mybir.dt.bfloat16
AF = mybir.ActivationFunctionType
ALU = mybir.AluOpType
AXX = mybir.AxisListType.X

P = 128
D = 2048
T = 2048
NK = D // P
NTILE = T // P
PROJ = 4640
CONV = 2560
NCCH = CONV // P
HEADS = 32
HD = 64
STATE = 256
CHUNK = 64
EPS = 1e-5
E_RT = 8
I_RT = 1408
I_SH = 2816
KSTACK = E_RT * I_RT + I_SH
NKST = KSTACK // P
INV_D = 1.0 / D

ENGINES = ("tensor", "vector", "scalar", "gpsimd", "sync")
SEMKEYS = ("pe", "dve", "act", "pool", "qs", "qa", "qg")
DMA_SEM = {"sync": "qs", "scalar": "qa", "gpsimd": "qg"}


class Prog:

    def __init__(self):
        self.ops = {e: [] for e in ENGINES}
        self.nid = 0
        self.used = set()
        self.last = {}

    def _new_id(self):
        self.nid += 1
        return self.nid

    def _op(self, eng, build, waits, semkey, is_dma):
        opid = self._new_id()
        wl = []
        for t in waits:
            if t is None:
                continue
            sk, oid = t
            self.used.add((sk, oid))
            wl.append((sk, oid))
        if is_dma:
            self.used.add((semkey, opid))
        self.ops[eng].append((wl, build, semkey, opid, is_dma))
        self.last[semkey] = (semkey, opid)
        return (semkey, opid)

    def fence(self, keys=("pe", "dve", "act", "pool")):
        return [self.last[k] for k in keys if k in self.last]

    def fence_all(self):
        return [self.last[k] for k in SEMKEYS if k in self.last]

    def t(self, build, waits=()):
        return self._op("tensor", build, waits, "pe", False)

    def v(self, build, waits=()):
        return self._op("vector", build, waits, "dve", False)

    def s(self, build, waits=()):
        return self._op("scalar", build, waits, "act", False)

    def g(self, build, waits=()):
        return self._op("gpsimd", build, waits, "pool", False)

    def dma(self, eng, build, waits=()):
        return self._op(eng, build, waits, DMA_SEM[eng], True)

    def finalize(self, nc, block, sems):
        marked = {k: [] for k in SEMKEYS}
        for eng in ENGINES:
            for (wl, build, sk, opid, is_dma) in self.ops[eng]:
                if (sk, opid) in self.used:
                    marked[sk].append(opid)
        rank = {}
        for sk in SEMKEYS:
            for i, oid in enumerate(sorted(marked[sk])):
                rank[(sk, oid)] = i + 1

        def section(eng):
            lastwait = {}

            def body(h):
                for (wl, build, sk, opid, is_dma) in self.ops[eng]:
                    for (wsk, woid) in wl:
                        step = 16 if wsk in ("qs", "qa", "qg") else 1
                        val = rank[(wsk, woid)] * step
                        if lastwait.get(wsk, 0) < val:
                            h.wait_ge(sems[wsk], val)
                            lastwait[wsk] = val
                    ins = build(h)
                    if (sk, opid) in self.used:
                        step = 16 if sk in ("qs", "qa", "qg") else 1
                        ins.then_inc(sems[sk], step)
            return body

        for eng in ENGINES:
            if self.ops[eng]:
                getattr(block, eng)(section(eng))


def apx(base_ap, free_dims, extra_off=0):
    return bass.AP(base_ap.tensor, base_ap.offset + extra_off,
                   [list(base_ap.ap[0])] + [list(d) for d in free_dims])


def build_program(debug=False):
    nc = bass.Bass("TRN2", target_bir_lowering=False)
    pr = Prog()

    def din(name, shape, dt=F32):
        return nc.dram_tensor(name, shape, dt, kind="ExternalInput").ap()

    x = din("x", [T, D])
    w_in = din("w_in", [D, PROJ])
    conv_w = din("conv_w", [CONV, 4])
    conv_b2 = din("conv_b2", [NCCH, P])
    dt_bias = din("dt_bias", [HEADS, 1])
    a_log = din("a_log", [HEADS, 1])
    d_skip = din("d_skip", [1, HEADS])
    normw2 = din("normw2", [P, NK])
    gnormw2 = din("gnormw2", [P, NK])
    w_out = din("w_out", [D, D])
    wo_w = din("wo_w", [D, D])
    gate_w = din("gate_w", [E_RT, D])
    w1 = din("w1", [E_RT, D, I_RT])
    w3 = din("w3", [E_RT, D, I_RT])
    w2 = din("w2", [E_RT, I_RT, D])
    sw1 = din("sw1", [D, I_SH])
    sw3 = din("sw3", [D, I_SH])
    sw2 = din("sw2", [I_SH, D])
    iden_f = din("iden_f", [P, P])
    tri_cum = din("tri_cum", [P, P])
    ut_strict = din("ut_strict", [P, P])
    e63 = din("e63", [P, P])
    e127 = din("e127", [P, P])
    ones1 = din("ones1", [1, P])
    esel = din("esel", [HEADS, HEADS * P])

    z_dram = nc.dram_tensor("z_dram", [T, D], F32, kind="Internal").ap()
    xbc_dram = nc.dram_tensor("xbc_dram", [CONV, T], F32, kind="Internal").ap()
    convq_dram = nc.dram_tensor("convq_dram", [4, CONV, 512], BF16, kind="Internal").ap()
    actT_dram = nc.dram_tensor("actT_dram", [KSTACK, T], BF16, kind="Internal").ap()
    ffn_dram = nc.dram_tensor("ffn_dram", [T, D], F32, kind="Internal").ap()
    srs_dram = nc.dram_tensor("srs_dram", [T], F32, kind="Internal").ap()
    m_dram = nc.dram_tensor("m_dram", [P, P], F32, kind="Internal").ap()
    u_dram = nc.dram_tensor("u_dram", [D, T], BF16, kind="Internal").ap()
    out = nc.dram_tensor("out", [T, D], F32, kind="ExternalOutput").ap()
    if debug:
        dbg_z = nc.dram_tensor("dbg_z", [T, D], F32, kind="ExternalOutput").ap()
        dbg_xbc = nc.dram_tensor("dbg_xbc", [CONV, T], F32, kind="ExternalOutput").ap()
        dbg_dt = nc.dram_tensor("dbg_dt", [HEADS, T], F32, kind="ExternalOutput").ap()
        dbg_xnf = nc.dram_tensor("dbg_xnf", [P, NK * T], F32, kind="ExternalOutput").ap()
        dbg_srs = nc.dram_tensor("dbg_srs", [P, NTILE], F32, kind="ExternalOutput").ap()
        dbg_stat = nc.dram_tensor("dbg_stat", [P, 16], F32, kind="ExternalOutput").ap()
        dbg_ffn = nc.dram_tensor("dbg_ffn", [T, D], F32, kind="ExternalOutput").ap()
        dbg_gfm = nc.dram_tensor("dbg_gfm", [P, NK * T], F32, kind="ExternalOutput").ap()
        dbg_m = nc.dram_tensor("dbg_m", [P, P], F32, kind="ExternalOutput").ap()
        dbg_cq = nc.dram_tensor("dbg_cq", [4 * CONV, 512], F32, kind="ExternalOutput").ap()
        dbg_dtv = nc.dram_tensor("dbg_dtv", [HEADS, 2 * T], F32, kind="ExternalOutput").ap()
        dbg_acr = nc.dram_tensor("dbg_acr", [P, P], F32, kind="ExternalOutput").ap()
        dbg_ssm = nc.dram_tensor("dbg_ssm", [P, 160], F32, kind="ExternalOutput").ap()
        dbg_yfin = nc.dram_tensor("dbg_yfin", [P, T], F32, kind="ExternalOutput").ap()
        dbg_xtw = nc.dram_tensor("dbg_xtw", [P, T], F32, kind="ExternalOutput").ap()
        dbg_S = nc.dram_tensor("dbg_S", [P, 2 * T], F32, kind="ExternalOutput").ap()
        dbg_act = nc.dram_tensor("dbg_act", [KSTACK, T], F32, kind="ExternalOutput").ap()
        dbg_zs = nc.dram_tensor("dbg_zs", [P, T], F32, kind="ExternalOutput").ap()
        dbg_gt = nc.dram_tensor("dbg_gt", [P, T], F32, kind="ExternalOutput").ap()
        dbg_gst = nc.dram_tensor("dbg_gst", [P, 4], F32, kind="ExternalOutput").ap()
        dbg_yg = nc.dram_tensor("dbg_yg", [P, T], F32, kind="ExternalOutput").ap()

    es = ExitStack()

    def mkSB(stack):
        return lambda n, s, d=F32: stack.enter_context(nc.sbuf_tensor(n, s, d))

    SB = mkSB(es)
    PSUM = lambda n, s, d=F32: es.enter_context(nc.psum_tensor(n, s, d))

    pbig = PSUM("pbig", [P, 2048])
    pdub = PSUM("pdub", [P, 1024])
    pbf = PSUM("pbf", [P, 1024], BF16)
    pmix = PSUM("pmix", [P, 512])

    sb_iden_f = SB("c_idenf", [P, P])
    sb_iden_b = SB("c_idenb", [P, P], BF16)
    sb_tri = SB("c_tri", [P, P])
    sb_ut = SB("c_ut", [P, P])
    sb_e63 = SB("c_e63", [P, P])
    sb_e127 = SB("c_e127", [P, P])
    sb_ones = SB("c_ones", [1, P])
    sb_normw = SB("c_normw", [P, NK])
    sb_gnormw = SB("c_gnormw", [P, NK])
    srs = SB("srs", [P, NTILE])
    stat = SB("stat", [P, 16])
    cpool = SB("cpool", [P, 4])
    dt_fm = SB("dt_fm", [HEADS, T])
    dt3 = SB("dt3", [HEADS, 2 * T])
    dskb = SB("dskb", [P, HEADS])
    dtb_sb = SB("dtb_sb", [HEADS, 1])
    alog_sb = SB("alog_sb", [HEADS, 2])
    dskrow = SB("dskrow", [1, HEADS])

    sems = {}
    for sk in SEMKEYS:
        sems[sk] = es.enter_context(nc.semaphore("sem_" + sk))

    c_tok = []
    for dst, src in [(sb_iden_f, iden_f), (sb_tri, tri_cum),
                     (sb_ut, ut_strict), (sb_e63, e63), (sb_e127, e127),
                     (sb_ones, ones1), (sb_normw, normw2), (sb_gnormw, gnormw2)]:
        c_tok.append(pr.dma("sync", lambda h, d=dst, s=src: h.dma_start(out=d[:], in_=s)))
    c_tok.append(pr.dma("gpsimd", lambda h: h.dma_start(out=sb_iden_b[:], in_=iden_f)))
    tok_consts = c_tok[-2]
    tok_consts_b = c_tok[-1]
    pr.v(lambda h: h.memset(cpool[:, 0:1], EPS))
    pr.v(lambda h: h.memset(cpool[:, 2:3], -1e30))
    tok_cpool = pr.v(lambda h: h.memset(cpool[:, 1:2], INV_D))
    c_eps = cpool[:, 0:1]
    c_invd = cpool[:, 1:2]
    c_n30 = cpool[:, 2:3]

    s_xnf = ExitStack()
    xnf = mkSB(s_xnf)("xnf", [P, NK * T], BF16)

    s12 = ExitStack()
    SB12 = mkSB(s12)
    junk2k = SB12("junk2k", [P, 2048])
    xt = [SB12("xt0", [P, D]), SB12("xt1", [P, D])]
    xnt = [SB12("xnt0", [P, D], BF16), SB12("xnt1", [P, D], BF16)]

    tok_xt_free = [None, None]
    tok_xnt_free = [None, None]
    tok_pbf_free = None
    tok_srs_last = None

    for ti in range(NTILE):
        pa = ti % 2
        t_dmax = pr.dma(
            "sync", lambda h, pa=pa, ti=ti: h.dma_start(
                out=xt[pa][:], in_=x[ti * P:(ti + 1) * P, :]),
            waits=[tok_xt_free[pa]])
        w0 = [t_dmax]
        t_sum = pr.v(lambda h, pa=pa: h.reduce_sum(stat[:, 0:1], xt[pa][:], axis=AXX),
                     waits=w0)
        t_sq = pr.s(lambda h, pa=pa: h.activation(
            junk2k[:], xt[pa][:], AF.Square, accum_out=stat[:, 1:2]), waits=w0)
        t_mean = pr.v(lambda h: h.tensor_scalar_mul(stat[:, 2:3], stat[:, 0:1], INV_D),
                      waits=[t_sum])
        t_m2 = pr.v(lambda h: h.tensor_mul(stat[:, 3:4], stat[:, 2:3], stat[:, 2:3]),
                    waits=[t_mean])
        t_v0 = pr.v(lambda h: h.tensor_scalar_mul(stat[:, 4:5], stat[:, 1:2], INV_D),
                    waits=[t_sq])
        t_var = pr.v(lambda h: h.tensor_sub(stat[:, 4:5], stat[:, 4:5], stat[:, 3:4]),
                     waits=[t_v0, t_m2])
        t_lnv = pr.s(lambda h: h.activation(stat[:, 5:6], stat[:, 4:5], AF.Ln,
                                            bias=c_eps),
                     waits=[t_var, tok_cpool])
        t_rstd = pr.s(lambda h: h.activation(stat[:, 6:7], stat[:, 5:6], AF.Exp,
                                             scale=-0.5), waits=[t_lnv])
        t_negmb = pr.v(lambda h: h.scalar_tensor_tensor(
            stat[:, 7:8], stat[:, 2:3], -1.0, stat[:, 6:7], ALU.mult, ALU.mult),
            waits=[t_rstd, t_mean])
        t_msq = pr.s(lambda h, pa=pa: h.activation(
            junk2k[:], xt[pa][:], AF.Square, bias=stat[:, 7:8], scale=stat[:, 6:7],
            accum_out=stat[:, 8:9]), waits=[t_negmb])
        t_msqm = pr.v(lambda h: h.tensor_scalar_mul(stat[:, 9:10], stat[:, 8:9], INV_D),
                      waits=[t_msq])
        t_l2 = pr.s(lambda h: h.activation(stat[:, 10:11], stat[:, 9:10], AF.Ln,
                                           bias=c_eps), waits=[t_msqm])
        t_rs2 = pr.s(lambda h, ti=ti: h.activation(
            srs[:, ti:ti + 1], stat[:, 10:11], AF.Exp, scale=-0.5), waits=[t_l2])
        tok_srs_last = t_rs2
        t_comb = pr.v(lambda h: h.tensor_copy(stat[:, 12:13], stat[:, 6:7]),
                      waits=[t_rstd])
        t_negc = pr.v(lambda h: h.scalar_tensor_tensor(
            stat[:, 13:14], stat[:, 2:3], -1.0, stat[:, 12:13], ALU.mult, ALU.mult),
            waits=[t_comb])
        tok_xt_free[pa] = pr.v(lambda h, pa=pa: h.tensor_scalar(
            xnt[pa][:], xt[pa][:], stat[:, 12:13], stat[:, 13:14],
            ALU.mult, ALU.add),
            waits=[t_negc, tok_xnt_free[pa]])
        for grp in range(2):
            tts = []
            for j in range(8):
                k = grp * 8 + j
                tts.append(pr.t(lambda h, pa=pa, k=k, j=j: h.transpose(
                    pbf[:, j * P:(j + 1) * P], xnt[pa][:, k * P:(k + 1) * P],
                    sb_iden_b[:]),
                    waits=[tok_xt_free[pa], tok_pbf_free, tok_consts,
                           tok_consts_b]))
            off = (grp * 8) * T + ti * P
            tok_pbf_free = pr.v(lambda h, off=off: h.tensor_copy(
                apx(xnf[:], [[T, 8], [1, P]], extra_off=off),
                apx(pbf[:], [[P, 8], [1, P]])), waits=[tts[-1]])
            if grp == 1:
                tok_xnt_free[pa] = tok_pbf_free

    wst32 = SB12("wst32", [P, NK * 256])
    wstbf = [SB12("wstbf0", [P, NK * 512], BF16), SB12("wstbf1", [P, NK * 512], BF16)]
    ostage = [SB12("ostage%d" % i, [P, 512]) for i in range(4)]
    srsT = SB12("srsT", [NTILE, P])
    rs2_b = SB12("rs2_b", [P, 2048])
    srs_row = SB12("srs_row", [1, 2048])

    w_in_r = w_in.rearrange("(k p) c -> p k c", p=P)

    t_tr = pr.t(lambda h: h.transpose(pmix[0:NTILE, 0:P], srs[:], sb_iden_f[:]),
                waits=[tok_srs_last, tok_consts])
    t_srs_ev = pr.v(lambda h: h.tensor_copy(srsT[:], pmix[0:NTILE, 0:P]), waits=[t_tr])
    t_srs_out = pr.dma("scalar", lambda h: h.dma_start(
        out=srs_dram.rearrange("(a b) -> a b", a=NTILE), in_=srsT[:]),
        waits=[t_srs_ev])
    t_srs_in = pr.dma("sync", lambda h: h.dma_start(
        out=srs_row[:], in_=srs_dram.rearrange("(a b) -> a b", a=1)),
        waits=[t_srs_out])
    k1s = []
    for j in range(4):
        k1s.append(pr.t(lambda h, j=j: h.matmul(
            pbig[:, j * 512:(j + 1) * 512], sb_ones[:],
            srs_row[:, j * 512:(j + 1) * 512], start=True, stop=True),
            waits=[t_srs_in]))
    t_rs2b = pr.v(lambda h: h.tensor_copy(rs2_b[:], pbig[:]), waits=[k1s[-1]])

    blocks = [(c0, 512, "z") for c0 in range(0, 2048, 512)] \
        + [(c0, 512, "xbc") for c0 in range(2048, 4608, 512)] \
        + [(4608, 32, "dt")]

    tok_cast = [None, None]
    tok_w32_free = None
    tok_ost_free = [None] * 4
    tok_psB = [None, None]
    ost_i = [0]

    def next_ost():
        i = ost_i[0] % 4
        ost_i[0] += 1
        return i

    for bi, (c0, C, kind) in enumerate(blocks):
        pa = bi % 2
        casts = []
        for hb in range(0, C, 256):
            hw = min(256, C - hb)
            t_w32 = pr.dma("sync", lambda h, c0=c0, hb=hb, hw=hw: h.dma_start(
                out=apx(wst32[:], [[256, NK], [1, hw]]),
                in_=w_in_r[:, :, c0 + hb:c0 + hb + hw]), waits=[tok_w32_free])
            for k in range(NK):
                casts.append(pr.v(
                    lambda h, k=k, C=C, pa=pa, hb=hb, hw=hw: h.tensor_scalar_mul(
                        wstbf[pa][:, k * C + hb:k * C + hb + hw],
                        wst32[:, k * 256:k * 256 + hw],
                        sb_normw[:, k:k + 1]),
                    waits=[t_w32, tok_cast[pa], tok_consts]))
            tok_w32_free = casts[-1]
        tok_cast[pa] = None
        last_use = None

        if kind == "z":
            for M in range(NTILE):
                pp = M % 2
                ps = pdub[:, pp * 512:(pp + 1) * 512]
                mms = []
                for k in range(NK):
                    mms.append(pr.t(lambda h, k=k, M=M, pa=pa, C=C, ps=ps:
                                    h.matmul(ps, xnf[:, k * T + M * P:k * T + (M + 1) * P],
                                             wstbf[pa][:, k * C:(k + 1) * C],
                                             start=(k == 0), stop=(k == NK - 1)),
                                    waits=[casts[-1], tok_psB[pp]]))
                last_use = mms[-1]
                oi = next_ost()
                t_ev = pr.v(lambda h, oi=oi, ps=ps, M=M: h.tensor_scalar_mul(
                    ostage[oi][:], ps, srs[:, M:M + 1]),
                    waits=[mms[-1], tok_ost_free[oi], t_rs2b])
                tok_psB[pp] = t_ev
                tok_ost_free[oi] = pr.dma(
                    "scalar", lambda h, oi=oi, M=M, c0=c0: h.dma_start(
                        out=z_dram[M * P:(M + 1) * P, c0:c0 + 512], in_=ostage[oi][:]),
                    waits=[t_ev])
        elif kind == "xbc":
            for PC in range(4):
                for TN in range(4):
                    pp = (PC * 4 + TN) % 2
                    ps = pdub[:, pp * 512:(pp + 1) * 512]
                    mms = []
                    for k in range(NK):
                        mms.append(pr.t(lambda h, k=k, PC=PC, TN=TN, pa=pa, C=C, ps=ps:
                                        h.matmul(ps,
                                                 wstbf[pa][:, k * C + PC * P:k * C + (PC + 1) * P],
                                                 xnf[:, k * T + TN * 512:k * T + (TN + 1) * 512],
                                                 start=(k == 0), stop=(k == NK - 1)),
                                        waits=[casts[-1], tok_psB[pp]]))
                    last_use = mms[-1]
                    oi = next_ost()
                    t_ev = pr.v(lambda h, oi=oi, ps=ps, TN=TN: h.tensor_tensor(
                        ostage[oi][:], ps, rs2_b[:, TN * 512:(TN + 1) * 512],
                        ALU.mult),
                        waits=[mms[-1], tok_ost_free[oi], t_rs2b])
                    tok_psB[pp] = t_ev
                    r0 = c0 - 2048 + PC * P
                    tok_ost_free[oi] = pr.dma(
                        "scalar", lambda h, oi=oi, r0=r0, TN=TN: h.dma_start(
                            out=xbc_dram[r0:r0 + P, TN * 512:(TN + 1) * 512],
                            in_=ostage[oi][:]),
                        waits=[t_ev])
        else:
            for TN in range(4):
                pp = TN % 2
                ps = pdub[0:HEADS, pp * 512:(pp + 1) * 512]
                mms = []
                for k in range(NK):
                    mms.append(pr.t(lambda h, k=k, TN=TN, pa=pa, C=C, ps=ps:
                                    h.matmul(ps,
                                             wstbf[pa][:, k * C:k * C + HEADS],
                                             xnf[:, k * T + TN * 512:k * T + (TN + 1) * 512],
                                             start=(k == 0), stop=(k == NK - 1)),
                                    waits=[casts[-1], tok_psB[pp]]))
                last_use = mms[-1]
                tok_psB[pp] = pr.v(lambda h, TN=TN, ps=ps: h.tensor_tensor(
                    dt_fm[:, TN * 512:(TN + 1) * 512], ps,
                    rs2_b[0:HEADS, TN * 512:(TN + 1) * 512], ALU.mult),
                    waits=[mms[-1], t_rs2b])
        tok_cast[pa] = last_use

    s12.close()
    barrier23 = [t for t in tok_ost_free if t is not None] + pr.fence_all()

    s3 = ExitStack()
    SB3 = mkSB(s3)
    gwf = SB3("gwf", [E_RT, D])
    gwT32 = SB3("gwT32", [P, NTILE * E_RT])
    gwhi = SB3("gwhi", [P, NTILE * E_RT], BF16)
    gwlo = SB3("gwlo", [P, NTILE * E_RT], BF16)
    glog = SB3("glog", [P, E_RT])
    ge1 = SB3("ge1", [P, E_RT])
    ge2 = SB3("ge2", [P, E_RT])
    glm = SB3("glm", [P, E_RT])
    gsc = SB3("gsc", [P, 8])
    m_tm = SB3("m_tm", [P, NTILE * E_RT])
    mT = SB3("mT", [P, P])
    me_b = SB3("me_b", [P, T])
    w1blk = [SB3("w1blk0", [P, NK * 512], BF16), SB3("w1blk1", [P, NK * 512], BF16)]
    w3blk = [SB3("w3blk0", [P, NK * 512], BF16), SB3("w3blk1", [P, NK * 512], BF16)]
    astage = [SB3("ast%d" % i, [P, 512], BF16) for i in range(4)]
    shtile = [SB3("sht%d" % i, [P, 512]) for i in range(2)]
    a1tile = [SB3("a1t%d" % i, [P, 512]) for i in range(2)]

    t_gw = pr.dma("sync", lambda h: h.dma_start(out=gwf[:], in_=gate_w),
                  waits=barrier23)
    gts = []
    for k in range(NK):
        gts.append(pr.t(lambda h, k=k: h.transpose(
            pmix[:, k * 8:(k + 1) * 8], gwf[0:E_RT, k * P:(k + 1) * P],
            sb_iden_f[0:E_RT, 0:E_RT]), waits=[t_gw, t_srs_ev]))
    t_gwT = pr.v(lambda h: h.tensor_copy(gwT32[:], pmix[:, 0:NTILE * E_RT]),
                 waits=[gts[-1]])
    t_ghi = pr.v(lambda h: h.tensor_copy(gwhi[:], gwT32[:]), waits=[t_gwT])
    t_glo = pr.v(lambda h: h.tensor_tensor(gwlo[:], gwT32[:], gwhi[:],
                                           ALU.subtract), waits=[t_ghi])

    t_pm_free = t_gwT
    tok_m_last = None
    for M in range(NTILE):
        mms = []
        for k in range(NK):
            mms.append(pr.t(lambda h, k=k, M=M: h.matmul(
                pmix[:, 0:E_RT], xnf[:, k * T + M * P:k * T + (M + 1) * P],
                gwhi[:, k * 8:(k + 1) * 8], start=(k == 0), stop=False),
                waits=[t_glo, t_pm_free]))
        for k in range(NK):
            mms.append(pr.t(lambda h, k=k, M=M: h.matmul(
                pmix[:, 0:E_RT], xnf[:, k * T + M * P:k * T + (M + 1) * P],
                gwlo[:, k * 8:(k + 1) * 8], start=False, stop=(k == NK - 1)),
                waits=[t_glo]))
        t_lg = pr.v(lambda h: h.tensor_copy(glog[:], pmix[:, 0:E_RT]),
                    waits=[mms[-1], tok_m_last])
        t_pm_free = t_lg
        t_mx1 = pr.v(lambda h: h.reduce_max(gsc[:, 0:1], glog[:], axis=AXX),
                     waits=[t_lg])
        t_e1 = pr.v(lambda h: h.tensor_tensor(
            ge1[:], glog[:], apx(gsc[:, 0:1], [[0, E_RT]]), ALU.is_equal),
            waits=[t_mx1])
        t_l30 = pr.v(lambda h: h.tensor_scalar_mul(glm[:], ge1[:], c_n30),
                     waits=[t_e1, tok_cpool])
        t_lm = pr.v(lambda h: h.tensor_add(glm[:], glm[:], glog[:]), waits=[t_l30])
        t_mx2 = pr.v(lambda h: h.reduce_max(gsc[:, 1:2], glm[:], axis=AXX),
                     waits=[t_lm])
        t_e2 = pr.v(lambda h: h.tensor_tensor(
            ge2[:], glm[:], apx(gsc[:, 1:2], [[0, E_RT]]), ALU.is_equal),
            waits=[t_mx2])
        t_d = pr.v(lambda h: h.tensor_sub(gsc[:, 2:3], gsc[:, 1:2], gsc[:, 0:1]),
                   waits=[t_mx2])
        t_ed = pr.s(lambda h: h.activation(gsc[:, 3:4], gsc[:, 2:3], AF.Exp),
                    waits=[t_d])
        t_den = pr.v(lambda h: h.tensor_scalar_add(gsc[:, 4:5], gsc[:, 3:4], 1.0),
                     waits=[t_ed])
        t_r = pr.v(lambda h: h.reciprocal(gsc[:, 5:6], gsc[:, 4:5]), waits=[t_den])
        t_w2v = pr.v(lambda h: h.tensor_mul(gsc[:, 6:7], gsc[:, 3:4], gsc[:, 5:6]),
                     waits=[t_r])
        t_m1 = pr.v(lambda h: h.tensor_scalar_mul(ge1[:], ge1[:], gsc[:, 5:6]),
                    waits=[t_r, t_e1])
        tok_m_last = pr.v(lambda h, M=M: h.scalar_tensor_tensor(
            m_tm[:, M * 8:(M + 1) * 8], ge2[:], gsc[:, 6:7], ge1[:],
            ALU.mult, ALU.add), waits=[t_w2v, t_m1])

    t_mtr = pr.t(lambda h: h.transpose(pmix[:, 0:P], m_tm[:], sb_iden_f[:]),
                 waits=[tok_m_last, t_pm_free])
    t_mT = pr.v(lambda h: h.tensor_copy(mT[:], pmix[:, 0:P]), waits=[t_mtr])
    t_mout = pr.dma("scalar", lambda h: h.dma_start(out=m_dram, in_=mT[:]),
                    waits=[t_mT])

    cxin = SB3("cxin", [P, 2051])
    ubuf = SB3("ubuf", [P, T])
    cxo = SB3("cxo", [P, T], BF16)
    convw_sb = SB3("convw_sb", [P, NCCH * 4])
    convb_t = SB3("convb_t", [NCCH, P])
    convb_sb = SB3("convb_sb", [P, NCCH])

    t_dtb = pr.dma("sync", lambda h: h.dma_start(out=dtb_sb[:], in_=dt_bias),
                   waits=barrier23)
    t_alog = pr.dma("sync", lambda h: h.dma_start(out=alog_sb[:, 0:1], in_=a_log))
    t_dskr = pr.dma("sync", lambda h: h.dma_start(out=dskrow[:], in_=d_skip))
    t_dskm = pr.t(lambda h: h.matmul(pmix[:, 0:HEADS], sb_ones[:], dskrow[:],
                                     start=True, stop=True),
                  waits=[t_dskr, t_mT])
    t_dskb = pr.v(lambda h: h.tensor_copy(dskb[:], pmix[:, 0:HEADS]), waits=[t_dskm])
    t_na1 = pr.s(lambda h: h.activation(alog_sb[:, 1:2], alog_sb[:, 0:1], AF.Exp),
                 waits=[t_alog])
    t_na = pr.v(lambda h: h.tensor_scalar_mul(alog_sb[:, 1:2], alog_sb[:, 1:2], -1.0),
                waits=[t_na1])
    t_p1 = pr.v(lambda h: h.tensor_scalar_add(dt3[:, T:2 * T], dt_fm[:], dtb_sb[:]),
                waits=[t_dtb] + barrier23)
    t_p2 = pr.s(lambda h: h.activation(dt3[:, 0:T], dt3[:, T:2 * T], AF.Abs),
                waits=[t_p1])
    t_p3 = pr.s(lambda h: h.activation(dt3[:, 0:T], dt3[:, 0:T],
                                       AF.Exp, scale=-1.0), waits=[t_p2])
    t_p4 = pr.s(lambda h: h.activation(dt3[:, 0:T], dt3[:, 0:T],
                                       AF.Ln, bias=1.0), waits=[t_p3])
    t_p5 = pr.v(lambda h: h.tensor_scalar_max(dt3[:, T:2 * T], dt3[:, T:2 * T], 0.0),
                waits=[t_p2])
    t_dtv = pr.v(lambda h: h.tensor_add(dt3[:, T:2 * T], dt3[:, T:2 * T],
                                        dt3[:, 0:T]), waits=[t_p4, t_p5])
    t_adt = pr.v(lambda h: h.tensor_scalar_mul(dt3[:, 0:T], dt3[:, T:2 * T],
                                               alog_sb[:, 1:2]),
                 waits=[t_dtv, t_na])
    t_lndt = t_adt

    t_cw = pr.dma("sync", lambda h: h.dma_start(
        out=convw_sb.ap().rearrange("p (c k) -> p c k", k=4),
        in_=conv_w.rearrange("(c p) k -> p c k", p=P)), waits=barrier23)
    t_cb1 = pr.dma("sync", lambda h: h.dma_start(out=convb_t[:], in_=conv_b2),
                   waits=barrier23)
    t_cb2 = pr.t(lambda h: h.transpose(pmix[:, 32:32 + NCCH], convb_t[:],
                                       sb_iden_f[0:NCCH, 0:NCCH]),
                 waits=[t_cb1, t_dskb])
    t_cb = pr.v(lambda h: h.tensor_copy(convb_sb[:], pmix[:, 32:32 + NCCH]),
                waits=[t_cb2])
    tz0 = pr.v(lambda h: h.memset(cxin[:, 0:3], 0.0), waits=barrier23)

    conv_state = {"cxin": tz0, "cxo": None, "u": None, "qd_last": None}

    def emit_conv_chunk(cc):
        t_ld = pr.dma("sync", lambda h, cc=cc: h.dma_start(
            out=cxin[:, 3:2051], in_=xbc_dram[cc * P:(cc + 1) * P, :]),
            waits=[conv_state["cxin"]])
        tu = pr.v(lambda h, cc=cc: h.tensor_scalar_mul(
            ubuf[:], cxin[:, 0:2048], convw_sb[:, cc * 4:cc * 4 + 1]),
            waits=[t_ld, t_cw, conv_state["u"]])
        for kk2 in range(1, 4):
            tu = pr.v(lambda h, cc=cc, kk2=kk2: h.scalar_tensor_tensor(
                ubuf[:], cxin[:, kk2:2048 + kk2],
                convw_sb[:, cc * 4 + kk2:cc * 4 + kk2 + 1], ubuf[:],
                ALU.mult, ALU.add), waits=[tu])
        conv_state["cxin"] = tu
        t_si = pr.s(lambda h, cc=cc: h.activation(
            cxo[:], ubuf[:], AF.Silu, bias=convb_sb[:, cc:cc + 1]),
            waits=[tu, t_cb, conv_state["cxo"]])
        conv_state["u"] = t_si
        last = None
        for q in range(4):
            last = pr.dma("scalar", lambda h, cc=cc, q=q: h.dma_start(
                out=convq_dram[q, cc * P:(cc + 1) * P, :],
                in_=cxo[:, q * 512:(q + 1) * 512]), waits=[t_si])
        conv_state["cxo"] = last
        conv_state["qd_last"] = last

    w1r = w1.rearrange("e (k p) i -> e p k i", p=P)
    w3r = w3.rearrange("e (k p) i -> e p k i", p=P)
    sw1r = sw1.rearrange("(k p) i -> p k i", p=P)
    sw3r = sw3.rearrange("(k p) i -> p k i", p=P)

    tok_wblk = [None, None]
    tok_hps = [None, None]
    tok_gps = [None, None]
    tok_sh = [None, None]
    tok_a1 = [None, None]
    tok_ast = [None] * 4
    tok_meb = None
    t_meb_frees = [t_rs2b]
    it_i = [0]
    ast_i = [0]
    blk_i = [0]

    ph_slot = [(pdub[:], 0), (pbig[:], 0)]
    pg_slot = [(pdub[:], 512), (pbig[:], 512)]

    for ei in range(9):
        routed = ei < E_RT
        row0 = ei * I_RT if routed else E_RT * I_RT
        if routed:
            t_mer = pr.dma("sync", lambda h, ei=ei: h.dma_start(
                out=apx(gwf[0:1, :], [[P, NTILE], [1, P]]),
                in_=bass.AP(m_dram.tensor, ei * P,
                            [[1, 1], [E_RT * P, NTILE], [1, P]])),
                waits=[t_mout, tok_meb])
            kk = []
            for j in range(4):
                kk.append(pr.t(lambda h, j=j: h.matmul(
                    pbig[:, j * 512:(j + 1) * 512], sb_ones[:],
                    gwf[0:1, j * 512:(j + 1) * 512], start=True, stop=True),
                    waits=[t_mer, tok_hps[1], tok_gps[1]] + t_meb_frees))
            tok_meb = pr.v(lambda h: h.tensor_copy(me_b[:], pbig[:]), waits=[kk[-1]])
            t_meb_frees = [tok_meb]
        if routed:
            nclist = [(0, 512), (512, 512), (1024, 384)]
        else:
            nclist = [(c, min(512, I_SH - c)) for c in range(0, I_SH, 512)]
        for (c0, C) in nclist:
            pa = blk_i[0] % 2
            blk_i[0] += 1
            if blk_i[0] - 1 < NCCH:
                emit_conv_chunk(blk_i[0] - 1)
            if routed:
                src1 = w1r[ei, :, :, c0:c0 + C]
                src3 = w3r[ei, :, :, c0:c0 + C]
            else:
                src1 = sw1r[:, :, c0:c0 + C]
                src3 = sw3r[:, :, c0:c0 + C]
            first2 = blk_i[0] <= 2
            t_s1 = pr.dma("gpsimd", lambda h, pa=pa, src1=src1, C=C: h.dma_start(
                out=apx(w1blk[pa][:], [[512, NK], [1, C]]), in_=src1),
                waits=[tok_wblk[pa]] + (barrier23 if first2 else []))
            t_s3 = pr.dma("gpsimd", lambda h, pa=pa, src3=src3, C=C: h.dma_start(
                out=apx(w3blk[pa][:], [[512, NK], [1, C]]), in_=src3),
                waits=[tok_wblk[pa]] + (barrier23 if first2 else []))
            last_mm = None
            for ic in range(C // P):
                for tn in range(4):
                    sp = it_i[0] % 2
                    it_i[0] += 1
                    pht, phc = ph_slot[sp]
                    pgt, pgc = pg_slot[sp]
                    hmms = []
                    gmms = []
                    for k in range(NK):
                        hmms.append(pr.t(
                            lambda h, k=k, ic=ic, tn=tn, pa=pa, pht=pht, phc=phc:
                            h.matmul(pht[:, phc:phc + 512],
                                     w1blk[pa][:, k * 512 + ic * P:k * 512 + (ic + 1) * P],
                                     xnf[:, k * T + tn * 512:k * T + (tn + 1) * 512],
                                     start=(k == 0), stop=(k == NK - 1)),
                            waits=[t_s1, tok_hps[sp]]
                            + ([t_meb_frees[0]] if routed else [])))
                    for k in range(NK):
                        gmms.append(pr.t(
                            lambda h, k=k, ic=ic, tn=tn, pa=pa, pgt=pgt, pgc=pgc:
                            h.matmul(pgt[:, pgc:pgc + 512],
                                     w3blk[pa][:, k * 512 + ic * P:k * 512 + (ic + 1) * P],
                                     xnf[:, k * T + tn * 512:k * T + (tn + 1) * 512],
                                     start=(k == 0), stop=(k == NK - 1)),
                            waits=[t_s3, tok_gps[sp]]))
                    last_mm = gmms[-1]
                    t_sh = pr.s(lambda h, sp=sp, pht=pht, phc=phc: h.activation(
                        shtile[sp][:], pht[:, phc:phc + 512], AF.Silu),
                        waits=[hmms[-1], tok_sh[sp]])
                    tok_hps[sp] = t_sh
                    ai = ast_i[0] % 4
                    ast_i[0] += 1
                    if routed:
                        t_a1 = pr.v(lambda h, sp=sp, pgt=pgt, pgc=pgc: h.tensor_mul(
                            a1tile[sp][:], shtile[sp][:], pgt[:, pgc:pgc + 512]),
                            waits=[t_sh, gmms[-1], tok_a1[sp]])
                        tok_gps[sp] = t_a1
                        tok_sh[sp] = t_a1
                        t_a2 = pr.v(lambda h, sp=sp, ai=ai, tn=tn: h.tensor_mul(
                            astage[ai][:], a1tile[sp][:],
                            me_b[:, tn * 512:(tn + 1) * 512]),
                            waits=[t_a1, tok_meb, tok_ast[ai]])
                        tok_a1[sp] = t_a2
                    else:
                        t_a2 = pr.v(lambda h, sp=sp, ai=ai, pgt=pgt, pgc=pgc:
                                    h.tensor_mul(astage[ai][:], shtile[sp][:],
                                                 pgt[:, pgc:pgc + 512]),
                                    waits=[t_sh, gmms[-1], tok_ast[ai]])
                        tok_gps[sp] = t_a2
                        tok_sh[sp] = t_a2
                    r0 = row0 + c0 + ic * P
                    tok_ast[ai] = pr.dma(
                        "scalar", lambda h, ai=ai, r0=r0, tn=tn: h.dma_start(
                            out=actT_dram[r0:r0 + P, tn * 512:(tn + 1) * 512],
                            in_=astage[ai][:]), waits=[t_a2])
            tok_wblk[pa] = last_mm

    barrier3b = [tok_ast[i] for i in range(4)] + [tok_meb] + pr.fence_all()
    if debug:
        t_dbg_xnf = pr.dma("gpsimd", lambda h: h.dma_start(out=dbg_xnf, in_=xnf[:]),
                           waits=pr.fence())
        barrier3b.append(t_dbg_xnf)
    s3.close()
    s_xnf.close()

    s4 = ExitStack()
    SB4 = mkSB(s4)
    y_t = SB4("y_t", [P, NTILE * 1024])
    atst = SB4("atst", [P, 16 * 1024], BF16)
    w2st = [SB4("w2st0", [P, 16 * 1024], BF16), SB4("w2st1", [P, 16 * 1024], BF16)]

    w2r = w2.rearrange("e (c p) n -> e p c n", p=P)
    sw2r = sw2.rearrange("(c p) n -> p c n", p=P)

    def w2_runs(cc0, ncc):
        runs = []
        cc = cc0
        while cc < cc0 + ncc:
            if cc < 88:
                e = cc // 11
                cstart = cc % 11
                ln = min(11 - cstart, cc0 + ncc - cc)
                runs.append(("w2", e, cstart, ln, cc - cc0))
            else:
                cstart = cc - 88
                ln = cc0 + ncc - cc
                runs.append(("sw2", 0, cstart, ln, cc - cc0))
            cc += ln
        return runs

    tok_w2st = [barrier3b, barrier3b]
    tok_at = [None, None]
    tok_yps = [None, None, None]
    tok_yrow = {}
    kgs = [(kg * 16, min(16, NKST - kg * 16)) for kg in range(7)]
    actT_r = actT_dram.rearrange("(c p) t -> p c t", p=P)

    for dh in range(2):
        tok_ylast = {}
        for gi, (cc0, ncc) in enumerate(kgs):
            wpa = gi % 2
            stoks = []
            ci2tok = [None] * ncc
            for (kind, e, cstart, ln, boff) in w2_runs(cc0, ncc):
                if kind == "w2":
                    src = w2r[e, :, cstart:cstart + ln, dh * 1024:(dh + 1) * 1024]
                else:
                    src = sw2r[:, cstart:cstart + ln, dh * 1024:(dh + 1) * 1024]
                tk_run = pr.dma(
                    "gpsimd", lambda h, wpa=wpa, src=src, ln=ln, boff=boff:
                    h.dma_start(out=apx(w2st[wpa][:], [[1024, ln], [1, 1024]],
                                        extra_off=boff * 1024), in_=src),
                    waits=tok_w2st[wpa])
                stoks.append(tk_run)
                for cci in range(boff, boff + ln):
                    ci2tok[cci] = tk_run
            for thalf in range(2):
                athalf_toks = []
                for ah in range(2):
                    nc_h = min(8, ncc - ah * 8)
                    if nc_h <= 0:
                        athalf_toks.append(athalf_toks[0])
                        continue
                    athalf_toks.append(pr.dma(
                        "sync", lambda h, cc0=cc0, ah=ah, nc_h=nc_h, thalf=thalf:
                        h.dma_start(
                            out=apx(atst[:], [[1024, nc_h], [1, 1024]],
                                    extra_off=ah * 8 * 1024),
                            in_=actT_r[:, cc0 + ah * 8:cc0 + ah * 8 + nc_h,
                                       thalf * 1024:(thalf + 1) * 1024]),
                        waits=[tok_at[ah]]
                        + (barrier3b if (dh == 0 and gi == 0) else [])))
                for M in range(8):
                    Mg = thalf * 8 + M
                    psl = (gi * 16 + thalf * 8 + M) % 3
                    pps = [pdub[:, 0:1024], pbig[:, 0:1024], pbig[:, 1024:2048]][psl]
                    mms = []
                    for ci in range(ncc):
                        lhs_w = [athalf_toks[ci // 8]]
                        for hh2 in range(2):
                            mms.append(pr.t(
                                lambda h, ci=ci, M=M, wpa=wpa, pps=pps, hh2=hh2:
                                h.matmul(pps[:, hh2 * 512:(hh2 + 1) * 512],
                                         atst[:, ci * 1024 + M * P:ci * 1024 + (M + 1) * P],
                                         w2st[wpa][:, ci * 1024 + hh2 * 512:ci * 1024 + (hh2 + 1) * 512],
                                         start=(ci == 0), stop=(ci == ncc - 1)),
                                waits=lhs_w + [ci2tok[ci], tok_yps[psl]]))
                    if gi == 0:
                        t_y = pr.v(lambda h, Mg=Mg, pps=pps: h.tensor_copy(
                            y_t[:, Mg * 1024:(Mg + 1) * 1024], pps),
                            waits=[mms[-1], tok_yrow.get(Mg)])
                    else:
                        t_y = pr.v(lambda h, Mg=Mg, pps=pps: h.tensor_add(
                            y_t[:, Mg * 1024:(Mg + 1) * 1024],
                            y_t[:, Mg * 1024:(Mg + 1) * 1024], pps),
                            waits=[mms[-1], tok_ylast[Mg]])
                    tok_ylast[Mg] = t_y
                    tok_yps[psl] = t_y
                    if ncc > 8:
                        tok_at[0] = mms[2 * 8 - 1]
                        tok_at[1] = mms[-1]
                    else:
                        tok_at[0] = mms[-1]
            tok_w2st[wpa] = [mms[-1]]
        for Mg in range(NTILE):
            tok_yrow[Mg] = pr.dma(
                "scalar", lambda h, Mg=Mg, dh=dh: h.dma_start(
                    out=ffn_dram[Mg * P:(Mg + 1) * P, dh * 1024:(dh + 1) * 1024],
                    in_=y_t[:, Mg * 1024:(Mg + 1) * 1024]),
                waits=[tok_ylast[Mg]])

    barrier4 = [tok_yrow[Mg] for Mg in range(NTILE)] + pr.fence_all()
    s4.close()

    s5 = ExitStack()
    SB5 = mkSB(s5)
    g_fm = SB5("g_fm", [P, NK * T], BF16)
    esel_b = SB5("esel_b", [HEADS, HEADS * P], BF16)
    t_esel = pr.dma("gpsimd", lambda h: h.dma_start(out=esel_b[:], in_=esel),
                    waits=barrier4)

    barrier4b = barrier4 + [conv_state["qd_last"], t_adt, t_dskb, t_esel]

    s5b = ExitStack()
    SB5B = mkSB(s5b)
    cq = SB5B("cq", [P, NCCH * 512], BF16)
    S_buf = [SB5B("S0", [P, 2 * T]), SB5B("S1", [P, 2 * T])]
    prev_bf = SB5B("prev_bf", [P, 2 * T], BF16)
    zt = SB5B("zt", [P, T])
    dtt = SB5B("dtt", [P, 96])
    acr = SB5B("acr", [P, P])
    ssm = SB5B("ssm", [P, 160])
    ahi = SB5B("ahi", [HEADS, P], BF16)
    alo = SB5B("alo", [HEADS, P], BF16)
    cbm = SB5B("cbm", [P, P], BF16)
    xs_tm = SB5B("xs_tm", [P, T], BF16)
    btm = SB5B("btm", [P, 2 * P], BF16)
    xtw = SB5B("xtw", [P, T], BF16)
    relb = [SB5B("relb0", [P, P]), SB5B("relb1", [P, P])]
    mpb = [SB5B("mpb0", [P, P], BF16), SB5B("mpb1", [P, P], BF16)]
    ybuf = SB5B("ybuf", [P, 1024])
    ybuf2 = SB5B("ybuf2", [P, 1024])
    yfin = SB5B("yfin", [P, T])
    tbuf = [SB5B("tbuf0", [P, 512]), SB5B("tbuf1", [P, 512])]
    gst = SB5B("gst", [P, 4])
    gtbf = SB5B("gtbf", [P, T], BF16)

    t_s0 = pr.v(lambda h: h.memset(S_buf[0][:], 0.0), waits=barrier4b)
    scur = 0
    tok_S = [t_s0, None]
    tok_Sread = [[], []]
    tok_tbuf = [None, None]
    tok_ybuf2 = None
    tok_zsb = None
    tok_cq = None
    tok_zt = t_s0
    tok_prev = None
    tok_pdub_free = None
    tok_pbig_free = None
    tok_pmix_free = t_cb
    tok_gtr_done = None

    for q in range(4):
        t_cq = pr.dma("sync", lambda h, q=q: h.dma_start(
            out=cq.ap().rearrange("p (c t) -> p c t", t=512),
            in_=convq_dram[q].rearrange("(c p) t -> p c t", p=P)),
            waits=[tok_cq] + (barrier4b if q == 0 else []))
        for ppq in range(4):
            pi = q * 4 + ppq
            t0 = pi * P
            pf = pr.fence()
            tok_zt = pr.dma("sync", lambda h, t0=t0: h.dma_start(
                out=zt[:], in_=z_dram[t0:t0 + P, :]), waits=[tok_zt])

            ta = []
            for j in range(2):
                ta.append(pr.t(lambda h, j=j, t0=t0: h.transpose(
                    pmix[:, j * 32:(j + 1) * 32], dt3[:, j * T + t0:j * T + t0 + P],
                    sb_iden_f[0:32, 0:32]),
                    waits=[t_adt, t_lndt, tok_pmix_free] + pf))
            t_dtt0 = pr.v(lambda h: h.tensor_copy(dtt[:, 0:64], pmix[:, 0:64]),
                          waits=[ta[-1]] + pf)
            t_dtt = pr.s(lambda h: h.activation(dtt[:, 64:96], dtt[:, 32:64],
                                                AF.Ln), waits=[t_dtt0] + pf)
            tb = []
            for j, cst in enumerate((sb_tri, sb_ut, sb_e63, sb_e127)):
                tb.append(pr.t(lambda h, j=j, cst=cst: h.matmul(
                    pdub[:, j * 32:(j + 1) * 32], cst[:], dtt[:, 0:32],
                    start=True, stop=True),
                    waits=[t_dtt0, tok_pdub_free]))
            t_acr = pr.v(lambda h: h.tensor_copy(acr[:], pdub[:, 0:P]),
                         waits=[tb[-1]] + pf)
            t_eacs = pr.s(lambda h: h.activation(ssm[:, 0:32], acr[:, 0:32], AF.Exp),
                          waits=[t_acr] + pf)
            t_wdec = pr.s(lambda h: h.activation(ssm[:, 32:64], acr[:, 32:64],
                                                 AF.Exp), waits=[t_acr] + pf)
            t_w128 = pr.v(lambda h: h.tensor_mul(ssm[:, 64:96], ssm[:, 32:64],
                                                 dtt[:, 32:64]),
                          waits=[t_wdec, t_dtt0])
            t_cdb1 = pr.s(lambda h: h.activation(ssm[:, 96:128], acr[:, 64:96],
                                                 AF.Exp), waits=[t_acr] + pf)
            t_cdb2 = pr.s(lambda h: h.activation(ssm[:, 128:160], acr[:, 96:128],
                                                 AF.Exp), waits=[t_acr] + pf)
            t_afm = pr.t(lambda h: h.transpose(pmix[0:32, 0:P], acr[:, 0:32],
                                               sb_iden_f[:]),
                         waits=[t_acr, t_dtt0])
            t_ahi = pr.v(lambda h: h.tensor_copy(ahi[:], pmix[0:32, 0:P]),
                         waits=[t_afm] + pf)
            t_alo = pr.v(lambda h: h.tensor_tensor(alo[:], pmix[0:32, 0:P], ahi[:],
                                                   ALU.subtract), waits=[t_ahi])
            tok_pmix_free = t_alo
            te1 = pr.t(lambda h, ppq=ppq: h.matmul(
                pdub[:, 128:256], cq[:, 16 * 512 + ppq * P:16 * 512 + (ppq + 1) * P],
                cq[:, 18 * 512 + ppq * P:18 * 512 + (ppq + 1) * P],
                start=True, stop=False), waits=[t_cq, t_acr])
            te2 = pr.t(lambda h, ppq=ppq: h.matmul(
                pdub[:, 128:256], cq[:, 17 * 512 + ppq * P:17 * 512 + (ppq + 1) * P],
                cq[:, 19 * 512 + ppq * P:19 * 512 + (ppq + 1) * P],
                start=False, stop=True), waits=[te1])
            t_cbm = pr.v(lambda h: h.tensor_mul(cbm[:], pdub[:, 128:256], sb_tri[:]),
                         waits=[te2] + pf)
            ev_toks = []
            for grp in range(3):
                kl = [grp * 8 + j for j in range(8)] if grp < 2 else [16, 17]
                tts = []
                for j, k in enumerate(kl):
                    tts.append(pr.t(lambda h, j=j, k=k, ppq=ppq: h.transpose(
                        pbf[:, j * P:(j + 1) * P],
                        cq[:, k * 512 + ppq * P:k * 512 + (ppq + 1) * P],
                        sb_iden_b[:]),
                        waits=[t_cq, tok_pbf_free]))
                if grp < 2:
                    tok_pbf_free = pr.v(lambda h, grp=grp: h.tensor_copy(
                        xs_tm[:, grp * 1024:(grp + 1) * 1024], pbf[:, 0:1024]),
                        waits=[tts[-1]] + pf)
                else:
                    tok_pbf_free = pr.v(lambda h: h.tensor_copy(
                        btm[:], pbf[:, 0:256]), waits=[tts[-1]] + pf)
                ev_toks.append(tok_pbf_free)
            t_xs, t_xs2, t_btm = ev_toks
            t_xtw = pr.v(lambda h: h.tensor_tensor(
                apx(xtw[:], [[64, 32], [1, 64]]),
                apx(xs_tm[:], [[64, 32], [1, 64]]),
                apx(ssm[:, 64:96], [[1, 32], [0, 64]]), ALU.mult),
                waits=[t_xs, t_xs2, t_w128] + pf)
            tc0 = []
            for ncn in range(2):
                tc0.append(pr.s(lambda h, ncn=ncn, scur=scur: h.activation(
                    prev_bf[:, ncn * T:(ncn + 1) * T],
                    S_buf[scur][:, ncn * T:(ncn + 1) * T], AF.Copy),
                    waits=[tok_S[scur], tok_prev]))
            t_prev_w = tc0[-1]
            tok_Sread[scur].append(t_prev_w)
            t_ydiag = []
            tok_slot = [None] * 4
            tok_rel = [None, None]
            tok_lm = [None, None]
            tok_mp = [None, None]
            for hh in range(HEADS):
                sl = hh % 4
                bp = pdub[:, 512 + sl * P:512 + (sl + 1) * P]
                b1 = pr.t(lambda h, hh=hh, bp=bp: h.matmul(
                    bp, esel_b[:, hh * P:(hh + 1) * P], ahi[:],
                    start=True, stop=False),
                    waits=[t_alo, tok_slot[sl], t_esel])
                b2 = pr.t(lambda h, hh=hh, bp=bp: h.matmul(
                    bp, esel_b[:, hh * P:(hh + 1) * P], alo[:],
                    start=False, stop=True), waits=[b1])
                pb2 = hh % 2
                t_rel = pr.s(lambda h, hh=hh, bp=bp, pb2=pb2: h.activation(
                    relb[pb2][:], bp, AF.Relu, scale=-1.0,
                    bias=acr[:, hh:hh + 1]), waits=[b2, tok_rel[pb2]])
                tok_slot[sl] = t_rel
                t_lmx = pr.s(lambda h, hh=hh, pb2=pb2: h.activation(
                    relb[pb2][:], relb[pb2][:], AF.Exp, scale=-1.0,
                    bias=dtt[:, 64 + hh:64 + hh + 1]), waits=[t_rel, tok_lm[pb2]])
                t_mp = pr.g(lambda h, pb2=pb2: h.tensor_mul(
                    mpb[pb2][:], relb[pb2][:], cbm[:]),
                    waits=[t_lmx, t_cbm, tok_mp[pb2]])
                tok_lm[pb2] = t_mp
                tok_rel[pb2] = t_mp
                t_yd = pr.t(lambda h, hh=hh, pb2=pb2: h.matmul(
                    pbig[:, hh * HD:(hh + 1) * HD], mpb[pb2][:],
                    xs_tm[:, hh * HD:(hh + 1) * HD], start=True, stop=True),
                    waits=[t_mp, t_xs, t_xs2, tok_pbig_free])
                tok_mp[pb2] = t_yd
                t_ydiag.append(t_yd)
            t_asm = []
            sA, sB = scur, 1 - scur
            for c in range(2):
                src_b = sA if c == 0 else sB
                dst_b = sB if c == 0 else sA
                Lc = slice(c * 64, (c + 1) * 64)
                cdb_off = 96 + c * 32
                if c == 1:
                    tc1 = []
                    for ncn in range(2):
                        tc1.append(pr.s(lambda h, ncn=ncn, src_b=src_b: h.activation(
                            prev_bf[:, ncn * T:(ncn + 1) * T],
                            S_buf[src_b][:, ncn * T:(ncn + 1) * T], AF.Copy),
                            waits=[tok_S[src_b], tok_prev]))
                    t_prev_w = tc1[-1]
                    tok_Sread[src_b].append(t_prev_w)
                for r in range(2):
                    yo = []
                    for jj in range(2):
                        for ncn in range(2):
                            yo.append(pr.t(
                                lambda h, c=c, r=r, jj=jj, ncn=ncn, ppq=ppq, Lc=Lc:
                                h.matmul(
                                    pdub[Lc, jj * 512:(jj + 1) * 512],
                                    cq[:, (18 + ncn) * 512 + ppq * P + c * 64:
                                        (18 + ncn) * 512 + ppq * P + c * 64 + 64],
                                    prev_bf[:, ncn * T + r * 1024 + jj * 512:
                                            ncn * T + r * 1024 + (jj + 1) * 512],
                                    start=(ncn == 0), stop=(ncn == 1)),
                                waits=[t_prev_w, t_cq, tok_pdub_free,
                                       tok_slot[0], tok_slot[1], tok_slot[2],
                                       tok_slot[3]]))
                    tok_prev = yo[-1]
                    a1 = pr.v(lambda h, Lc=Lc, r=r: h.tensor_tensor(
                        apx(ybuf[Lc, :], [[64, 16], [1, 64]]),
                        apx(pdub[Lc, 0:1024], [[64, 16], [1, 64]]),
                        apx(ssm[Lc, r * 16:r * 16 + 16], [[1, 16], [0, 64]]),
                        ALU.mult), waits=[yo[-1], t_eacs])
                    a2 = pr.g(lambda h, Lc=Lc, r=r: h.tensor_tensor(
                        apx(ybuf2[Lc, :], [[64, 16], [1, 64]]),
                        apx(xs_tm[Lc, r * 1024:(r + 1) * 1024], [[64, 16], [1, 64]]),
                        apx(dskb[Lc, r * 16:r * 16 + 16], [[1, 16], [0, 64]]),
                        ALU.mult), waits=[t_xs, t_xs2, t_dskb, tok_ybuf2])
                    a3 = pr.v(lambda h, Lc=Lc: h.tensor_add(
                        ybuf[Lc, :], ybuf[Lc, :], ybuf2[Lc, :]), waits=[a1, a2])
                    tok_ybuf2 = a3
                    a4 = pr.v(lambda h, Lc=Lc, r=r: h.tensor_add(
                        yfin[Lc, r * 1024:(r + 1) * 1024], ybuf[Lc, :],
                        pbig[Lc, r * 1024:(r + 1) * 1024]),
                        waits=[a3, t_ydiag[-1]])
                    t_asm.append(a4)
                    tok_pdub_free = a4
                for ncn in range(2):
                    for jj in range(4):
                        psl2 = jj % 2
                        pst = pdub[:, psl2 * 512:(psl2 + 1) * 512]
                        t_st = pr.t(lambda h, c=c, ncn=ncn, jj=jj, pst=pst:
                                    h.matmul(
                                        pst,
                                        btm[c * 64:(c + 1) * 64, ncn * P:(ncn + 1) * P],
                                        xtw[c * 64:(c + 1) * 64, jj * 512:(jj + 1) * 512],
                                        start=True, stop=True),
                                    waits=[t_btm, t_xtw, tok_pdub_free])
                        t_tb = pr.g(lambda h, ncn=ncn, jj=jj, src_b=src_b,
                                    cdb_off=cdb_off:
                                    h.tensor_tensor(
                                        apx(tbuf[jj % 2][:], [[64, 8], [1, 64]]),
                                        apx(S_buf[src_b][:, ncn * T + jj * 512:ncn * T + (jj + 1) * 512],
                                            [[64, 8], [1, 64]]),
                                        apx(ssm[:, cdb_off + jj * 8:cdb_off + jj * 8 + 8],
                                            [[1, 8], [0, 64]]),
                                        ALU.mult),
                                    waits=[tok_S[src_b], t_cdb1, t_cdb2,
                                           tok_tbuf[jj % 2]])
                        tok_Sread[src_b].append(t_tb)
                        t_up = pr.v(lambda h, ncn=ncn, jj=jj, dst_b=dst_b, pst=pst:
                                    h.tensor_add(
                                        S_buf[dst_b][:, ncn * T + jj * 512:ncn * T + (jj + 1) * 512],
                                        tbuf[jj % 2][:], pst),
                                    waits=[t_st, t_tb] + tok_Sread[dst_b])
                        tok_pdub_free = t_up
                        tok_tbuf[jj % 2] = t_up
                        tok_S[dst_b] = t_up
                tok_Sread[dst_b] = []
            dump_toks = []
            if debug and pi == 0:
                dump_toks.append(pr.dma("scalar", lambda h: h.dma_start(
                    out=dbg_yfin, in_=yfin[:]), waits=t_asm))
                dump_toks.append(pr.dma("scalar", lambda h: h.dma_start(
                    out=dbg_acr, in_=acr[:]), waits=[t_acr] + t_asm))
                dump_toks.append(pr.dma("scalar", lambda h: h.dma_start(
                    out=dbg_ssm, in_=ssm[:]), waits=[t_w128, t_cdb2] + t_asm))
                dump_toks.append(pr.dma("gpsimd", lambda h: h.dma_start(
                    out=dbg_xtw, in_=xtw[:]), waits=[t_xtw] + t_asm))
                dump_toks.append(pr.dma("scalar", lambda h: h.dma_start(
                    out=dbg_S, in_=S_buf[sA][:]), waits=[tok_S[sA]] + t_asm))
            t_zs = pr.s(lambda h: h.activation(zt[:], zt[:], AF.Silu),
                        waits=[tok_zt, tok_zsb])
            if debug and pi == 0:
                dump_toks.append(pr.dma("scalar", lambda h: h.dma_start(
                    out=dbg_zs, in_=zt[:]), waits=[t_zs]))
            t_yg = pr.v(lambda h: h.tensor_mul(yfin[:], yfin[:], zt[:]),
                        waits=[t_zs] + t_asm + dump_toks)
            if debug and pi == 0:
                t_dyg = pr.dma("scalar", lambda h: h.dma_start(
                    out=dbg_yg, in_=yfin[:]), waits=[t_yg])
                dump_toks.append(t_dyg)
            t_gq = pr.s(lambda h: h.activation(zt[:], yfin[:], AF.Square,
                                               accum_out=gst[:, 0:1]),
                        waits=[t_yg] + dump_toks)
            tok_zsb = t_gq
            t_gl = pr.s(lambda h: h.activation(gst[:, 1:2], gst[:, 0:1], AF.Ln,
                                               bias=c_eps, scale=c_invd),
                        waits=[t_gq])
            t_gr = pr.s(lambda h: h.activation(gst[:, 2:3], gst[:, 1:2], AF.Exp,
                                               scale=-0.5), waits=[t_gl])
            t_gbf = pr.v(lambda h: h.tensor_scalar_mul(gtbf[:], yfin[:],
                                                       gst[:, 2:3]),
                         waits=[t_gr, t_yg] + pf)
            if debug and pi == 0:
                t_dgt = pr.dma("gpsimd", lambda h: h.dma_start(
                    out=dbg_gt, in_=gtbf[:]), waits=[t_gbf])
                t_dgs = pr.dma("scalar", lambda h: h.dma_start(
                    out=dbg_gst, in_=gst[:]), waits=[t_gr])
                t_gbf = pr.v(lambda h: h.engine_nop(), waits=[t_gbf, t_dgt, t_dgs])
            for grp in range(2):
                tts = []
                for j in range(8):
                    k = grp * 8 + j
                    tts.append(pr.t(lambda h, j=j, k=k: h.transpose(
                        pbf[:, j * P:(j + 1) * P], gtbf[:, k * P:(k + 1) * P],
                        sb_iden_b[:]), waits=[t_gbf, tok_pbf_free]))
                off = (grp * 8) * T + t0
                tok_pbf_free = pr.v(lambda h, off=off: h.tensor_copy(
                    apx(g_fm[:], [[T, 8], [1, P]], extra_off=off),
                    apx(pbf[:], [[P, 8], [1, P]])), waits=[tts[-1]])
            tok_gtr_done = tok_pbf_free
            tok_pbig_free = t_asm[-1]
            tok_zt = t_gq
        tok_cq = tok_gtr_done

    s5b.close()
    s5c = ExitStack()
    SB5C = mkSB(s5c)
    w32st = SB5C("w32st", [P, NK * 512])
    wfull = SB5C("wfull", [P, NK * 2048], BF16)
    ustage = [SB5C("ust0", [P, 512], BF16), SB5C("ust1", [P, 512], BF16)]

    w_out_r = w_out.rearrange("(k p) c -> p k c", p=P)
    t_w32_free2 = None
    tok_wf = []
    for b in range(4):
        t_wd = pr.dma("sync", lambda h, b=b: h.dma_start(
            out=apx(w32st[:], [[512, NK], [1, 512]]),
            in_=w_out_r[:, :, b * 512:(b + 1) * 512]),
            waits=[t_w32_free2, tok_gtr_done] + (pr.fence() if b == 0 else []))
        casts2 = []
        for k in range(NK):
            casts2.append(pr.v(lambda h, b=b, k=k: h.tensor_scalar_mul(
                wfull[:, k * 2048 + b * 512:k * 2048 + (b + 1) * 512],
                w32st[:, k * 512:(k + 1) * 512], sb_gnormw[:, k:k + 1]),
                waits=[t_wd, tok_consts]))
        t_w32_free2 = casts2[-1]
        tok_wf.append(casts2[-1])

    tok_ups = [tok_pdub_free, tok_pbig_free]
    tok_ust = [None, None]
    ui = [0]
    for uc in range(NK):
        for tn in range(4):
            sp = (uc * 4 + tn) % 2
            pps = [pdub[:, 0:512], pbig[:, 0:512]][sp]
            mms = []
            for k in range(NK):
                mms.append(pr.t(lambda h, uc=uc, tn=tn, k=k, pps=pps: h.matmul(
                    pps, wfull[:, k * 2048 + uc * P:k * 2048 + (uc + 1) * P],
                    g_fm[:, k * T + tn * 512:k * T + (tn + 1) * 512],
                    start=(k == 0), stop=(k == NK - 1)),
                    waits=tok_wf + [tok_ups[sp], tok_gtr_done]))
            usl = ui[0] % 2
            ui[0] += 1
            t_ue = pr.s(lambda h, usl=usl, pps=pps: h.activation(
                ustage[usl][:], pps, AF.Copy), waits=[mms[-1], tok_ust[usl]])
            tok_ups[sp] = t_ue
            tok_ust[usl] = pr.dma(
                "scalar", lambda h, usl=usl, uc=uc, tn=tn: h.dma_start(
                    out=u_dram[uc * P:(uc + 1) * P, tn * 512:(tn + 1) * 512],
                    in_=ustage[usl][:]), waits=[t_ue])

    barrier5 = [tok_ust[0], tok_ust[1]] + pr.fence_all()
    if debug:
        t_dbg_g = pr.dma("gpsimd", lambda h: h.dma_start(out=dbg_gfm, in_=g_fm[:]),
                         waits=[tok_gtr_done] + pr.fence())
        barrier5.append(t_dbg_g)
        t_dbg_dtv = pr.dma("scalar", lambda h: h.dma_start(out=dbg_dtv, in_=dt3[:]),
                           waits=[t_adt] + pr.fence())
        barrier5.append(t_dbg_dtv)
    s5c.close()
    s5.close()

    s6 = ExitStack()
    SB6 = mkSB(s6)
    ublock = SB6("ublock", [P, NK * 1024], BF16)
    wobf = SB6("wobf", [P, NK * 2048], BF16)
    xtile = SB6("xtile", [P, D])
    ftile = SB6("ftile", [P, D])
    ostg = [SB6("ostg%d" % i, [P, 512]) for i in range(4)]

    wo_r = wo_w.rearrange("(k p) c -> p k c", p=P)
    t_wo = pr.dma("gpsimd", lambda h: h.dma_start(
        out=apx(wobf[:], [[2048, NK], [1, 2048]]), in_=wo_r), waits=barrier5)

    tok_ub = None
    tok_xf = None
    tok_og = [None] * 4
    tok_vps = [None, None, None]
    oi2 = [0]
    t_lastout = None
    u_r = u_dram.rearrange("(k p) t -> p k t", p=P)
    for thalf in range(2):
        tok_ub = pr.dma("sync", lambda h, thalf=thalf: h.dma_start(
            out=apx(ublock[:], [[1024, NK], [1, 1024]]),
            in_=u_r[:, :, thalf * 1024:(thalf + 1) * 1024]),
            waits=[tok_ub] + barrier5 + ([tok_vps[0], tok_vps[1], tok_vps[2]]
                                         if thalf else []))
        for M in range(8):
            Mg = thalf * 8 + M
            tok_xf = pr.dma("sync", lambda h, Mg=Mg: h.dma_start(
                out=xtile[:], in_=x[Mg * P:(Mg + 1) * P, :]),
                waits=[tok_xf] + (barrier5 if Mg == 0 else []))
            tok_xf = pr.dma("sync", lambda h, Mg=Mg: h.dma_start(
                out=ftile[:], in_=ffn_dram[Mg * P:(Mg + 1) * P, :]),
                waits=[tok_xf])
            last_o2 = None
            for vn in range(4):
                sp = (M * 4 + vn) % 3
                pps = [pdub[:, 0:512], pbig[:, 0:512], pbig[:, 512:1024]][sp]
                mms = []
                for k in range(NK):
                    mms.append(pr.t(lambda h, M=M, vn=vn, k=k, pps=pps: h.matmul(
                        pps, ublock[:, k * 1024 + M * P:k * 1024 + (M + 1) * P],
                        wobf[:, k * 2048 + vn * 512:k * 2048 + (vn + 1) * 512],
                        start=(k == 0), stop=(k == NK - 1)),
                        waits=[t_wo, tok_ub, tok_vps[sp]]))
                oi = oi2[0] % 4
                oi2[0] += 1
                o1 = pr.v(lambda h, oi=oi, vn=vn, pps=pps: h.tensor_add(
                    ostg[oi][:], pps, xtile[:, vn * 512:(vn + 1) * 512]),
                    waits=[mms[-1], tok_xf, tok_og[oi]])
                o2 = pr.v(lambda h, oi=oi, vn=vn: h.tensor_add(
                    ostg[oi][:], ostg[oi][:], ftile[:, vn * 512:(vn + 1) * 512]),
                    waits=[o1])
                tok_vps[sp] = o2
                last_o2 = o2
                t_lastout = pr.dma(
                    "scalar", lambda h, oi=oi, Mg=Mg, vn=vn: h.dma_start(
                        out=out[Mg * P:(Mg + 1) * P, vn * 512:(vn + 1) * 512],
                        in_=ostg[oi][:]), waits=[o2])
                tok_og[oi] = t_lastout
            tok_xf = last_o2

    tout = t_lastout
    if debug:
        tout = pr.dma("scalar", lambda h: h.dma_start(out=dbg_z, in_=z_dram),
                      waits=[tout])
        tout = pr.dma("scalar", lambda h: h.dma_start(out=dbg_xbc, in_=xbc_dram),
                      waits=[tout])
        tout = pr.dma("scalar", lambda h: h.dma_start(out=dbg_ffn, in_=ffn_dram),
                      waits=[tout])
        tout = pr.dma("scalar", lambda h: h.dma_start(out=dbg_dt, in_=dt_fm[:]),
                      waits=[tout])
        tout = pr.dma("scalar", lambda h: h.dma_start(out=dbg_srs, in_=srs[:]),
                      waits=[tout])
        tout = pr.dma("scalar", lambda h: h.dma_start(out=dbg_stat, in_=stat[:]),
                      waits=[tout])
        tout = pr.dma("scalar", lambda h: h.dma_start(out=dbg_m, in_=m_dram),
                      waits=[tout])
        tout = pr.dma("gpsimd", lambda h: h.dma_start(
            out=dbg_cq, in_=convq_dram.rearrange("q c t -> (q c) t")), waits=[tout])
        tout = pr.dma("gpsimd", lambda h: h.dma_start(out=dbg_act, in_=actT_dram),
                      waits=[tout])

    pr._op("sync", lambda h: h.nop(), [tout], "qs", False)

    s6.close()
    with es, nc.Block() as block:
        pr.finalize(nc, block, sems)
    return nc


def make_constants():
    idx = np.arange(P)
    tri = ((idx[:, None] <= idx[None, :]) &
           (idx[:, None] // CHUNK == idx[None, :] // CHUNK)).astype(np.float32)
    ut = ((idx[:, None] > idx[None, :]) &
          (idx[:, None] // CHUNK == idx[None, :] // CHUNK)).astype(np.float32)
    e63a = np.zeros((P, P), np.float32)
    e63a[0:64, :] = 1.0
    e127a = np.zeros((P, P), np.float32)
    e127a[64:128, :] = 1.0
    esl = np.zeros((HEADS, HEADS * P), np.float32)
    for hh in range(HEADS):
        esl[hh, hh * P:(hh + 1) * P] = 1.0
    return {
        "iden_f": np.eye(P, dtype=np.float32),
        "tri_cum": tri, "ut_strict": ut, "e63": e63a, "e127": e127a,
        "ones1": np.ones((1, P), np.float32),
        "esel": esl,
    }


def make_in_map(core, inputs):
    m = {
        "x": np.ascontiguousarray(inputs["L0_attn_x"][core]),
        "w_in": inputs["L0_attn_w_in"],
        "conv_w": inputs["L0_attn_conv_w"],
        "conv_b2": np.ascontiguousarray(inputs["L0_attn_conv_b"].reshape(NCCH, P)),
        "dt_bias": np.ascontiguousarray(inputs["L0_attn_dt_bias"].reshape(HEADS, 1)),
        "a_log": np.ascontiguousarray(inputs["L0_attn_A_log"].reshape(HEADS, 1)),
        "d_skip": np.ascontiguousarray(inputs["L0_attn_D_skip"].reshape(1, HEADS)),
        "normw2": np.ascontiguousarray(inputs["L0_attn_norm_w"].reshape(NK, P).T),
        "gnormw2": np.ascontiguousarray(inputs["L0_attn_gnorm_w"].reshape(NK, P).T),
        "w_out": inputs["L0_attn_w_out"],
        "wo_w": inputs["L0_attn_wo_w"],
        "gate_w": inputs["L0_ffn_gate_w"],
        "w1": inputs["L0_ffn_w1"],
        "w3": inputs["L0_ffn_w3"],
        "w2": inputs["L0_ffn_w2"],
        "sw1": inputs["L0_ffn_sw1"],
        "sw3": inputs["L0_ffn_sw3"],
        "sw2": inputs["L0_ffn_sw2"],
    }
    m.update(make_constants())
    return m


def kernel(**inputs):
    inputs = {k: np.asarray(v) for k, v in inputs.items()}
    nc = build_program()
    in_maps = [make_in_map(b, inputs) for b in range(8)]
    res = run_bass_kernel_spmd(nc, in_maps, core_ids=list(range(8)))
    return np.stack([np.asarray(res.results[b]["out"]) for b in range(8)], axis=0)


# version 04
# speedup vs baseline: 1.1061x; 1.0655x over previous
import numpy as np
from contextlib import ExitStack

import concourse.bass as bass
import concourse.mybir as mybir
from concourse.bass_utils import run_bass_kernel_spmd

F32 = mybir.dt.float32
BF16 = mybir.dt.bfloat16
AF = mybir.ActivationFunctionType
ALU = mybir.AluOpType
AXX = mybir.AxisListType.X

P = 128
D = 2048
T = 2048
NK = D // P
NTILE = T // P
PROJ = 4640
CONV = 2560
NCCH = CONV // P
HEADS = 32
HD = 64
STATE = 256
CHUNK = 64
EPS = 1e-5
E_RT = 8
I_RT = 1408
I_SH = 2816
KSTACK = E_RT * I_RT + I_SH
NKST = KSTACK // P
INV_D = 1.0 / D
DBGPI = 1

ENGINES = ("tensor", "vector", "scalar", "gpsimd", "sync")
SEMKEYS = ("pe", "dve", "act", "pool", "qs", "qa", "qg")
DMA_SEM = {"sync": "qs", "scalar": "qa", "gpsimd": "qg"}


class Prog:

    def __init__(self):
        self.ops = {e: [] for e in ENGINES}
        self.nid = 0
        self.used = set()
        self.last = {}

    def _new_id(self):
        self.nid += 1
        return self.nid

    def _op(self, eng, build, waits, semkey, is_dma):
        opid = self._new_id()
        wl = []
        for t in waits:
            if t is None:
                continue
            sk, oid = t
            self.used.add((sk, oid))
            wl.append((sk, oid))
        if is_dma:
            self.used.add((semkey, opid))
        self.ops[eng].append((wl, build, semkey, opid, is_dma))
        self.last[semkey] = (semkey, opid)
        return (semkey, opid)

    def fence(self, keys=("pe", "dve", "act", "pool")):
        return [self.last[k] for k in keys if k in self.last]

    def fence_all(self):
        return [self.last[k] for k in SEMKEYS if k in self.last]

    def t(self, build, waits=()):
        return self._op("tensor", build, waits, "pe", False)

    def v(self, build, waits=()):
        return self._op("vector", build, waits, "dve", False)

    def s(self, build, waits=()):
        return self._op("scalar", build, waits, "act", False)

    def g(self, build, waits=()):
        return self._op("gpsimd", build, waits, "pool", False)

    def dma(self, eng, build, waits=()):
        return self._op(eng, build, waits, DMA_SEM[eng], True)

    def finalize(self, nc, block, sems):
        marked = {k: [] for k in SEMKEYS}
        for eng in ENGINES:
            for (wl, build, sk, opid, is_dma) in self.ops[eng]:
                if (sk, opid) in self.used:
                    marked[sk].append(opid)
        rank = {}
        for sk in SEMKEYS:
            for i, oid in enumerate(sorted(marked[sk])):
                rank[(sk, oid)] = i + 1

        def section(eng):
            lastwait = {}

            def body(h):
                for (wl, build, sk, opid, is_dma) in self.ops[eng]:
                    for (wsk, woid) in wl:
                        step = 16 if wsk in ("qs", "qa", "qg") else 1
                        val = rank[(wsk, woid)] * step
                        if lastwait.get(wsk, 0) < val:
                            h.wait_ge(sems[wsk], val)
                            lastwait[wsk] = val
                    ins = build(h)
                    if (sk, opid) in self.used:
                        step = 16 if sk in ("qs", "qa", "qg") else 1
                        ins.then_inc(sems[sk], step)
            return body

        for eng in ENGINES:
            if self.ops[eng]:
                getattr(block, eng)(section(eng))


def apx(base_ap, free_dims, extra_off=0):
    return bass.AP(base_ap.tensor, base_ap.offset + extra_off,
                   [list(base_ap.ap[0])] + [list(d) for d in free_dims])


def build_program(debug=False):
    nc = bass.Bass("TRN2", target_bir_lowering=False)
    pr = Prog()

    def din(name, shape, dt=F32):
        return nc.dram_tensor(name, shape, dt, kind="ExternalInput").ap()

    x = din("x", [T, D])
    w_in = din("w_in", [D, PROJ])
    conv_w = din("conv_w", [CONV, 4])
    conv_b2 = din("conv_b2", [NCCH, P])
    dt_bias = din("dt_bias", [HEADS, 1])
    a_log = din("a_log", [HEADS, 1])
    d_skip = din("d_skip", [1, HEADS])
    normw2 = din("normw2", [P, NK])
    gnormw2 = din("gnormw2", [P, NK])
    w_out = din("w_out", [D, D])
    wo_w = din("wo_w", [D, D])
    gate_w = din("gate_w", [E_RT, D])
    w1 = din("w1", [E_RT, D, I_RT])
    w3 = din("w3", [E_RT, D, I_RT])
    w2 = din("w2", [E_RT, I_RT, D])
    sw1 = din("sw1", [D, I_SH])
    sw3 = din("sw3", [D, I_SH])
    sw2 = din("sw2", [I_SH, D])
    iden_f = din("iden_f", [P, P])
    tri_cum = din("tri_cum", [P, P])
    ut_strict = din("ut_strict", [P, P])
    e63 = din("e63", [P, P])
    e127 = din("e127", [P, P])
    ones1 = din("ones1", [1, P])
    esel = din("esel", [HEADS, HEADS * P])

    z_dram = nc.dram_tensor("z_dram", [T, D], F32, kind="Internal").ap()
    xbc_dram = nc.dram_tensor("xbc_dram", [CONV, T], F32, kind="Internal").ap()
    convq_dram = nc.dram_tensor("convq_dram", [4, CONV, 512], BF16, kind="Internal").ap()
    actT_dram = nc.dram_tensor("actT_dram", [KSTACK, T], BF16, kind="Internal").ap()
    ffn_dram = nc.dram_tensor("ffn_dram", [T, D], F32, kind="Internal").ap()
    srs_dram = nc.dram_tensor("srs_dram", [T], F32, kind="Internal").ap()
    m_dram = nc.dram_tensor("m_dram", [P, P], F32, kind="Internal").ap()
    u_dram = nc.dram_tensor("u_dram", [D, T], BF16, kind="Internal").ap()
    out = nc.dram_tensor("out", [T, D], F32, kind="ExternalOutput").ap()
    if debug:
        dbg_z = nc.dram_tensor("dbg_z", [T, D], F32, kind="ExternalOutput").ap()
        dbg_xbc = nc.dram_tensor("dbg_xbc", [CONV, T], F32, kind="ExternalOutput").ap()
        dbg_dt = nc.dram_tensor("dbg_dt", [HEADS, T], F32, kind="ExternalOutput").ap()
        dbg_xnf = nc.dram_tensor("dbg_xnf", [P, NK * T], F32, kind="ExternalOutput").ap()
        dbg_srs = nc.dram_tensor("dbg_srs", [P, NTILE], F32, kind="ExternalOutput").ap()
        dbg_stat = nc.dram_tensor("dbg_stat", [P, 16], F32, kind="ExternalOutput").ap()
        dbg_ffn = nc.dram_tensor("dbg_ffn", [T, D], F32, kind="ExternalOutput").ap()
        dbg_gfm = nc.dram_tensor("dbg_gfm", [P, NK * T], F32, kind="ExternalOutput").ap()
        dbg_m = nc.dram_tensor("dbg_m", [P, P], F32, kind="ExternalOutput").ap()
        dbg_cq = nc.dram_tensor("dbg_cq", [4 * CONV, 512], F32, kind="ExternalOutput").ap()
        dbg_dtv = nc.dram_tensor("dbg_dtv", [HEADS, 2 * T], F32, kind="ExternalOutput").ap()
        dbg_acr = nc.dram_tensor("dbg_acr", [P, P], F32, kind="ExternalOutput").ap()
        dbg_ssm = nc.dram_tensor("dbg_ssm", [P, 160], F32, kind="ExternalOutput").ap()
        dbg_yfin = nc.dram_tensor("dbg_yfin", [P, T], F32, kind="ExternalOutput").ap()
        dbg_xtw = nc.dram_tensor("dbg_xtw", [P, T], F32, kind="ExternalOutput").ap()
        dbg_S = nc.dram_tensor("dbg_S", [P, 2 * T], F32, kind="ExternalOutput").ap()
        dbg_act = nc.dram_tensor("dbg_act", [KSTACK, T], F32, kind="ExternalOutput").ap()
        dbg_zs = nc.dram_tensor("dbg_zs", [P, T], F32, kind="ExternalOutput").ap()
        dbg_gt = nc.dram_tensor("dbg_gt", [P, T], F32, kind="ExternalOutput").ap()
        dbg_gst = nc.dram_tensor("dbg_gst", [P, 4], F32, kind="ExternalOutput").ap()
        dbg_yg = nc.dram_tensor("dbg_yg", [P, T], F32, kind="ExternalOutput").ap()

    es = ExitStack()

    def mkSB(stack):
        return lambda n, s, d=F32: stack.enter_context(nc.sbuf_tensor(n, s, d))

    SB = mkSB(es)
    PSUM = lambda n, s, d=F32: es.enter_context(nc.psum_tensor(n, s, d))

    pbig = PSUM("pbig", [P, 2048])
    pdub = PSUM("pdub", [P, 1024])
    pbf = PSUM("pbf", [P, 1024], BF16)
    pmix = PSUM("pmix", [P, 512])

    sb_iden_f = SB("c_idenf", [P, P])
    sb_iden_b = SB("c_idenb", [P, P], BF16)
    sb_tri = SB("c_tri", [P, P])
    sb_ut = SB("c_ut", [P, P])
    sb_e63 = SB("c_e63", [P, P])
    sb_e127 = SB("c_e127", [P, P])
    sb_ones = SB("c_ones", [1, P])
    sb_normw = SB("c_normw", [P, NK])
    sb_gnormw = SB("c_gnormw", [P, NK])
    srs = SB("srs", [P, NTILE])
    stat = SB("stat", [P, 16])
    cpool = SB("cpool", [P, 4])
    dt_fm = SB("dt_fm", [HEADS, T])
    dt3 = SB("dt3", [HEADS, 2 * T])
    dskb = SB("dskb", [P, HEADS])
    dtb_sb = SB("dtb_sb", [HEADS, 1])
    alog_sb = SB("alog_sb", [HEADS, 2])
    dskrow = SB("dskrow", [1, HEADS])

    sems = {}
    for sk in SEMKEYS:
        sems[sk] = es.enter_context(nc.semaphore("sem_" + sk))

    c_tok = []
    for dst, src in [(sb_iden_f, iden_f), (sb_tri, tri_cum),
                     (sb_ut, ut_strict), (sb_e63, e63), (sb_e127, e127),
                     (sb_ones, ones1), (sb_normw, normw2), (sb_gnormw, gnormw2)]:
        c_tok.append(pr.dma("sync", lambda h, d=dst, s=src: h.dma_start(out=d[:], in_=s)))
    c_tok.append(pr.dma("gpsimd", lambda h: h.dma_start(out=sb_iden_b[:], in_=iden_f)))
    tok_consts = c_tok[-2]
    tok_consts_b = c_tok[-1]
    pr.v(lambda h: h.memset(cpool[:, 0:1], EPS))
    pr.v(lambda h: h.memset(cpool[:, 2:3], -1e30))
    tok_cpool = pr.v(lambda h: h.memset(cpool[:, 1:2], INV_D))
    c_eps = cpool[:, 0:1]
    c_invd = cpool[:, 1:2]
    c_n30 = cpool[:, 2:3]

    s_xnf = ExitStack()
    xnf = mkSB(s_xnf)("xnf", [P, NK * T], BF16)

    s12 = ExitStack()
    SB12 = mkSB(s12)
    junk2k = SB12("junk2k", [P, 2048])
    xt = [SB12("xt0", [P, D]), SB12("xt1", [P, D])]
    xnt = [SB12("xnt0", [P, D], BF16), SB12("xnt1", [P, D], BF16)]

    tok_xt_free = [None, None]
    tok_xnt_free = [None, None]
    tok_pbf_free = None
    tok_srs_last = None

    for ti in range(NTILE):
        pa = ti % 2
        t_dmax = pr.dma(
            "sync", lambda h, pa=pa, ti=ti: h.dma_start(
                out=xt[pa][:], in_=x[ti * P:(ti + 1) * P, :]),
            waits=[tok_xt_free[pa]])
        w0 = [t_dmax]
        t_sum = pr.v(lambda h, pa=pa: h.reduce_sum(stat[:, 0:1], xt[pa][:], axis=AXX),
                     waits=w0)
        t_sq = pr.s(lambda h, pa=pa: h.activation(
            junk2k[:], xt[pa][:], AF.Square, accum_out=stat[:, 1:2]), waits=w0)
        t_mean = pr.v(lambda h: h.tensor_scalar_mul(stat[:, 2:3], stat[:, 0:1], INV_D),
                      waits=[t_sum])
        t_m2 = pr.v(lambda h: h.tensor_mul(stat[:, 3:4], stat[:, 2:3], stat[:, 2:3]),
                    waits=[t_mean])
        t_v0 = pr.v(lambda h: h.tensor_scalar_mul(stat[:, 4:5], stat[:, 1:2], INV_D),
                    waits=[t_sq])
        t_var = pr.v(lambda h: h.tensor_sub(stat[:, 4:5], stat[:, 4:5], stat[:, 3:4]),
                     waits=[t_v0, t_m2])
        t_lnv = pr.s(lambda h: h.activation(stat[:, 5:6], stat[:, 4:5], AF.Ln,
                                            bias=c_eps),
                     waits=[t_var, tok_cpool])
        t_rstd = pr.s(lambda h: h.activation(stat[:, 6:7], stat[:, 5:6], AF.Exp,
                                             scale=-0.5), waits=[t_lnv])
        t_negmb = pr.v(lambda h: h.scalar_tensor_tensor(
            stat[:, 7:8], stat[:, 2:3], -1.0, stat[:, 6:7], ALU.mult, ALU.mult),
            waits=[t_rstd, t_mean])
        t_msq = pr.s(lambda h, pa=pa: h.activation(
            junk2k[:], xt[pa][:], AF.Square, bias=stat[:, 7:8], scale=stat[:, 6:7],
            accum_out=stat[:, 8:9]), waits=[t_negmb])
        t_msqm = pr.v(lambda h: h.tensor_scalar_mul(stat[:, 9:10], stat[:, 8:9], INV_D),
                      waits=[t_msq])
        t_l2 = pr.s(lambda h: h.activation(stat[:, 10:11], stat[:, 9:10], AF.Ln,
                                           bias=c_eps), waits=[t_msqm])
        t_rs2 = pr.s(lambda h, ti=ti: h.activation(
            srs[:, ti:ti + 1], stat[:, 10:11], AF.Exp, scale=-0.5), waits=[t_l2])
        tok_srs_last = t_rs2
        t_comb = pr.v(lambda h: h.tensor_copy(stat[:, 12:13], stat[:, 6:7]),
                      waits=[t_rstd])
        t_negc = pr.v(lambda h: h.scalar_tensor_tensor(
            stat[:, 13:14], stat[:, 2:3], -1.0, stat[:, 12:13], ALU.mult, ALU.mult),
            waits=[t_comb])
        tok_xt_free[pa] = pr.v(lambda h, pa=pa: h.tensor_scalar(
            xnt[pa][:], xt[pa][:], stat[:, 12:13], stat[:, 13:14],
            ALU.mult, ALU.add),
            waits=[t_negc, tok_xnt_free[pa]])
        for grp in range(2):
            tts = []
            for j in range(8):
                k = grp * 8 + j
                tts.append(pr.t(lambda h, pa=pa, k=k, j=j: h.transpose(
                    pbf[:, j * P:(j + 1) * P], xnt[pa][:, k * P:(k + 1) * P],
                    sb_iden_b[:]),
                    waits=[tok_xt_free[pa], tok_pbf_free, tok_consts,
                           tok_consts_b]))
            off = (grp * 8) * T + ti * P
            tok_pbf_free = pr.v(lambda h, off=off: h.tensor_copy(
                apx(xnf[:], [[T, 8], [1, P]], extra_off=off),
                apx(pbf[:], [[P, 8], [1, P]])), waits=[tts[-1]])
            if grp == 1:
                tok_xnt_free[pa] = tok_pbf_free

    wst32 = SB12("wst32", [P, NK * 256])
    wstbf = [SB12("wstbf0", [P, NK * 512], BF16), SB12("wstbf1", [P, NK * 512], BF16)]
    ostage = [SB12("ostage%d" % i, [P, 512]) for i in range(4)]
    srsT = SB12("srsT", [NTILE, P])
    rs2_b = SB12("rs2_b", [P, 2048])
    srs_row = SB12("srs_row", [1, 2048])

    w_in_r = w_in.rearrange("(k p) c -> p k c", p=P)

    t_tr = pr.t(lambda h: h.transpose(pmix[0:NTILE, 0:P], srs[:], sb_iden_f[:]),
                waits=[tok_srs_last, tok_consts])
    t_srs_ev = pr.v(lambda h: h.tensor_copy(srsT[:], pmix[0:NTILE, 0:P]), waits=[t_tr])
    t_srs_out = pr.dma("scalar", lambda h: h.dma_start(
        out=srs_dram.rearrange("(a b) -> a b", a=NTILE), in_=srsT[:]),
        waits=[t_srs_ev])
    t_srs_in = pr.dma("sync", lambda h: h.dma_start(
        out=srs_row[:], in_=srs_dram.rearrange("(a b) -> a b", a=1)),
        waits=[t_srs_out])
    k1s = []
    for j in range(4):
        k1s.append(pr.t(lambda h, j=j: h.matmul(
            pbig[:, j * 512:(j + 1) * 512], sb_ones[:],
            srs_row[:, j * 512:(j + 1) * 512], start=True, stop=True),
            waits=[t_srs_in]))
    t_rs2b = pr.v(lambda h: h.tensor_copy(rs2_b[:], pbig[:]), waits=[k1s[-1]])

    blocks = [(c0, 512, "z") for c0 in range(0, 2048, 512)] \
        + [(c0, 512, "xbc") for c0 in range(2048, 4608, 512)] \
        + [(4608, 32, "dt")]

    tok_cast = [None, None]
    tok_w32_free = None
    tok_ost_free = [None] * 4
    tok_psB = [None, None]
    ost_i = [0]

    def next_ost():
        i = ost_i[0] % 4
        ost_i[0] += 1
        return i

    for bi, (c0, C, kind) in enumerate(blocks):
        pa = bi % 2
        casts = []
        for hb in range(0, C, 256):
            hw = min(256, C - hb)
            t_w32 = pr.dma("sync", lambda h, c0=c0, hb=hb, hw=hw: h.dma_start(
                out=apx(wst32[:], [[256, NK], [1, hw]]),
                in_=w_in_r[:, :, c0 + hb:c0 + hb + hw]), waits=[tok_w32_free])
            for k in range(NK):
                casts.append(pr.v(
                    lambda h, k=k, C=C, pa=pa, hb=hb, hw=hw: h.tensor_scalar_mul(
                        wstbf[pa][:, k * C + hb:k * C + hb + hw],
                        wst32[:, k * 256:k * 256 + hw],
                        sb_normw[:, k:k + 1]),
                    waits=[t_w32, tok_cast[pa], tok_consts]))
            tok_w32_free = casts[-1]
        tok_cast[pa] = None
        last_use = None

        if kind == "z":
            for M in range(NTILE):
                pp = M % 2
                ps = pdub[:, pp * 512:(pp + 1) * 512]
                mms = []
                for k in range(NK):
                    mms.append(pr.t(lambda h, k=k, M=M, pa=pa, C=C, ps=ps:
                                    h.matmul(ps, xnf[:, k * T + M * P:k * T + (M + 1) * P],
                                             wstbf[pa][:, k * C:(k + 1) * C],
                                             start=(k == 0), stop=(k == NK - 1)),
                                    waits=[casts[-1], tok_psB[pp]]))
                last_use = mms[-1]
                oi = next_ost()
                t_ev = pr.v(lambda h, oi=oi, ps=ps, M=M: h.tensor_scalar_mul(
                    ostage[oi][:], ps, srs[:, M:M + 1]),
                    waits=[mms[-1], tok_ost_free[oi], t_rs2b])
                tok_psB[pp] = t_ev
                tok_ost_free[oi] = pr.dma(
                    "scalar", lambda h, oi=oi, M=M, c0=c0: h.dma_start(
                        out=z_dram[M * P:(M + 1) * P, c0:c0 + 512], in_=ostage[oi][:]),
                    waits=[t_ev])
        elif kind == "xbc":
            for PC in range(4):
                for TN in range(4):
                    pp = (PC * 4 + TN) % 2
                    ps = pdub[:, pp * 512:(pp + 1) * 512]
                    mms = []
                    for k in range(NK):
                        mms.append(pr.t(lambda h, k=k, PC=PC, TN=TN, pa=pa, C=C, ps=ps:
                                        h.matmul(ps,
                                                 wstbf[pa][:, k * C + PC * P:k * C + (PC + 1) * P],
                                                 xnf[:, k * T + TN * 512:k * T + (TN + 1) * 512],
                                                 start=(k == 0), stop=(k == NK - 1)),
                                        waits=[casts[-1], tok_psB[pp]]))
                    last_use = mms[-1]
                    oi = next_ost()
                    t_ev = pr.v(lambda h, oi=oi, ps=ps, TN=TN: h.tensor_tensor(
                        ostage[oi][:], ps, rs2_b[:, TN * 512:(TN + 1) * 512],
                        ALU.mult),
                        waits=[mms[-1], tok_ost_free[oi], t_rs2b])
                    tok_psB[pp] = t_ev
                    r0 = c0 - 2048 + PC * P
                    tok_ost_free[oi] = pr.dma(
                        "scalar", lambda h, oi=oi, r0=r0, TN=TN: h.dma_start(
                            out=xbc_dram[r0:r0 + P, TN * 512:(TN + 1) * 512],
                            in_=ostage[oi][:]),
                        waits=[t_ev])
        else:
            for TN in range(4):
                pp = TN % 2
                ps = pdub[0:HEADS, pp * 512:(pp + 1) * 512]
                mms = []
                for k in range(NK):
                    mms.append(pr.t(lambda h, k=k, TN=TN, pa=pa, C=C, ps=ps:
                                    h.matmul(ps,
                                             wstbf[pa][:, k * C:k * C + HEADS],
                                             xnf[:, k * T + TN * 512:k * T + (TN + 1) * 512],
                                             start=(k == 0), stop=(k == NK - 1)),
                                    waits=[casts[-1], tok_psB[pp]]))
                last_use = mms[-1]
                tok_psB[pp] = pr.v(lambda h, TN=TN, ps=ps: h.tensor_tensor(
                    dt_fm[:, TN * 512:(TN + 1) * 512], ps,
                    rs2_b[0:HEADS, TN * 512:(TN + 1) * 512], ALU.mult),
                    waits=[mms[-1], t_rs2b])
        tok_cast[pa] = last_use

    s12.close()
    barrier23 = [t for t in tok_ost_free if t is not None] + pr.fence_all()

    s3 = ExitStack()
    SB3 = mkSB(s3)
    gwf = SB3("gwf", [E_RT, D])
    gwT32 = SB3("gwT32", [P, NTILE * E_RT])
    gwhi = SB3("gwhi", [P, NTILE * E_RT], BF16)
    gwlo = SB3("gwlo", [P, NTILE * E_RT], BF16)
    glog = SB3("glog", [P, E_RT])
    ge1 = SB3("ge1", [P, E_RT])
    ge2 = SB3("ge2", [P, E_RT])
    glm = SB3("glm", [P, E_RT])
    gsc = SB3("gsc", [P, 8])
    m_tm = SB3("m_tm", [P, NTILE * E_RT])
    mT = SB3("mT", [P, P])
    me_b = SB3("me_b", [P, T])
    w1blk = [SB3("w1blk0", [P, NK * 512], BF16), SB3("w1blk1", [P, NK * 512], BF16)]
    w3blk = [SB3("w3blk0", [P, NK * 512], BF16), SB3("w3blk1", [P, NK * 512], BF16)]
    astage = [SB3("ast%d" % i, [P, 512], BF16) for i in range(4)]
    shtile = [SB3("sht%d" % i, [P, 512]) for i in range(2)]
    a1tile = [SB3("a1t%d" % i, [P, 512]) for i in range(2)]

    t_gw = pr.dma("sync", lambda h: h.dma_start(out=gwf[:], in_=gate_w),
                  waits=barrier23)
    gts = []
    for k in range(NK):
        gts.append(pr.t(lambda h, k=k: h.transpose(
            pmix[:, k * 8:(k + 1) * 8], gwf[0:E_RT, k * P:(k + 1) * P],
            sb_iden_f[0:E_RT, 0:E_RT]), waits=[t_gw, t_srs_ev]))
    t_gwT = pr.v(lambda h: h.tensor_copy(gwT32[:], pmix[:, 0:NTILE * E_RT]),
                 waits=[gts[-1]])
    t_ghi = pr.v(lambda h: h.tensor_copy(gwhi[:], gwT32[:]), waits=[t_gwT])
    t_glo = pr.v(lambda h: h.tensor_tensor(gwlo[:], gwT32[:], gwhi[:],
                                           ALU.subtract), waits=[t_ghi])

    t_pm_free = t_gwT
    tok_m_last = None
    for M in range(NTILE):
        mms = []
        for k in range(NK):
            mms.append(pr.t(lambda h, k=k, M=M: h.matmul(
                pmix[:, 0:E_RT], xnf[:, k * T + M * P:k * T + (M + 1) * P],
                gwhi[:, k * 8:(k + 1) * 8], start=(k == 0), stop=False),
                waits=[t_glo, t_pm_free]))
        for k in range(NK):
            mms.append(pr.t(lambda h, k=k, M=M: h.matmul(
                pmix[:, 0:E_RT], xnf[:, k * T + M * P:k * T + (M + 1) * P],
                gwlo[:, k * 8:(k + 1) * 8], start=False, stop=(k == NK - 1)),
                waits=[t_glo]))
        t_lg = pr.v(lambda h: h.tensor_copy(glog[:], pmix[:, 0:E_RT]),
                    waits=[mms[-1], tok_m_last])
        t_pm_free = t_lg
        t_mx1 = pr.v(lambda h: h.reduce_max(gsc[:, 0:1], glog[:], axis=AXX),
                     waits=[t_lg])
        t_e1 = pr.v(lambda h: h.tensor_tensor(
            ge1[:], glog[:], apx(gsc[:, 0:1], [[0, E_RT]]), ALU.is_equal),
            waits=[t_mx1])
        t_l30 = pr.v(lambda h: h.tensor_scalar_mul(glm[:], ge1[:], c_n30),
                     waits=[t_e1, tok_cpool])
        t_lm = pr.v(lambda h: h.tensor_add(glm[:], glm[:], glog[:]), waits=[t_l30])
        t_mx2 = pr.v(lambda h: h.reduce_max(gsc[:, 1:2], glm[:], axis=AXX),
                     waits=[t_lm])
        t_e2 = pr.v(lambda h: h.tensor_tensor(
            ge2[:], glm[:], apx(gsc[:, 1:2], [[0, E_RT]]), ALU.is_equal),
            waits=[t_mx2])
        t_d = pr.v(lambda h: h.tensor_sub(gsc[:, 2:3], gsc[:, 1:2], gsc[:, 0:1]),
                   waits=[t_mx2])
        t_ed = pr.s(lambda h: h.activation(gsc[:, 3:4], gsc[:, 2:3], AF.Exp),
                    waits=[t_d])
        t_den = pr.v(lambda h: h.tensor_scalar_add(gsc[:, 4:5], gsc[:, 3:4], 1.0),
                     waits=[t_ed])
        t_r = pr.v(lambda h: h.reciprocal(gsc[:, 5:6], gsc[:, 4:5]), waits=[t_den])
        t_w2v = pr.v(lambda h: h.tensor_mul(gsc[:, 6:7], gsc[:, 3:4], gsc[:, 5:6]),
                     waits=[t_r])
        t_m1 = pr.v(lambda h: h.tensor_scalar_mul(ge1[:], ge1[:], gsc[:, 5:6]),
                    waits=[t_r, t_e1])
        tok_m_last = pr.v(lambda h, M=M: h.scalar_tensor_tensor(
            m_tm[:, M * 8:(M + 1) * 8], ge2[:], gsc[:, 6:7], ge1[:],
            ALU.mult, ALU.add), waits=[t_w2v, t_m1])

    t_mtr = pr.t(lambda h: h.transpose(pmix[:, 0:P], m_tm[:], sb_iden_f[:]),
                 waits=[tok_m_last, t_pm_free])
    t_mT = pr.v(lambda h: h.tensor_copy(mT[:], pmix[:, 0:P]), waits=[t_mtr])
    t_mout = pr.dma("scalar", lambda h: h.dma_start(out=m_dram, in_=mT[:]),
                    waits=[t_mT])

    cxin = SB3("cxin", [P, 2051])
    ubuf = SB3("ubuf", [P, T])
    cxo = SB3("cxo", [P, T], BF16)
    convw_sb = SB3("convw_sb", [P, NCCH * 4])
    convb_t = SB3("convb_t", [NCCH, P])
    convb_sb = SB3("convb_sb", [P, NCCH])

    t_dtb = pr.dma("sync", lambda h: h.dma_start(out=dtb_sb[:], in_=dt_bias),
                   waits=barrier23)
    t_alog = pr.dma("sync", lambda h: h.dma_start(out=alog_sb[:, 0:1], in_=a_log))
    t_dskr = pr.dma("sync", lambda h: h.dma_start(out=dskrow[:], in_=d_skip))
    t_dskm = pr.t(lambda h: h.matmul(pmix[:, 0:HEADS], sb_ones[:], dskrow[:],
                                     start=True, stop=True),
                  waits=[t_dskr, t_mT])
    t_dskb = pr.v(lambda h: h.tensor_copy(dskb[:], pmix[:, 0:HEADS]), waits=[t_dskm])
    t_na1 = pr.s(lambda h: h.activation(alog_sb[:, 1:2], alog_sb[:, 0:1], AF.Exp),
                 waits=[t_alog])
    t_na = pr.v(lambda h: h.tensor_scalar_mul(alog_sb[:, 1:2], alog_sb[:, 1:2], -1.0),
                waits=[t_na1])
    t_p1 = pr.v(lambda h: h.tensor_scalar_add(dt3[:, T:2 * T], dt_fm[:], dtb_sb[:]),
                waits=[t_dtb] + barrier23)
    t_p2 = pr.s(lambda h: h.activation(dt3[:, 0:T], dt3[:, T:2 * T], AF.Abs),
                waits=[t_p1])
    t_p3 = pr.s(lambda h: h.activation(dt3[:, 0:T], dt3[:, 0:T],
                                       AF.Exp, scale=-1.0), waits=[t_p2])
    t_p4 = pr.s(lambda h: h.activation(dt3[:, 0:T], dt3[:, 0:T],
                                       AF.Ln, bias=1.0), waits=[t_p3])
    t_p5 = pr.v(lambda h: h.tensor_scalar_max(dt3[:, T:2 * T], dt3[:, T:2 * T], 0.0),
                waits=[t_p2])
    t_dtv = pr.v(lambda h: h.tensor_add(dt3[:, T:2 * T], dt3[:, T:2 * T],
                                        dt3[:, 0:T]), waits=[t_p4, t_p5])
    t_adt = pr.v(lambda h: h.tensor_scalar_mul(dt3[:, 0:T], dt3[:, T:2 * T],
                                               alog_sb[:, 1:2]),
                 waits=[t_dtv, t_na])
    t_lndt = t_adt

    t_cw = pr.dma("sync", lambda h: h.dma_start(
        out=convw_sb.ap().rearrange("p (c k) -> p c k", k=4),
        in_=conv_w.rearrange("(c p) k -> p c k", p=P)), waits=barrier23)
    t_cb1 = pr.dma("sync", lambda h: h.dma_start(out=convb_t[:], in_=conv_b2),
                   waits=barrier23)
    t_cb2 = pr.t(lambda h: h.transpose(pmix[:, 32:32 + NCCH], convb_t[:],
                                       sb_iden_f[0:NCCH, 0:NCCH]),
                 waits=[t_cb1, t_dskb])
    t_cb = pr.v(lambda h: h.tensor_copy(convb_sb[:], pmix[:, 32:32 + NCCH]),
                waits=[t_cb2])
    tz0 = pr.v(lambda h: h.memset(cxin[:, 0:3], 0.0), waits=barrier23)

    conv_state = {"cxin": tz0, "cxo": None, "u": None, "qd_last": None}

    def emit_conv_chunk(cc):
        t_ld = pr.dma("sync", lambda h, cc=cc: h.dma_start(
            out=cxin[:, 3:2051], in_=xbc_dram[cc * P:(cc + 1) * P, :]),
            waits=[conv_state["cxin"]])
        tu = pr.v(lambda h, cc=cc: h.tensor_scalar_mul(
            ubuf[:], cxin[:, 0:2048], convw_sb[:, cc * 4:cc * 4 + 1]),
            waits=[t_ld, t_cw, conv_state["u"]])
        for kk2 in range(1, 4):
            tu = pr.v(lambda h, cc=cc, kk2=kk2: h.scalar_tensor_tensor(
                ubuf[:], cxin[:, kk2:2048 + kk2],
                convw_sb[:, cc * 4 + kk2:cc * 4 + kk2 + 1], ubuf[:],
                ALU.mult, ALU.add), waits=[tu])
        conv_state["cxin"] = tu
        t_si = pr.s(lambda h, cc=cc: h.activation(
            cxo[:], ubuf[:], AF.Silu, bias=convb_sb[:, cc:cc + 1]),
            waits=[tu, t_cb, conv_state["cxo"]])
        conv_state["u"] = t_si
        last = None
        for q in range(4):
            last = pr.dma("scalar", lambda h, cc=cc, q=q: h.dma_start(
                out=convq_dram[q, cc * P:(cc + 1) * P, :],
                in_=cxo[:, q * 512:(q + 1) * 512]), waits=[t_si])
        conv_state["cxo"] = last
        conv_state["qd_last"] = last

    w1r = w1.rearrange("e (k p) i -> e p k i", p=P)
    w3r = w3.rearrange("e (k p) i -> e p k i", p=P)
    sw1r = sw1.rearrange("(k p) i -> p k i", p=P)
    sw3r = sw3.rearrange("(k p) i -> p k i", p=P)

    tok_wblk = [None, None]
    tok_hps = [None, None]
    tok_gps = [None, None]
    tok_sh = [None, None]
    tok_a1 = [None, None]
    tok_ast = [None] * 4
    tok_meb = None
    t_meb_frees = [t_rs2b]
    it_i = [0]
    ast_i = [0]
    blk_i = [0]

    ph_slot = [(pdub[:], 0), (pbig[:], 0)]
    pg_slot = [(pdub[:], 512), (pbig[:], 512)]

    for ei in range(9):
        routed = ei < E_RT
        row0 = ei * I_RT if routed else E_RT * I_RT
        if routed:
            t_mer = pr.dma("sync", lambda h, ei=ei: h.dma_start(
                out=apx(gwf[0:1, :], [[P, NTILE], [1, P]]),
                in_=bass.AP(m_dram.tensor, ei * P,
                            [[1, 1], [E_RT * P, NTILE], [1, P]])),
                waits=[t_mout, tok_meb])
            kk = []
            for j in range(4):
                kk.append(pr.t(lambda h, j=j: h.matmul(
                    pbig[:, j * 512:(j + 1) * 512], sb_ones[:],
                    gwf[0:1, j * 512:(j + 1) * 512], start=True, stop=True),
                    waits=[t_mer, tok_hps[1], tok_gps[1]] + t_meb_frees))
            tok_meb = pr.v(lambda h: h.tensor_copy(me_b[:], pbig[:]), waits=[kk[-1]])
            t_meb_frees = [tok_meb]
        if routed:
            nclist = [(0, 512), (512, 512), (1024, 384)]
        else:
            nclist = [(c, min(512, I_SH - c)) for c in range(0, I_SH, 512)]
        for (c0, C) in nclist:
            pa = blk_i[0] % 2
            blk_i[0] += 1
            if blk_i[0] - 1 < NCCH:
                emit_conv_chunk(blk_i[0] - 1)
            if routed:
                src1 = w1r[ei, :, :, c0:c0 + C]
                src3 = w3r[ei, :, :, c0:c0 + C]
            else:
                src1 = sw1r[:, :, c0:c0 + C]
                src3 = sw3r[:, :, c0:c0 + C]
            first2 = blk_i[0] <= 2
            t_s1 = pr.dma("gpsimd", lambda h, pa=pa, src1=src1, C=C: h.dma_start(
                out=apx(w1blk[pa][:], [[512, NK], [1, C]]), in_=src1),
                waits=[tok_wblk[pa]] + (barrier23 if first2 else []))
            t_s3 = pr.dma("gpsimd", lambda h, pa=pa, src3=src3, C=C: h.dma_start(
                out=apx(w3blk[pa][:], [[512, NK], [1, C]]), in_=src3),
                waits=[tok_wblk[pa]] + (barrier23 if first2 else []))
            last_mm = None
            for ic in range(C // P):
                for tn in range(4):
                    sp = it_i[0] % 2
                    it_i[0] += 1
                    pht, phc = ph_slot[sp]
                    pgt, pgc = pg_slot[sp]
                    hmms = []
                    gmms = []
                    for k in range(NK):
                        hmms.append(pr.t(
                            lambda h, k=k, ic=ic, tn=tn, pa=pa, pht=pht, phc=phc:
                            h.matmul(pht[:, phc:phc + 512],
                                     w1blk[pa][:, k * 512 + ic * P:k * 512 + (ic + 1) * P],
                                     xnf[:, k * T + tn * 512:k * T + (tn + 1) * 512],
                                     start=(k == 0), stop=(k == NK - 1)),
                            waits=[t_s1, tok_hps[sp]]
                            + ([t_meb_frees[0]] if routed else [])))
                    for k in range(NK):
                        gmms.append(pr.t(
                            lambda h, k=k, ic=ic, tn=tn, pa=pa, pgt=pgt, pgc=pgc:
                            h.matmul(pgt[:, pgc:pgc + 512],
                                     w3blk[pa][:, k * 512 + ic * P:k * 512 + (ic + 1) * P],
                                     xnf[:, k * T + tn * 512:k * T + (tn + 1) * 512],
                                     start=(k == 0), stop=(k == NK - 1)),
                            waits=[t_s3, tok_gps[sp]]))
                    last_mm = gmms[-1]
                    t_sh = pr.s(lambda h, sp=sp, pht=pht, phc=phc: h.activation(
                        shtile[sp][:], pht[:, phc:phc + 512], AF.Silu),
                        waits=[hmms[-1], tok_sh[sp]])
                    tok_hps[sp] = t_sh
                    ai = ast_i[0] % 4
                    ast_i[0] += 1
                    if routed:
                        t_a1 = pr.v(lambda h, sp=sp, pgt=pgt, pgc=pgc: h.tensor_mul(
                            a1tile[sp][:], shtile[sp][:], pgt[:, pgc:pgc + 512]),
                            waits=[t_sh, gmms[-1], tok_a1[sp]])
                        tok_gps[sp] = t_a1
                        tok_sh[sp] = t_a1
                        t_a2 = pr.v(lambda h, sp=sp, ai=ai, tn=tn: h.tensor_mul(
                            astage[ai][:], a1tile[sp][:],
                            me_b[:, tn * 512:(tn + 1) * 512]),
                            waits=[t_a1, tok_meb, tok_ast[ai]])
                        tok_a1[sp] = t_a2
                    else:
                        t_a2 = pr.v(lambda h, sp=sp, ai=ai, pgt=pgt, pgc=pgc:
                                    h.tensor_mul(astage[ai][:], shtile[sp][:],
                                                 pgt[:, pgc:pgc + 512]),
                                    waits=[t_sh, gmms[-1], tok_ast[ai]])
                        tok_gps[sp] = t_a2
                        tok_sh[sp] = t_a2
                    r0 = row0 + c0 + ic * P
                    tok_ast[ai] = pr.dma(
                        "scalar", lambda h, ai=ai, r0=r0, tn=tn: h.dma_start(
                            out=actT_dram[r0:r0 + P, tn * 512:(tn + 1) * 512],
                            in_=astage[ai][:]), waits=[t_a2])
            tok_wblk[pa] = last_mm

    barrier3b = [tok_ast[i] for i in range(4)] + [tok_meb] + pr.fence_all()
    if debug:
        t_dbg_xnf = pr.dma("gpsimd", lambda h: h.dma_start(out=dbg_xnf, in_=xnf[:]),
                           waits=pr.fence())
        barrier3b.append(t_dbg_xnf)
    s3.close()
    s_xnf.close()

    s4 = ExitStack()
    SB4 = mkSB(s4)
    y_t = SB4("y_t", [P, NTILE * 1024])
    atst = SB4("atst", [P, 16 * 1024], BF16)
    w2st = [SB4("w2st0", [P, 16 * 1024], BF16), SB4("w2st1", [P, 16 * 1024], BF16)]

    w2r = w2.rearrange("e (c p) n -> e p c n", p=P)
    sw2r = sw2.rearrange("(c p) n -> p c n", p=P)

    def w2_runs(cc0, ncc):
        runs = []
        cc = cc0
        while cc < cc0 + ncc:
            if cc < 88:
                e = cc // 11
                cstart = cc % 11
                ln = min(11 - cstart, cc0 + ncc - cc)
                runs.append(("w2", e, cstart, ln, cc - cc0))
            else:
                cstart = cc - 88
                ln = cc0 + ncc - cc
                runs.append(("sw2", 0, cstart, ln, cc - cc0))
            cc += ln
        return runs

    tok_w2st = [barrier3b, barrier3b]
    tok_at = [None, None]
    tok_yps = [None, None, None]
    tok_yrow = {}
    kgs = [(kg * 16, min(16, NKST - kg * 16)) for kg in range(7)]
    actT_r = actT_dram.rearrange("(c p) t -> p c t", p=P)

    for dh in range(2):
        tok_ylast = {}
        for gi, (cc0, ncc) in enumerate(kgs):
            wpa = gi % 2
            stoks = []
            ci2tok = [None] * ncc
            for (kind, e, cstart, ln, boff) in w2_runs(cc0, ncc):
                if kind == "w2":
                    src = w2r[e, :, cstart:cstart + ln, dh * 1024:(dh + 1) * 1024]
                else:
                    src = sw2r[:, cstart:cstart + ln, dh * 1024:(dh + 1) * 1024]
                tk_run = pr.dma(
                    "gpsimd", lambda h, wpa=wpa, src=src, ln=ln, boff=boff:
                    h.dma_start(out=apx(w2st[wpa][:], [[1024, ln], [1, 1024]],
                                        extra_off=boff * 1024), in_=src),
                    waits=tok_w2st[wpa])
                stoks.append(tk_run)
                for cci in range(boff, boff + ln):
                    ci2tok[cci] = tk_run
            for thalf in range(2):
                athalf_toks = []
                for ah in range(2):
                    nc_h = min(8, ncc - ah * 8)
                    if nc_h <= 0:
                        athalf_toks.append(athalf_toks[0])
                        continue
                    athalf_toks.append(pr.dma(
                        "sync", lambda h, cc0=cc0, ah=ah, nc_h=nc_h, thalf=thalf:
                        h.dma_start(
                            out=apx(atst[:], [[1024, nc_h], [1, 1024]],
                                    extra_off=ah * 8 * 1024),
                            in_=actT_r[:, cc0 + ah * 8:cc0 + ah * 8 + nc_h,
                                       thalf * 1024:(thalf + 1) * 1024]),
                        waits=[tok_at[ah]]
                        + (barrier3b if (dh == 0 and gi == 0) else [])))
                for M in range(8):
                    Mg = thalf * 8 + M
                    psl = (gi * 16 + thalf * 8 + M) % 3
                    pps = [pdub[:, 0:1024], pbig[:, 0:1024], pbig[:, 1024:2048]][psl]
                    mms = []
                    for ci in range(ncc):
                        lhs_w = [athalf_toks[ci // 8]]
                        for hh2 in range(2):
                            mms.append(pr.t(
                                lambda h, ci=ci, M=M, wpa=wpa, pps=pps, hh2=hh2:
                                h.matmul(pps[:, hh2 * 512:(hh2 + 1) * 512],
                                         atst[:, ci * 1024 + M * P:ci * 1024 + (M + 1) * P],
                                         w2st[wpa][:, ci * 1024 + hh2 * 512:ci * 1024 + (hh2 + 1) * 512],
                                         start=(ci == 0), stop=(ci == ncc - 1)),
                                waits=lhs_w + [ci2tok[ci], tok_yps[psl]]))
                    if gi == 0:
                        t_y = pr.v(lambda h, Mg=Mg, pps=pps: h.tensor_copy(
                            y_t[:, Mg * 1024:(Mg + 1) * 1024], pps),
                            waits=[mms[-1], tok_yrow.get(Mg)])
                    else:
                        t_y = pr.v(lambda h, Mg=Mg, pps=pps: h.tensor_add(
                            y_t[:, Mg * 1024:(Mg + 1) * 1024],
                            y_t[:, Mg * 1024:(Mg + 1) * 1024], pps),
                            waits=[mms[-1], tok_ylast[Mg]])
                    tok_ylast[Mg] = t_y
                    tok_yps[psl] = t_y
                    if ncc > 8:
                        tok_at[0] = mms[2 * 8 - 1]
                        tok_at[1] = mms[-1]
                    else:
                        tok_at[0] = mms[-1]
            tok_w2st[wpa] = [mms[-1]]
        for Mg in range(NTILE):
            tok_yrow[Mg] = pr.dma(
                "scalar", lambda h, Mg=Mg, dh=dh: h.dma_start(
                    out=ffn_dram[Mg * P:(Mg + 1) * P, dh * 1024:(dh + 1) * 1024],
                    in_=y_t[:, Mg * 1024:(Mg + 1) * 1024]),
                waits=[tok_ylast[Mg]])

    barrier4 = [tok_yrow[Mg] for Mg in range(NTILE)] + pr.fence_all()
    s4.close()

    s5 = ExitStack()
    SB5 = mkSB(s5)
    g_fm = SB5("g_fm", [P, NK * T], BF16)
    esel_b = SB5("esel_b", [HEADS, HEADS * P], BF16)
    t_esel = pr.dma("gpsimd", lambda h: h.dma_start(out=esel_b[:], in_=esel),
                    waits=barrier4)

    barrier4b = barrier4 + [conv_state["qd_last"], t_adt, t_dskb, t_esel]

    s5b = ExitStack()
    SB5B = mkSB(s5b)
    cq = SB5B("cq", [P, NCCH * 512], BF16)
    S_buf = [SB5B("S0", [P, 2 * T]), SB5B("S1", [P, 2 * T])]
    prev_bf = SB5B("prev_bf", [P, 2 * T], BF16)
    zt = SB5B("zt", [P, T], BF16)
    dtt = [SB5B("dtt0", [P, 64]), SB5B("dtt1", [P, 64])]
    acr = [SB5B("acr0", [P, P]), SB5B("acr1", [P, P])]
    ssm = [SB5B("ssm0", [P, P]), SB5B("ssm1", [P, P])]
    ahi = [SB5B("ahi0", [HEADS, P], BF16), SB5B("ahi1", [HEADS, P], BF16)]
    alo = [SB5B("alo0", [HEADS, P], BF16), SB5B("alo1", [HEADS, P], BF16)]
    cbm = [SB5B("cbm0", [P, P], BF16), SB5B("cbm1", [P, P], BF16)]
    xs_tm = SB5B("xs_tm", [P, T], BF16)
    btm = SB5B("btm", [P, 2 * P], BF16)
    xtb = SB5B("xtb", [P, T], BF16)
    relb = [SB5B("relb0", [P, 512]), SB5B("relb1", [P, 512])]
    mpb = [SB5B("mpb0", [P, 512], BF16), SB5B("mpb1", [P, 512], BF16)]
    ybuf = SB5B("ybuf", [P, 1024])
    ybuf2 = SB5B("ybuf2", [P, 1024])
    yfin = SB5B("yfin", [P, T])
    tbuf = SB5B("tbuf", [P, 1024])
    gst = SB5B("gst", [P, 4])
    gtbf = SB5B("gtbf", [P, T], BF16)

    t_s0 = pr.v(lambda h: h.memset(S_buf[0][:], 0.0), waits=barrier4b)
    scur = 0
    tok_S = [t_s0, None]
    tok_Sread = [[], []]
    tok_tbuf = None
    tok_ybuf2 = None
    tok_zsb = None
    tok_cq = None
    tok_zt = t_s0
    tok_prev = None
    tok_pdub_free = t_s0
    tok_pbig_free = t_s0
    tok_pmix_free = t_s0
    tok_gtr_done = None
    tok_gq_prev = None
    tok_gbf_prev = None
    war_dtt = [None, None]
    war_acr = [[], []]
    war_ssm = [None, None]
    war_ahl = [None, None]
    war_xs = [None, None]
    war_xtb = None
    war_btm = None

    for q in range(4):
        t_cq = pr.dma("sync", lambda h, q=q: h.dma_start(
            out=cq.ap().rearrange("p (c t) -> p c t", t=512),
            in_=convq_dram[q].rearrange("(c p) t -> p c t", p=P)),
            waits=[tok_cq] + (barrier4b if q == 0 else []))
        for ppq in range(4):
            pi = q * 4 + ppq
            t0 = pi * P
            pa2 = pi % 2
            tok_zt = pr.dma("gpsimd", lambda h, t0=t0: h.dma_start(
                out=zt[:], in_=z_dram[t0:t0 + P, :]), waits=[tok_zt])

            ta = []
            for j in range(2):
                ta.append(pr.t(lambda h, j=j, t0=t0: h.transpose(
                    pmix[:, j * 32:(j + 1) * 32], dt3[:, j * T + t0:j * T + t0 + P],
                    sb_iden_f[0:32, 0:32]),
                    waits=[t_adt, tok_pmix_free]))
            t_dtt0 = pr.v(lambda h, pa2=pa2: h.tensor_copy(
                dtt[pa2][:], pmix[:, 0:64]), waits=[ta[-1], war_dtt[pa2]])
            tb = []
            for j, cst in enumerate((sb_tri, sb_ut, sb_e63, sb_e127)):
                tb.append(pr.t(lambda h, j=j, cst=cst, pa2=pa2: h.matmul(
                    pmix[:, 128 + j * 32:128 + (j + 1) * 32], cst[:],
                    dtt[pa2][:, 0:32], start=True, stop=True),
                    waits=[t_dtt0]))
            war_dtt[pa2] = tb[-1]
            t_acr = pr.v(lambda h, pa2=pa2: h.tensor_copy(
                acr[pa2][:], pmix[:, 128:256]),
                waits=[tb[-1]] + war_acr[pa2])
            war_acr[pa2] = []
            t_eacs = pr.s(lambda h, pa2=pa2: h.activation(
                ssm[pa2][:, 0:32], acr[pa2][:, 0:32], AF.Exp),
                waits=[t_acr, war_ssm[pa2]])
            t_wdec = pr.s(lambda h, pa2=pa2: h.activation(
                ssm[pa2][:, 32:64], acr[pa2][:, 32:64], AF.Exp), waits=[t_acr])
            t_cdb1 = pr.s(lambda h, pa2=pa2: h.activation(
                ssm[pa2][:, 64:96], acr[pa2][:, 64:96], AF.Exp), waits=[t_acr])
            t_cdb2 = pr.s(lambda h, pa2=pa2: h.activation(
                ssm[pa2][:, 96:128], acr[pa2][:, 96:128], AF.Exp), waits=[t_acr])
            war_acr[pa2].append(t_cdb2)
            t_afm = pr.t(lambda h, pa2=pa2: h.transpose(
                pmix[0:32, 256:384], acr[pa2][:, 0:32], sb_iden_f[:]),
                waits=[t_acr, t_dtt0])
            war_acr[pa2].append(t_afm)
            t_ahi = pr.v(lambda h, pa2=pa2: h.tensor_copy(
                ahi[pa2][:], pmix[0:32, 256:384]),
                waits=[t_afm, war_ahl[pa2]])
            t_alo = pr.v(lambda h, pa2=pa2: h.tensor_tensor(
                alo[pa2][:], pmix[0:32, 256:384], ahi[pa2][:],
                ALU.subtract), waits=[t_ahi])
            te1 = pr.t(lambda h, ppq=ppq: h.matmul(
                pmix[:, 384:512], cq[:, 16 * 512 + ppq * P:16 * 512 + (ppq + 1) * P],
                cq[:, 18 * 512 + ppq * P:18 * 512 + (ppq + 1) * P],
                start=True, stop=False), waits=[t_cq, t_alo])
            te2 = pr.t(lambda h, ppq=ppq: h.matmul(
                pmix[:, 384:512], cq[:, 17 * 512 + ppq * P:17 * 512 + (ppq + 1) * P],
                cq[:, 19 * 512 + ppq * P:19 * 512 + (ppq + 1) * P],
                start=False, stop=True), waits=[te1])
            t_cbm = pr.v(lambda h, pa2=pa2: h.tensor_mul(
                cbm[pa2][:], pmix[:, 384:512], sb_tri[:]), waits=[te2])
            tok_pmix_free = t_cbm
            ev_toks = []
            for grp in range(3):
                kl = [grp * 8 + j for j in range(8)] if grp < 2 else [16, 17]
                tts = []
                for j, k in enumerate(kl):
                    tts.append(pr.t(lambda h, j=j, k=k, ppq=ppq: h.transpose(
                        pbf[:, j * P:(j + 1) * P],
                        cq[:, k * 512 + ppq * P:k * 512 + (ppq + 1) * P],
                        sb_iden_b[:]),
                        waits=[t_cq, tok_pbf_free]))
                if grp < 2:
                    tok_pbf_free = pr.v(lambda h, grp=grp: h.tensor_copy(
                        xs_tm[:, grp * 1024:(grp + 1) * 1024], pbf[:, 0:1024]),
                        waits=[tts[-1], war_xs[grp]])
                    war_xs[grp] = None
                else:
                    tok_pbf_free = pr.v(lambda h: h.tensor_copy(
                        btm[:], pbf[:, 0:256]), waits=[tts[-1], war_btm])
                ev_toks.append(tok_pbf_free)
            t_xs, t_xs2, t_btm = ev_toks
            t_xt = pr.v(lambda h, pa2=pa2: h.tensor_tensor(
                apx(xtb[:], [[64, 32], [1, 64]]),
                apx(xs_tm[:], [[64, 32], [1, 64]]),
                apx(dtt[pa2][:, 32:64], [[1, 32], [0, 64]]), ALU.mult),
                waits=[t_xs, t_xs2, t_dtt0, war_xtb])
            tc0 = []
            for ncn in range(2):
                tc0.append(pr.s(lambda h, ncn=ncn, scur=scur: h.activation(
                    prev_bf[:, ncn * T:(ncn + 1) * T],
                    S_buf[scur][:, ncn * T:(ncn + 1) * T], AF.Copy),
                    waits=[tok_S[scur], tok_prev]))
            t_prev_w = tc0[-1]
            tok_Sread[scur].append(t_prev_w)
            t_ydiag = None
            tok_bcps = [None, None]
            tok_relb = [None, None]
            tok_mpb = [None, None]
            t_sub_last = None
            for hb in range(8):
                hp2 = hb % 2
                bcps = pdub[:, hp2 * 512:(hp2 + 1) * 512]
                bms = []
                for hh in range(4):
                    hg = hb * 4 + hh
                    bms.append(pr.t(lambda h, hg=hg, hh=hh, bcps=bcps, pa2=pa2:
                                    h.matmul(bcps[:, hh * P:(hh + 1) * P],
                                             esel_b[:, hg * P:(hg + 1) * P],
                                             ahi[pa2][:], start=True, stop=False),
                                    waits=[t_alo, tok_bcps[hp2], t_esel]))
                    bms.append(pr.t(lambda h, hg=hg, hh=hh, bcps=bcps, pa2=pa2:
                                    h.matmul(bcps[:, hh * P:(hh + 1) * P],
                                             esel_b[:, hg * P:(hg + 1) * P],
                                             alo[pa2][:], start=False, stop=True),
                                    waits=[]))
                war_ahl[pa2] = bms[-1]
                t_sub = pr.v(lambda h, hb=hb, hp2=hp2, bcps=bcps, pa2=pa2:
                             h.tensor_tensor(
                                 apx(relb[hp2][:], [[128, 4], [1, 128]]),
                                 apx(bcps, [[128, 4], [1, 128]]),
                                 apx(acr[pa2][:, hb * 4:hb * 4 + 4],
                                     [[1, 4], [0, 128]]),
                                 ALU.subtract),
                             waits=[bms[-1], tok_relb[hp2]])
                t_sub_last = t_sub
                tok_bcps[hp2] = t_sub
                t_rel = pr.s(lambda h, hp2=hp2: h.activation(
                    relb[hp2][:], relb[hp2][:], AF.Relu, scale=-1.0),
                    waits=[t_sub])
                t_exp = pr.s(lambda h, hp2=hp2: h.activation(
                    relb[hp2][:], relb[hp2][:], AF.Exp, scale=-1.0),
                    waits=[t_rel])
                t_mpx = pr.v(lambda h, hp2=hp2, pa2=pa2: h.tensor_tensor(
                    apx(mpb[hp2][:], [[128, 4], [1, 128]]),
                    apx(relb[hp2][:], [[128, 4], [1, 128]]),
                    apx(cbm[pa2][:], [[0, 4], [1, 128]]), ALU.mult),
                    waits=[t_exp, t_cbm, tok_mpb[hp2]])
                tok_relb[hp2] = t_mpx
                for hh in range(4):
                    hg = hb * 4 + hh
                    t_ydiag = pr.t(lambda h, hg=hg, hh=hh, hp2=hp2: h.matmul(
                        pbig[:, hg * HD:(hg + 1) * HD],
                        mpb[hp2][:, hh * P:(hh + 1) * P],
                        xtb[:, hg * HD:(hg + 1) * HD],
                        start=True, stop=True),
                        waits=[t_mpx, t_xt, tok_pbig_free])
                tok_mpb[hp2] = t_ydiag
            t_xtw = pr.v(lambda h, pa2=pa2: h.tensor_tensor(
                apx(xtb[:], [[64, 32], [1, 64]]),
                apx(xtb[:], [[64, 32], [1, 64]]),
                apx(ssm[pa2][:, 32:64], [[1, 32], [0, 64]]), ALU.mult),
                waits=[t_xt, t_ydiag, t_wdec])
            t_asm = []
            sA, sB = scur, 1 - scur
            for c in range(2):
                src_b = sA if c == 0 else sB
                dst_b = sB if c == 0 else sA
                Lc = slice(c * 64, (c + 1) * 64)
                cdb_off = 64 + c * 32
                if c == 1:
                    tc1 = []
                    for ncn in range(2):
                        tc1.append(pr.s(lambda h, ncn=ncn, src_b=src_b: h.activation(
                            prev_bf[:, ncn * T:(ncn + 1) * T],
                            S_buf[src_b][:, ncn * T:(ncn + 1) * T], AF.Copy),
                            waits=[tok_S[src_b], tok_prev]))
                    t_prev_w = tc1[-1]
                    tok_Sread[src_b].append(t_prev_w)
                for r in range(2):
                    yo = []
                    for jj in range(2):
                        for ncn in range(2):
                            yo.append(pr.t(
                                lambda h, c=c, r=r, jj=jj, ncn=ncn, ppq=ppq, Lc=Lc:
                                h.matmul(
                                    pdub[Lc, jj * 512:(jj + 1) * 512],
                                    cq[:, (18 + ncn) * 512 + ppq * P + c * 64:
                                        (18 + ncn) * 512 + ppq * P + c * 64 + 64],
                                    prev_bf[:, ncn * T + r * 1024 + jj * 512:
                                            ncn * T + r * 1024 + (jj + 1) * 512],
                                    start=(ncn == 0), stop=(ncn == 1)),
                                waits=[t_prev_w, t_cq, tok_pdub_free, t_sub_last]))
                    tok_prev = yo[-1]
                    a1 = pr.v(lambda h, Lc=Lc, r=r, pa2=pa2: h.tensor_tensor(
                        apx(ybuf[Lc, :], [[64, 16], [1, 64]]),
                        apx(pdub[Lc, 0:1024], [[64, 16], [1, 64]]),
                        apx(ssm[pa2][Lc, r * 16:r * 16 + 16], [[1, 16], [0, 64]]),
                        ALU.mult), waits=[yo[-1], t_eacs])
                    war_ssm[pa2] = a1
                    a2 = pr.g(lambda h, Lc=Lc, r=r: h.tensor_tensor(
                        apx(ybuf2[Lc, :], [[64, 16], [1, 64]]),
                        apx(xs_tm[Lc, r * 1024:(r + 1) * 1024], [[64, 16], [1, 64]]),
                        apx(dskb[Lc, r * 16:r * 16 + 16], [[1, 16], [0, 64]]),
                        ALU.mult), waits=[t_xs, t_xs2, t_dskb, tok_ybuf2])
                    war_xs[r] = a2
                    a3 = pr.v(lambda h, Lc=Lc: h.tensor_add(
                        ybuf[Lc, :], ybuf[Lc, :], ybuf2[Lc, :]), waits=[a1, a2])
                    tok_ybuf2 = a3
                    a4 = pr.v(lambda h, Lc=Lc, r=r: h.tensor_add(
                        yfin[Lc, r * 1024:(r + 1) * 1024], ybuf[Lc, :],
                        pbig[Lc, r * 1024:(r + 1) * 1024]),
                        waits=[a3, t_ydiag, tok_gq_prev])
                    t_asm.append(a4)
                    tok_pdub_free = a4
                for ncn in range(2):
                    for jh in range(2):
                        t_st = None
                        for jj in (jh * 2, jh * 2 + 1):
                            t_st = pr.t(lambda h, c=c, ncn=ncn, jj=jj:
                                        h.matmul(
                                            pdub[:, (jj % 2) * 512:(jj % 2 + 1) * 512],
                                            btm[c * 64:(c + 1) * 64, ncn * P:(ncn + 1) * P],
                                            xtb[c * 64:(c + 1) * 64, jj * 512:(jj + 1) * 512],
                                            start=True, stop=True),
                                        waits=[t_btm, t_xtw, tok_pdub_free])
                        war_xtb = t_st
                        war_btm = t_st
                        t_tb = pr.g(lambda h, ncn=ncn, jh=jh, src_b=src_b,
                                    cdb_off=cdb_off, pa2=pa2:
                                    h.tensor_tensor(
                                        apx(tbuf[:], [[64, 16], [1, 64]]),
                                        apx(S_buf[src_b][:, ncn * T + jh * 1024:ncn * T + (jh + 1) * 1024],
                                            [[64, 16], [1, 64]]),
                                        apx(ssm[pa2][:, cdb_off + jh * 16:cdb_off + jh * 16 + 16],
                                            [[1, 16], [0, 64]]),
                                        ALU.mult),
                                    waits=[tok_S[src_b], t_cdb1, t_cdb2,
                                           tok_tbuf])
                        war_ssm[pa2] = t_tb
                        tok_Sread[src_b].append(t_tb)
                        t_up = pr.v(lambda h, ncn=ncn, jh=jh, dst_b=dst_b:
                                    h.tensor_add(
                                        S_buf[dst_b][:, ncn * T + jh * 1024:ncn * T + (jh + 1) * 1024],
                                        tbuf[:],
                                        pdub[:, 0:1024]),
                                    waits=[t_st, t_tb] + tok_Sread[dst_b])
                        tok_pdub_free = t_up
                        tok_tbuf = t_up
                        tok_S[dst_b] = t_up
                tok_Sread[dst_b] = []
            dump_toks = []
            if debug and pi == DBGPI:
                dump_toks.append(pr.dma("scalar", lambda h: h.dma_start(
                    out=dbg_yfin, in_=yfin[:]), waits=t_asm))
                dump_toks.append(pr.dma("scalar", lambda h: h.dma_start(
                    out=dbg_acr, in_=acr[pa2][:]), waits=[t_acr] + t_asm))
                dump_toks.append(pr.dma("gpsimd", lambda h: h.dma_start(
                    out=dbg_xtw, in_=xtb[:]), waits=[t_xtw] + t_asm))
                dump_toks.append(pr.dma("scalar", lambda h: h.dma_start(
                    out=dbg_S, in_=S_buf[sA][:]), waits=[tok_S[sA]] + t_asm))
            t_zs = pr.s(lambda h: h.activation(zt[:], zt[:], AF.Silu),
                        waits=[tok_zt, tok_zsb])
            if debug and pi == DBGPI:
                dump_toks.append(pr.dma("gpsimd", lambda h: h.dma_start(
                    out=dbg_zs, in_=zt[:]), waits=[t_zs]))
            t_yg = pr.v(lambda h: h.tensor_mul(yfin[:], yfin[:], zt[:]),
                        waits=[t_zs] + t_asm + dump_toks)
            t_gq = pr.s(lambda h: h.activation(zt[:], yfin[:], AF.Square,
                                               accum_out=gst[:, 0:1]),
                        waits=[t_yg, tok_gbf_prev] + dump_toks)
            tok_zsb = t_gq
            tok_gq_prev = t_gq
            t_gl = pr.s(lambda h: h.activation(gst[:, 1:2], gst[:, 0:1], AF.Ln,
                                               bias=c_eps, scale=c_invd),
                        waits=[t_gq])
            t_gr = pr.s(lambda h: h.activation(gst[:, 2:3], gst[:, 1:2], AF.Exp,
                                               scale=-0.5), waits=[t_gl])
            t_gbf = pr.v(lambda h: h.tensor_scalar_mul(gtbf[:], yfin[:],
                                                       gst[:, 2:3]),
                         waits=[t_gr, t_yg])
            tok_gbf_prev = t_gbf
            if debug and pi == DBGPI:
                t_dgt = pr.dma("gpsimd", lambda h: h.dma_start(
                    out=dbg_gt, in_=gtbf[:]), waits=[t_gbf])
                t_dgs = pr.dma("scalar", lambda h: h.dma_start(
                    out=dbg_gst, in_=gst[:]), waits=[t_gr])
                t_dyg = pr.dma("scalar", lambda h: h.dma_start(
                    out=dbg_yg, in_=yfin[:]), waits=[t_yg])
                t_gbf = pr.v(lambda h: h.engine_nop(),
                             waits=[t_gbf, t_dgt, t_dgs, t_dyg])
            for grp in range(2):
                tts = []
                for j in range(8):
                    k = grp * 8 + j
                    tts.append(pr.t(lambda h, j=j, k=k: h.transpose(
                        pbf[:, j * P:(j + 1) * P], gtbf[:, k * P:(k + 1) * P],
                        sb_iden_b[:]), waits=[t_gbf, tok_pbf_free]))
                off = (grp * 8) * T + t0
                tok_pbf_free = pr.v(lambda h, off=off: h.tensor_copy(
                    apx(g_fm[:], [[T, 8], [1, P]], extra_off=off),
                    apx(pbf[:], [[P, 8], [1, P]])), waits=[tts[-1]])
            tok_gtr_done = tok_pbf_free
            tok_pbig_free = t_asm[-1]
            tok_zt = t_gq
        tok_cq = tok_gtr_done

    s5b.close()
    s5c = ExitStack()
    SB5C = mkSB(s5c)
    w32st = SB5C("w32st", [P, NK * 512])
    wfull = SB5C("wfull", [P, NK * 2048], BF16)
    ustage = [SB5C("ust0", [P, 512], BF16), SB5C("ust1", [P, 512], BF16)]

    w_out_r = w_out.rearrange("(k p) c -> p k c", p=P)
    t_w32_free2 = None
    tok_wf = []
    for b in range(4):
        t_wd = pr.dma("sync", lambda h, b=b: h.dma_start(
            out=apx(w32st[:], [[512, NK], [1, 512]]),
            in_=w_out_r[:, :, b * 512:(b + 1) * 512]),
            waits=[t_w32_free2, tok_gtr_done] + (pr.fence() if b == 0 else []))
        casts2 = []
        for k in range(NK):
            casts2.append(pr.v(lambda h, b=b, k=k: h.tensor_scalar_mul(
                wfull[:, k * 2048 + b * 512:k * 2048 + (b + 1) * 512],
                w32st[:, k * 512:(k + 1) * 512], sb_gnormw[:, k:k + 1]),
                waits=[t_wd, tok_consts]))
        t_w32_free2 = casts2[-1]
        tok_wf.append(casts2[-1])

    tok_ups = [tok_pdub_free, tok_pbig_free]
    tok_ust = [None, None]
    ui = [0]
    for uc in range(NK):
        for tn in range(4):
            sp = (uc * 4 + tn) % 2
            pps = [pdub[:, 0:512], pbig[:, 0:512]][sp]
            mms = []
            for k in range(NK):
                mms.append(pr.t(lambda h, uc=uc, tn=tn, k=k, pps=pps: h.matmul(
                    pps, wfull[:, k * 2048 + uc * P:k * 2048 + (uc + 1) * P],
                    g_fm[:, k * T + tn * 512:k * T + (tn + 1) * 512],
                    start=(k == 0), stop=(k == NK - 1)),
                    waits=tok_wf + [tok_ups[sp], tok_gtr_done]))
            usl = ui[0] % 2
            ui[0] += 1
            t_ue = pr.s(lambda h, usl=usl, pps=pps: h.activation(
                ustage[usl][:], pps, AF.Copy), waits=[mms[-1], tok_ust[usl]])
            tok_ups[sp] = t_ue
            tok_ust[usl] = pr.dma(
                "scalar", lambda h, usl=usl, uc=uc, tn=tn: h.dma_start(
                    out=u_dram[uc * P:(uc + 1) * P, tn * 512:(tn + 1) * 512],
                    in_=ustage[usl][:]), waits=[t_ue])

    barrier5 = [tok_ust[0], tok_ust[1]] + pr.fence_all()
    if debug:
        t_dbg_g = pr.dma("gpsimd", lambda h: h.dma_start(out=dbg_gfm, in_=g_fm[:]),
                         waits=[tok_gtr_done] + pr.fence())
        barrier5.append(t_dbg_g)
        t_dbg_dtv = pr.dma("scalar", lambda h: h.dma_start(out=dbg_dtv, in_=dt3[:]),
                           waits=[t_adt] + pr.fence())
        barrier5.append(t_dbg_dtv)
    s5c.close()
    s5.close()

    s6 = ExitStack()
    SB6 = mkSB(s6)
    ublock = SB6("ublock", [P, NK * 1024], BF16)
    wobf = SB6("wobf", [P, NK * 2048], BF16)
    xtile = SB6("xtile", [P, D])
    ftile = SB6("ftile", [P, D])
    ostg = [SB6("ostg%d" % i, [P, 512]) for i in range(4)]

    wo_r = wo_w.rearrange("(k p) c -> p k c", p=P)
    t_wo = pr.dma("gpsimd", lambda h: h.dma_start(
        out=apx(wobf[:], [[2048, NK], [1, 2048]]), in_=wo_r), waits=barrier5)

    tok_ub = None
    tok_xf = None
    tok_og = [None] * 4
    tok_vps = [None, None, None]
    oi2 = [0]
    t_lastout = None
    u_r = u_dram.rearrange("(k p) t -> p k t", p=P)
    for thalf in range(2):
        tok_ub = pr.dma("sync", lambda h, thalf=thalf: h.dma_start(
            out=apx(ublock[:], [[1024, NK], [1, 1024]]),
            in_=u_r[:, :, thalf * 1024:(thalf + 1) * 1024]),
            waits=[tok_ub] + barrier5 + ([tok_vps[0], tok_vps[1], tok_vps[2]]
                                         if thalf else []))
        for M in range(8):
            Mg = thalf * 8 + M
            tok_xf = pr.dma("sync", lambda h, Mg=Mg: h.dma_start(
                out=xtile[:], in_=x[Mg * P:(Mg + 1) * P, :]),
                waits=[tok_xf] + (barrier5 if Mg == 0 else []))
            tok_xf = pr.dma("sync", lambda h, Mg=Mg: h.dma_start(
                out=ftile[:], in_=ffn_dram[Mg * P:(Mg + 1) * P, :]),
                waits=[tok_xf])
            last_o2 = None
            for vn in range(4):
                sp = (M * 4 + vn) % 3
                pps = [pdub[:, 0:512], pbig[:, 0:512], pbig[:, 512:1024]][sp]
                mms = []
                for k in range(NK):
                    mms.append(pr.t(lambda h, M=M, vn=vn, k=k, pps=pps: h.matmul(
                        pps, ublock[:, k * 1024 + M * P:k * 1024 + (M + 1) * P],
                        wobf[:, k * 2048 + vn * 512:k * 2048 + (vn + 1) * 512],
                        start=(k == 0), stop=(k == NK - 1)),
                        waits=[t_wo, tok_ub, tok_vps[sp]]))
                oi = oi2[0] % 4
                oi2[0] += 1
                o1 = pr.v(lambda h, oi=oi, vn=vn, pps=pps: h.tensor_add(
                    ostg[oi][:], pps, xtile[:, vn * 512:(vn + 1) * 512]),
                    waits=[mms[-1], tok_xf, tok_og[oi]])
                o2 = pr.v(lambda h, oi=oi, vn=vn: h.tensor_add(
                    ostg[oi][:], ostg[oi][:], ftile[:, vn * 512:(vn + 1) * 512]),
                    waits=[o1])
                tok_vps[sp] = o2
                last_o2 = o2
                t_lastout = pr.dma(
                    "scalar", lambda h, oi=oi, Mg=Mg, vn=vn: h.dma_start(
                        out=out[Mg * P:(Mg + 1) * P, vn * 512:(vn + 1) * 512],
                        in_=ostg[oi][:]), waits=[o2])
                tok_og[oi] = t_lastout
            tok_xf = last_o2

    tout = t_lastout
    if debug:
        tout = pr.dma("scalar", lambda h: h.dma_start(out=dbg_z, in_=z_dram),
                      waits=[tout])
        tout = pr.dma("scalar", lambda h: h.dma_start(out=dbg_xbc, in_=xbc_dram),
                      waits=[tout])
        tout = pr.dma("scalar", lambda h: h.dma_start(out=dbg_ffn, in_=ffn_dram),
                      waits=[tout])
        tout = pr.dma("scalar", lambda h: h.dma_start(out=dbg_dt, in_=dt_fm[:]),
                      waits=[tout])
        tout = pr.dma("scalar", lambda h: h.dma_start(out=dbg_srs, in_=srs[:]),
                      waits=[tout])
        tout = pr.dma("scalar", lambda h: h.dma_start(out=dbg_stat, in_=stat[:]),
                      waits=[tout])
        tout = pr.dma("scalar", lambda h: h.dma_start(out=dbg_m, in_=m_dram),
                      waits=[tout])
        tout = pr.dma("gpsimd", lambda h: h.dma_start(
            out=dbg_cq, in_=convq_dram.rearrange("q c t -> (q c) t")), waits=[tout])
        tout = pr.dma("gpsimd", lambda h: h.dma_start(out=dbg_act, in_=actT_dram),
                      waits=[tout])

    pr._op("sync", lambda h: h.nop(), [tout], "qs", False)

    s6.close()
    with es, nc.Block() as block:
        pr.finalize(nc, block, sems)
    return nc


def make_constants():
    idx = np.arange(P)
    tri = ((idx[:, None] <= idx[None, :]) &
           (idx[:, None] // CHUNK == idx[None, :] // CHUNK)).astype(np.float32)
    ut = ((idx[:, None] > idx[None, :]) &
          (idx[:, None] // CHUNK == idx[None, :] // CHUNK)).astype(np.float32)
    e63a = np.zeros((P, P), np.float32)
    e63a[0:64, :] = 1.0
    e127a = np.zeros((P, P), np.float32)
    e127a[64:128, :] = 1.0
    esl = np.zeros((HEADS, HEADS * P), np.float32)
    for hh in range(HEADS):
        esl[hh, hh * P:(hh + 1) * P] = 1.0
    return {
        "iden_f": np.eye(P, dtype=np.float32),
        "tri_cum": tri, "ut_strict": ut, "e63": e63a, "e127": e127a,
        "ones1": np.ones((1, P), np.float32),
        "esel": esl,
    }


def make_in_map(core, inputs):
    m = {
        "x": np.ascontiguousarray(inputs["L0_attn_x"][core]),
        "w_in": inputs["L0_attn_w_in"],
        "conv_w": inputs["L0_attn_conv_w"],
        "conv_b2": np.ascontiguousarray(inputs["L0_attn_conv_b"].reshape(NCCH, P)),
        "dt_bias": np.ascontiguousarray(inputs["L0_attn_dt_bias"].reshape(HEADS, 1)),
        "a_log": np.ascontiguousarray(inputs["L0_attn_A_log"].reshape(HEADS, 1)),
        "d_skip": np.ascontiguousarray(inputs["L0_attn_D_skip"].reshape(1, HEADS)),
        "normw2": np.ascontiguousarray(inputs["L0_attn_norm_w"].reshape(NK, P).T),
        "gnormw2": np.ascontiguousarray(inputs["L0_attn_gnorm_w"].reshape(NK, P).T),
        "w_out": inputs["L0_attn_w_out"],
        "wo_w": inputs["L0_attn_wo_w"],
        "gate_w": inputs["L0_ffn_gate_w"],
        "w1": inputs["L0_ffn_w1"],
        "w3": inputs["L0_ffn_w3"],
        "w2": inputs["L0_ffn_w2"],
        "sw1": inputs["L0_ffn_sw1"],
        "sw3": inputs["L0_ffn_sw3"],
        "sw2": inputs["L0_ffn_sw2"],
    }
    m.update(make_constants())
    return m


def kernel(**inputs):
    inputs = {k: np.asarray(v) for k, v in inputs.items()}
    nc = build_program()
    in_maps = [make_in_map(b, inputs) for b in range(8)]
    res = run_bass_kernel_spmd(nc, in_maps, core_ids=list(range(8)))
    return np.stack([np.asarray(res.results[b]["out"]) for b in range(8)], axis=0)


# version 09 (final)
# speedup vs baseline: 1.1134x; 1.0066x over previous
import numpy as np
from contextlib import ExitStack

import concourse.bass as bass
import concourse.mybir as mybir
from concourse.bass_utils import run_bass_kernel_spmd

F32 = mybir.dt.float32
BF16 = mybir.dt.bfloat16
AF = mybir.ActivationFunctionType
ALU = mybir.AluOpType
AXX = mybir.AxisListType.X

P = 128
D = 2048
T = 2048
NK = D // P
NTILE = T // P
PROJ = 4640
CONV = 2560
NCCH = CONV // P
HEADS = 32
HD = 64
STATE = 256
CHUNK = 64
EPS = 1e-5
E_RT = 8
I_RT = 1408
I_SH = 2816
KSTACK = E_RT * I_RT + I_SH
NKST = KSTACK // P
INV_D = 1.0 / D
DBGPI = 1
import os
SAFE = int(os.environ.get("K_SAFE", "0"))

ENGINES = ("tensor", "vector", "scalar", "gpsimd", "sync")
SEMKEYS = ("pe", "dve", "act", "pool", "qs", "qa", "qg")
DMA_SEM = {"sync": "qs", "scalar": "qa", "gpsimd": "qg"}


class Prog:

    def __init__(self):
        self.ops = {e: [] for e in ENGINES}
        self.nid = 0
        self.used = set()
        self.last = {}

    def _new_id(self):
        self.nid += 1
        return self.nid

    def _op(self, eng, build, waits, semkey, is_dma):
        opid = self._new_id()
        wl = []
        for t in waits:
            if t is None:
                continue
            sk, oid = t
            self.used.add((sk, oid))
            wl.append((sk, oid))
        if is_dma:
            self.used.add((semkey, opid))
        self.ops[eng].append((wl, build, semkey, opid, is_dma))
        self.last[semkey] = (semkey, opid)
        return (semkey, opid)

    def fence(self, keys=("pe", "dve", "act", "pool")):
        return [self.last[k] for k in keys if k in self.last]

    def fence_all(self):
        return [self.last[k] for k in SEMKEYS if k in self.last]

    def t(self, build, waits=()):
        return self._op("tensor", build, waits, "pe", False)

    def v(self, build, waits=()):
        return self._op("vector", build, waits, "dve", False)

    def s(self, build, waits=()):
        return self._op("scalar", build, waits, "act", False)

    def g(self, build, waits=()):
        return self._op("gpsimd", build, waits, "pool", False)

    def dma(self, eng, build, waits=()):
        return self._op(eng, build, waits, DMA_SEM[eng], True)

    def finalize(self, nc, block, sems):
        marked = {k: [] for k in SEMKEYS}
        for eng in ENGINES:
            for (wl, build, sk, opid, is_dma) in self.ops[eng]:
                if (sk, opid) in self.used:
                    marked[sk].append(opid)
        rank = {}
        for sk in SEMKEYS:
            for i, oid in enumerate(sorted(marked[sk])):
                rank[(sk, oid)] = i + 1

        def section(eng):
            lastwait = {}

            def body(h):
                for (wl, build, sk, opid, is_dma) in self.ops[eng]:
                    for (wsk, woid) in wl:
                        step = 16 if wsk in ("qs", "qa", "qg") else 1
                        val = rank[(wsk, woid)] * step
                        if lastwait.get(wsk, 0) < val:
                            h.wait_ge(sems[wsk], val)
                            lastwait[wsk] = val
                    ins = build(h)
                    if (sk, opid) in self.used:
                        step = 16 if sk in ("qs", "qa", "qg") else 1
                        ins.then_inc(sems[sk], step)
            return body

        for eng in ENGINES:
            if self.ops[eng]:
                getattr(block, eng)(section(eng))


def apx(base_ap, free_dims, extra_off=0):
    return bass.AP(base_ap.tensor, base_ap.offset + extra_off,
                   [list(base_ap.ap[0])] + [list(d) for d in free_dims])


def build_program(debug=False):
    nc = bass.Bass("TRN2", target_bir_lowering=False)
    pr = Prog()

    def din(name, shape, dt=F32):
        return nc.dram_tensor(name, shape, dt, kind="ExternalInput").ap()

    x = din("x", [T, D])
    w_in = din("w_in", [D, PROJ])
    conv_w = din("conv_w", [CONV, 4])
    conv_b2 = din("conv_b2", [NCCH, P])
    dt_bias = din("dt_bias", [HEADS, 1])
    a_log = din("a_log", [HEADS, 1])
    d_skip = din("d_skip", [1, HEADS])
    normw2 = din("normw2", [P, NK])
    gnormw2 = din("gnormw2", [P, NK])
    w_out = din("w_out", [D, D])
    wo_w = din("wo_w", [D, D])
    gate_w = din("gate_w", [E_RT, D])
    w1 = din("w1", [E_RT, D, I_RT])
    w3 = din("w3", [E_RT, D, I_RT])
    w2 = din("w2", [E_RT, I_RT, D])
    sw1 = din("sw1", [D, I_SH])
    sw3 = din("sw3", [D, I_SH])
    sw2 = din("sw2", [I_SH, D])
    iden_f = din("iden_f", [P, P])
    tri_cum = din("tri_cum", [P, P])
    ut_strict = din("ut_strict", [P, P])
    e63 = din("e63", [P, P])
    e127 = din("e127", [P, P])
    ones1 = din("ones1", [1, P])
    esel = din("esel", [HEADS, HEADS * P])

    z_dram = nc.dram_tensor("z_dram", [T, D], F32, kind="Internal").ap()
    xbc_dram = nc.dram_tensor("xbc_dram", [CONV, T], F32, kind="Internal").ap()
    convq_dram = nc.dram_tensor("convq_dram", [4, CONV, 512], BF16, kind="Internal").ap()
    actT_dram = nc.dram_tensor("actT_dram", [KSTACK, T], BF16, kind="Internal").ap()
    ffn_dram = nc.dram_tensor("ffn_dram", [T, D], F32, kind="Internal").ap()
    srs_dram = nc.dram_tensor("srs_dram", [T], F32, kind="Internal").ap()
    m_dram = nc.dram_tensor("m_dram", [P, P], F32, kind="Internal").ap()
    u_dram = nc.dram_tensor("u_dram", [D, T], BF16, kind="Internal").ap()
    out = nc.dram_tensor("out", [T, D], F32, kind="ExternalOutput").ap()
    if debug:
        dbg_z = nc.dram_tensor("dbg_z", [T, D], F32, kind="ExternalOutput").ap()
        dbg_xbc = nc.dram_tensor("dbg_xbc", [CONV, T], F32, kind="ExternalOutput").ap()
        dbg_dt = nc.dram_tensor("dbg_dt", [HEADS, T], F32, kind="ExternalOutput").ap()
        dbg_xnf = nc.dram_tensor("dbg_xnf", [P, NK * T], F32, kind="ExternalOutput").ap()
        dbg_srs = nc.dram_tensor("dbg_srs", [P, NTILE], F32, kind="ExternalOutput").ap()
        dbg_stat = nc.dram_tensor("dbg_stat", [P, 16], F32, kind="ExternalOutput").ap()
        dbg_ffn = nc.dram_tensor("dbg_ffn", [T, D], F32, kind="ExternalOutput").ap()
        dbg_gfm = nc.dram_tensor("dbg_gfm", [P, NK * T], F32, kind="ExternalOutput").ap()
        dbg_m = nc.dram_tensor("dbg_m", [P, P], F32, kind="ExternalOutput").ap()
        dbg_cq = nc.dram_tensor("dbg_cq", [4 * CONV, 512], F32, kind="ExternalOutput").ap()
        dbg_dtv = nc.dram_tensor("dbg_dtv", [HEADS, 2 * T], F32, kind="ExternalOutput").ap()
        dbg_acr = nc.dram_tensor("dbg_acr", [P, P], F32, kind="ExternalOutput").ap()
        dbg_ssm = nc.dram_tensor("dbg_ssm", [P, 160], F32, kind="ExternalOutput").ap()
        dbg_yfin = nc.dram_tensor("dbg_yfin", [P, T], F32, kind="ExternalOutput").ap()
        dbg_xtw = nc.dram_tensor("dbg_xtw", [P, T], F32, kind="ExternalOutput").ap()
        dbg_S = nc.dram_tensor("dbg_S", [P, 2 * T], F32, kind="ExternalOutput").ap()
        dbg_act = nc.dram_tensor("dbg_act", [KSTACK, T], F32, kind="ExternalOutput").ap()
        dbg_zs = nc.dram_tensor("dbg_zs", [P, T], F32, kind="ExternalOutput").ap()
        dbg_gt = nc.dram_tensor("dbg_gt", [P, T], F32, kind="ExternalOutput").ap()
        dbg_gst = nc.dram_tensor("dbg_gst", [P, 4], F32, kind="ExternalOutput").ap()
        dbg_yg = nc.dram_tensor("dbg_yg", [P, T], F32, kind="ExternalOutput").ap()

    es = ExitStack()

    def mkSB(stack):
        return lambda n, s, d=F32: stack.enter_context(nc.sbuf_tensor(n, s, d))

    SB = mkSB(es)
    PSUM = lambda n, s, d=F32: es.enter_context(nc.psum_tensor(n, s, d))

    pbig = PSUM("pbig", [P, 2048])
    pdub = PSUM("pdub", [P, 1024])
    pbf = PSUM("pbf", [P, 1024], BF16)
    pmix = PSUM("pmix", [P, 512])

    sb_iden_f = SB("c_idenf", [P, P])
    sb_iden_b = SB("c_idenb", [P, P], BF16)
    sb_tri = SB("c_tri", [P, P])
    sb_ut = SB("c_ut", [P, P])
    sb_e63 = SB("c_e63", [P, P])
    sb_e127 = SB("c_e127", [P, P])
    sb_ones = SB("c_ones", [1, P])
    sb_normw = SB("c_normw", [P, NK])
    sb_gnormw = SB("c_gnormw", [P, NK])
    srs = SB("srs", [P, NTILE])
    stat = SB("stat", [P, 16])
    cpool = SB("cpool", [P, 4])
    dt_fm = SB("dt_fm", [HEADS, T])
    dt3 = SB("dt3", [HEADS, 2 * T])
    dskb = SB("dskb", [P, HEADS])
    dtb_sb = SB("dtb_sb", [HEADS, 1])
    alog_sb = SB("alog_sb", [HEADS, 2])
    dskrow = SB("dskrow", [1, HEADS])

    sems = {}
    for sk in SEMKEYS:
        sems[sk] = es.enter_context(nc.semaphore("sem_" + sk))

    c_tok = []
    for dst, src in [(sb_iden_f, iden_f), (sb_tri, tri_cum),
                     (sb_ut, ut_strict), (sb_e63, e63), (sb_e127, e127),
                     (sb_ones, ones1), (sb_normw, normw2), (sb_gnormw, gnormw2)]:
        c_tok.append(pr.dma("sync", lambda h, d=dst, s=src: h.dma_start(out=d[:], in_=s)))
    c_tok.append(pr.dma("gpsimd", lambda h: h.dma_start(out=sb_iden_b[:], in_=iden_f)))
    tok_consts = c_tok[-2]
    tok_consts_b = c_tok[-1]
    pr.v(lambda h: h.memset(cpool[:, 0:1], EPS))
    pr.v(lambda h: h.memset(cpool[:, 2:3], -1e30))
    tok_cpool = pr.v(lambda h: h.memset(cpool[:, 1:2], INV_D))
    c_eps = cpool[:, 0:1]
    c_invd = cpool[:, 1:2]
    c_n30 = cpool[:, 2:3]

    s_xnf = ExitStack()
    xnf = mkSB(s_xnf)("xnf", [P, NK * T], BF16)

    s12 = ExitStack()
    SB12 = mkSB(s12)
    junk2k = SB12("junk2k", [P, 2048])
    xt = [SB12("xt0", [P, D]), SB12("xt1", [P, D])]
    xnt = [SB12("xnt0", [P, D], BF16), SB12("xnt1", [P, D], BF16)]

    tok_xt_free = [None, None]
    tok_xnt_free = [None, None]
    tok_pbf_free = None
    tok_srs_last = None

    for ti in range(NTILE):
        pa = ti % 2
        t_dmax = pr.dma(
            "sync", lambda h, pa=pa, ti=ti: h.dma_start(
                out=xt[pa][:], in_=x[ti * P:(ti + 1) * P, :]),
            waits=[tok_xt_free[pa]])
        w0 = [t_dmax]
        t_sum = pr.v(lambda h, pa=pa: h.reduce_sum(stat[:, 0:1], xt[pa][:], axis=AXX),
                     waits=w0)
        t_sq = pr.s(lambda h, pa=pa: h.activation(
            junk2k[:], xt[pa][:], AF.Square, accum_out=stat[:, 1:2]), waits=w0)
        t_mean = pr.v(lambda h: h.tensor_scalar_mul(stat[:, 2:3], stat[:, 0:1], INV_D),
                      waits=[t_sum])
        t_m2 = pr.v(lambda h: h.tensor_mul(stat[:, 3:4], stat[:, 2:3], stat[:, 2:3]),
                    waits=[t_mean])
        t_v0 = pr.v(lambda h: h.tensor_scalar_mul(stat[:, 4:5], stat[:, 1:2], INV_D),
                    waits=[t_sq])
        t_var = pr.v(lambda h: h.tensor_sub(stat[:, 4:5], stat[:, 4:5], stat[:, 3:4]),
                     waits=[t_v0, t_m2])
        t_lnv = pr.s(lambda h: h.activation(stat[:, 5:6], stat[:, 4:5], AF.Ln,
                                            bias=c_eps),
                     waits=[t_var, tok_cpool])
        t_rstd = pr.s(lambda h: h.activation(stat[:, 6:7], stat[:, 5:6], AF.Exp,
                                             scale=-0.5), waits=[t_lnv])
        t_negmb = pr.v(lambda h: h.scalar_tensor_tensor(
            stat[:, 7:8], stat[:, 2:3], -1.0, stat[:, 6:7], ALU.mult, ALU.mult),
            waits=[t_rstd, t_mean])
        t_msq = pr.s(lambda h, pa=pa: h.activation(
            junk2k[:], xt[pa][:], AF.Square, bias=stat[:, 7:8], scale=stat[:, 6:7],
            accum_out=stat[:, 8:9]), waits=[t_negmb])
        t_msqm = pr.v(lambda h: h.tensor_scalar_mul(stat[:, 9:10], stat[:, 8:9], INV_D),
                      waits=[t_msq])
        t_l2 = pr.s(lambda h: h.activation(stat[:, 10:11], stat[:, 9:10], AF.Ln,
                                           bias=c_eps), waits=[t_msqm])
        t_rs2 = pr.s(lambda h, ti=ti: h.activation(
            srs[:, ti:ti + 1], stat[:, 10:11], AF.Exp, scale=-0.5), waits=[t_l2])
        tok_srs_last = t_rs2
        t_comb = pr.v(lambda h: h.tensor_copy(stat[:, 12:13], stat[:, 6:7]),
                      waits=[t_rstd])
        t_negc = pr.v(lambda h: h.scalar_tensor_tensor(
            stat[:, 13:14], stat[:, 2:3], -1.0, stat[:, 12:13], ALU.mult, ALU.mult),
            waits=[t_comb])
        tok_xt_free[pa] = pr.v(lambda h, pa=pa: h.tensor_scalar(
            xnt[pa][:], xt[pa][:], stat[:, 12:13], stat[:, 13:14],
            ALU.mult, ALU.add),
            waits=[t_negc, tok_xnt_free[pa]])
        for grp in range(2):
            tts = []
            for j in range(8):
                k = grp * 8 + j
                tts.append(pr.t(lambda h, pa=pa, k=k, j=j: h.transpose(
                    pbf[:, j * P:(j + 1) * P], xnt[pa][:, k * P:(k + 1) * P],
                    sb_iden_b[:]),
                    waits=[tok_xt_free[pa], tok_pbf_free, tok_consts,
                           tok_consts_b]))
            off = (grp * 8) * T + ti * P
            tok_pbf_free = pr.v(lambda h, off=off: h.tensor_copy(
                apx(xnf[:], [[T, 8], [1, P]], extra_off=off),
                apx(pbf[:], [[P, 8], [1, P]])), waits=[tts[-1]])
            if grp == 1:
                tok_xnt_free[pa] = tok_pbf_free

    wst32 = SB12("wst32", [P, NK * 256])
    wstbf = [SB12("wstbf0", [P, NK * 512], BF16), SB12("wstbf1", [P, NK * 512], BF16)]
    ostage = [SB12("ostage%d" % i, [P, 512]) for i in range(4)]
    srsT = SB12("srsT", [NTILE, P])
    rs2_b = SB12("rs2_b", [P, 2048])
    srs_row = SB12("srs_row", [1, 2048])

    w_in_r = w_in.rearrange("(k p) c -> p k c", p=P)

    t_tr = pr.t(lambda h: h.transpose(pmix[0:NTILE, 0:P], srs[:], sb_iden_f[:]),
                waits=[tok_srs_last, tok_consts])
    t_srs_ev = pr.v(lambda h: h.tensor_copy(srsT[:], pmix[0:NTILE, 0:P]), waits=[t_tr])
    t_srs_out = pr.dma("scalar", lambda h: h.dma_start(
        out=srs_dram.rearrange("(a b) -> a b", a=NTILE), in_=srsT[:]),
        waits=[t_srs_ev])
    t_srs_in = pr.dma("sync", lambda h: h.dma_start(
        out=srs_row[:], in_=srs_dram.rearrange("(a b) -> a b", a=1)),
        waits=[t_srs_out])
    k1s = []
    for j in range(4):
        k1s.append(pr.t(lambda h, j=j: h.matmul(
            pbig[:, j * 512:(j + 1) * 512], sb_ones[:],
            srs_row[:, j * 512:(j + 1) * 512], start=True, stop=True),
            waits=[t_srs_in]))
    t_rs2b = pr.v(lambda h: h.tensor_copy(rs2_b[:], pbig[:]), waits=[k1s[-1]])

    blocks = [(c0, 512, "z") for c0 in range(0, 2048, 512)] \
        + [(c0, 512, "xbc") for c0 in range(2048, 4608, 512)] \
        + [(4608, 32, "dt")]

    tok_cast = [None, None]
    tok_w32_free = None
    tok_ost_free = [None] * 4
    tok_psB = [None, None]
    ost_i = [0]

    def next_ost():
        i = ost_i[0] % 4
        ost_i[0] += 1
        return i

    for bi, (c0, C, kind) in enumerate(blocks):
        pa = bi % 2
        casts = []
        for hb in range(0, C, 256):
            hw = min(256, C - hb)
            t_w32 = pr.dma("sync", lambda h, c0=c0, hb=hb, hw=hw: h.dma_start(
                out=apx(wst32[:], [[256, NK], [1, hw]]),
                in_=w_in_r[:, :, c0 + hb:c0 + hb + hw]), waits=[tok_w32_free])
            for k in range(NK):
                casts.append(pr.v(
                    lambda h, k=k, C=C, pa=pa, hb=hb, hw=hw: h.tensor_scalar_mul(
                        wstbf[pa][:, k * C + hb:k * C + hb + hw],
                        wst32[:, k * 256:k * 256 + hw],
                        sb_normw[:, k:k + 1]),
                    waits=[t_w32, tok_cast[pa], tok_consts]))
            tok_w32_free = casts[-1]
        tok_cast[pa] = None
        last_use = None

        if kind == "z":
            for M in range(NTILE):
                pp = M % 2
                ps = pdub[:, pp * 512:(pp + 1) * 512]
                mms = []
                for k in range(NK):
                    mms.append(pr.t(lambda h, k=k, M=M, pa=pa, C=C, ps=ps:
                                    h.matmul(ps, xnf[:, k * T + M * P:k * T + (M + 1) * P],
                                             wstbf[pa][:, k * C:(k + 1) * C],
                                             start=(k == 0), stop=(k == NK - 1)),
                                    waits=[casts[-1], tok_psB[pp]]))
                last_use = mms[-1]
                oi = next_ost()
                t_ev = pr.v(lambda h, oi=oi, ps=ps, M=M: h.tensor_scalar_mul(
                    ostage[oi][:], ps, srs[:, M:M + 1]),
                    waits=[mms[-1], tok_ost_free[oi], t_rs2b])
                tok_psB[pp] = t_ev
                tok_ost_free[oi] = pr.dma(
                    "scalar", lambda h, oi=oi, M=M, c0=c0: h.dma_start(
                        out=z_dram[M * P:(M + 1) * P, c0:c0 + 512], in_=ostage[oi][:]),
                    waits=[t_ev])
        elif kind == "xbc":
            for PC in range(4):
                for TN in range(4):
                    pp = (PC * 4 + TN) % 2
                    ps = pdub[:, pp * 512:(pp + 1) * 512]
                    mms = []
                    for k in range(NK):
                        mms.append(pr.t(lambda h, k=k, PC=PC, TN=TN, pa=pa, C=C, ps=ps:
                                        h.matmul(ps,
                                                 wstbf[pa][:, k * C + PC * P:k * C + (PC + 1) * P],
                                                 xnf[:, k * T + TN * 512:k * T + (TN + 1) * 512],
                                                 start=(k == 0), stop=(k == NK - 1)),
                                        waits=[casts[-1], tok_psB[pp]]))
                    last_use = mms[-1]
                    oi = next_ost()
                    t_ev = pr.v(lambda h, oi=oi, ps=ps, TN=TN: h.tensor_tensor(
                        ostage[oi][:], ps, rs2_b[:, TN * 512:(TN + 1) * 512],
                        ALU.mult),
                        waits=[mms[-1], tok_ost_free[oi], t_rs2b])
                    tok_psB[pp] = t_ev
                    r0 = c0 - 2048 + PC * P
                    tok_ost_free[oi] = pr.dma(
                        "scalar", lambda h, oi=oi, r0=r0, TN=TN: h.dma_start(
                            out=xbc_dram[r0:r0 + P, TN * 512:(TN + 1) * 512],
                            in_=ostage[oi][:]),
                        waits=[t_ev])
        else:
            for TN in range(4):
                pp = TN % 2
                ps = pdub[0:HEADS, pp * 512:(pp + 1) * 512]
                mms = []
                for k in range(NK):
                    mms.append(pr.t(lambda h, k=k, TN=TN, pa=pa, C=C, ps=ps:
                                    h.matmul(ps,
                                             wstbf[pa][:, k * C:k * C + HEADS],
                                             xnf[:, k * T + TN * 512:k * T + (TN + 1) * 512],
                                             start=(k == 0), stop=(k == NK - 1)),
                                    waits=[casts[-1], tok_psB[pp]]))
                last_use = mms[-1]
                tok_psB[pp] = pr.v(lambda h, TN=TN, ps=ps: h.tensor_tensor(
                    dt_fm[:, TN * 512:(TN + 1) * 512], ps,
                    rs2_b[0:HEADS, TN * 512:(TN + 1) * 512], ALU.mult),
                    waits=[mms[-1], t_rs2b])
        tok_cast[pa] = last_use

    s12.close()
    barrier23 = [t for t in tok_ost_free if t is not None] + pr.fence_all()

    s3 = ExitStack()
    SB3 = mkSB(s3)
    gwf = SB3("gwf", [E_RT, D])
    gwT32 = SB3("gwT32", [P, NTILE * E_RT])
    gwhi = SB3("gwhi", [P, NTILE * E_RT], BF16)
    gwlo = SB3("gwlo", [P, NTILE * E_RT], BF16)
    glog = SB3("glog", [P, E_RT])
    ge1 = SB3("ge1", [P, E_RT])
    ge2 = SB3("ge2", [P, E_RT])
    glm = SB3("glm", [P, E_RT])
    gsc = SB3("gsc", [P, 8])
    m_tm = SB3("m_tm", [P, NTILE * E_RT])
    mT = SB3("mT", [P, P])
    me_b = SB3("me_b", [P, T])
    w1blk = [SB3("w1blk0", [P, NK * 512], BF16), SB3("w1blk1", [P, NK * 512], BF16)]
    w3blk = [SB3("w3blk0", [P, NK * 512], BF16), SB3("w3blk1", [P, NK * 512], BF16)]
    astage = [SB3("ast%d" % i, [P, 512], BF16) for i in range(4)]
    shtile = [SB3("sht%d" % i, [P, 512]) for i in range(2)]
    a1tile = [SB3("a1t%d" % i, [P, 512]) for i in range(2)]

    t_gw = pr.dma("sync", lambda h: h.dma_start(out=gwf[:], in_=gate_w),
                  waits=barrier23)
    gts = []
    for k in range(NK):
        gts.append(pr.t(lambda h, k=k: h.transpose(
            pmix[:, k * 8:(k + 1) * 8], gwf[0:E_RT, k * P:(k + 1) * P],
            sb_iden_f[0:E_RT, 0:E_RT]), waits=[t_gw, t_srs_ev]))
    t_gwT = pr.v(lambda h: h.tensor_copy(gwT32[:], pmix[:, 0:NTILE * E_RT]),
                 waits=[gts[-1]])
    t_ghi = pr.v(lambda h: h.tensor_copy(gwhi[:], gwT32[:]), waits=[t_gwT])
    t_glo = pr.v(lambda h: h.tensor_tensor(gwlo[:], gwT32[:], gwhi[:],
                                           ALU.subtract), waits=[t_ghi])

    t_pm_free = t_gwT
    tok_m_last = None
    for M in range(NTILE):
        mms = []
        for k in range(NK):
            mms.append(pr.t(lambda h, k=k, M=M: h.matmul(
                pmix[:, 0:E_RT], xnf[:, k * T + M * P:k * T + (M + 1) * P],
                gwhi[:, k * 8:(k + 1) * 8], start=(k == 0), stop=False),
                waits=[t_glo, t_pm_free]))
        for k in range(NK):
            mms.append(pr.t(lambda h, k=k, M=M: h.matmul(
                pmix[:, 0:E_RT], xnf[:, k * T + M * P:k * T + (M + 1) * P],
                gwlo[:, k * 8:(k + 1) * 8], start=False, stop=(k == NK - 1)),
                waits=[t_glo]))
        t_lg = pr.v(lambda h: h.tensor_copy(glog[:], pmix[:, 0:E_RT]),
                    waits=[mms[-1], tok_m_last])
        t_pm_free = t_lg
        t_mx1 = pr.v(lambda h: h.reduce_max(gsc[:, 0:1], glog[:], axis=AXX),
                     waits=[t_lg])
        t_e1 = pr.v(lambda h: h.tensor_tensor(
            ge1[:], glog[:], apx(gsc[:, 0:1], [[0, E_RT]]), ALU.is_equal),
            waits=[t_mx1])
        t_l30 = pr.v(lambda h: h.tensor_scalar_mul(glm[:], ge1[:], c_n30),
                     waits=[t_e1, tok_cpool])
        t_lm = pr.v(lambda h: h.tensor_add(glm[:], glm[:], glog[:]), waits=[t_l30])
        t_mx2 = pr.v(lambda h: h.reduce_max(gsc[:, 1:2], glm[:], axis=AXX),
                     waits=[t_lm])
        t_e2 = pr.v(lambda h: h.tensor_tensor(
            ge2[:], glm[:], apx(gsc[:, 1:2], [[0, E_RT]]), ALU.is_equal),
            waits=[t_mx2])
        t_d = pr.v(lambda h: h.tensor_sub(gsc[:, 2:3], gsc[:, 1:2], gsc[:, 0:1]),
                   waits=[t_mx2])
        t_ed = pr.s(lambda h: h.activation(gsc[:, 3:4], gsc[:, 2:3], AF.Exp),
                    waits=[t_d])
        t_den = pr.v(lambda h: h.tensor_scalar_add(gsc[:, 4:5], gsc[:, 3:4], 1.0),
                     waits=[t_ed])
        t_r = pr.v(lambda h: h.reciprocal(gsc[:, 5:6], gsc[:, 4:5]), waits=[t_den])
        t_w2v = pr.v(lambda h: h.tensor_mul(gsc[:, 6:7], gsc[:, 3:4], gsc[:, 5:6]),
                     waits=[t_r])
        t_m1 = pr.v(lambda h: h.tensor_scalar_mul(ge1[:], ge1[:], gsc[:, 5:6]),
                    waits=[t_r, t_e1])
        tok_m_last = pr.v(lambda h, M=M: h.scalar_tensor_tensor(
            m_tm[:, M * 8:(M + 1) * 8], ge2[:], gsc[:, 6:7], ge1[:],
            ALU.mult, ALU.add), waits=[t_w2v, t_m1])

    t_mtr = pr.t(lambda h: h.transpose(pmix[:, 0:P], m_tm[:], sb_iden_f[:]),
                 waits=[tok_m_last, t_pm_free])
    t_mT = pr.v(lambda h: h.tensor_copy(mT[:], pmix[:, 0:P]), waits=[t_mtr])
    t_mout = pr.dma("scalar", lambda h: h.dma_start(out=m_dram, in_=mT[:]),
                    waits=[t_mT])

    cxin = SB3("cxin", [P, 2051])
    ubuf = SB3("ubuf", [P, T])
    cxo = SB3("cxo", [P, T], BF16)
    convw_sb = SB3("convw_sb", [P, NCCH * 4])
    convb_t = SB3("convb_t", [NCCH, P])
    convb_sb = SB3("convb_sb", [P, NCCH])

    t_dtb = pr.dma("sync", lambda h: h.dma_start(out=dtb_sb[:], in_=dt_bias),
                   waits=barrier23)
    t_alog = pr.dma("sync", lambda h: h.dma_start(out=alog_sb[:, 0:1], in_=a_log))
    t_dskr = pr.dma("sync", lambda h: h.dma_start(out=dskrow[:], in_=d_skip))
    t_dskm = pr.t(lambda h: h.matmul(pmix[:, 0:HEADS], sb_ones[:], dskrow[:],
                                     start=True, stop=True),
                  waits=[t_dskr, t_mT])
    t_dskb = pr.v(lambda h: h.tensor_copy(dskb[:], pmix[:, 0:HEADS]), waits=[t_dskm])
    t_na1 = pr.s(lambda h: h.activation(alog_sb[:, 1:2], alog_sb[:, 0:1], AF.Exp),
                 waits=[t_alog])
    t_na = pr.v(lambda h: h.tensor_scalar_mul(alog_sb[:, 1:2], alog_sb[:, 1:2], -1.0),
                waits=[t_na1])
    t_p1 = pr.v(lambda h: h.tensor_scalar_add(dt3[:, T:2 * T], dt_fm[:], dtb_sb[:]),
                waits=[t_dtb] + barrier23)
    t_p2 = pr.s(lambda h: h.activation(dt3[:, 0:T], dt3[:, T:2 * T], AF.Abs),
                waits=[t_p1])
    t_p3 = pr.s(lambda h: h.activation(dt3[:, 0:T], dt3[:, 0:T],
                                       AF.Exp, scale=-1.0), waits=[t_p2])
    t_p4 = pr.s(lambda h: h.activation(dt3[:, 0:T], dt3[:, 0:T],
                                       AF.Ln, bias=1.0), waits=[t_p3])
    t_p5 = pr.v(lambda h: h.tensor_scalar_max(dt3[:, T:2 * T], dt3[:, T:2 * T], 0.0),
                waits=[t_p2])
    t_dtv = pr.v(lambda h: h.tensor_add(dt3[:, T:2 * T], dt3[:, T:2 * T],
                                        dt3[:, 0:T]), waits=[t_p4, t_p5])
    t_adt = pr.v(lambda h: h.tensor_scalar_mul(dt3[:, 0:T], dt3[:, T:2 * T],
                                               alog_sb[:, 1:2]),
                 waits=[t_dtv, t_na])
    t_lndt = t_adt

    t_cw = pr.dma("sync", lambda h: h.dma_start(
        out=convw_sb.ap().rearrange("p (c k) -> p c k", k=4),
        in_=conv_w.rearrange("(c p) k -> p c k", p=P)), waits=barrier23)
    t_cb1 = pr.dma("sync", lambda h: h.dma_start(out=convb_t[:], in_=conv_b2),
                   waits=barrier23)
    t_cb2 = pr.t(lambda h: h.transpose(pmix[:, 32:32 + NCCH], convb_t[:],
                                       sb_iden_f[0:NCCH, 0:NCCH]),
                 waits=[t_cb1, t_dskb])
    t_cb = pr.v(lambda h: h.tensor_copy(convb_sb[:], pmix[:, 32:32 + NCCH]),
                waits=[t_cb2])
    tz0 = pr.v(lambda h: h.memset(cxin[:, 0:3], 0.0), waits=barrier23)

    conv_state = {"cxin": tz0, "cxo": None, "u": None, "qd_last": None}

    def emit_conv_chunk(cc):
        t_ld = pr.dma("sync", lambda h, cc=cc: h.dma_start(
            out=cxin[:, 3:2051], in_=xbc_dram[cc * P:(cc + 1) * P, :]),
            waits=[conv_state["cxin"]])
        tu = pr.v(lambda h, cc=cc: h.tensor_scalar_mul(
            ubuf[:], cxin[:, 0:2048], convw_sb[:, cc * 4:cc * 4 + 1]),
            waits=[t_ld, t_cw, conv_state["u"]])
        for kk2 in range(1, 4):
            tu = pr.v(lambda h, cc=cc, kk2=kk2: h.scalar_tensor_tensor(
                ubuf[:], cxin[:, kk2:2048 + kk2],
                convw_sb[:, cc * 4 + kk2:cc * 4 + kk2 + 1], ubuf[:],
                ALU.mult, ALU.add), waits=[tu])
        conv_state["cxin"] = tu
        t_si = pr.s(lambda h, cc=cc: h.activation(
            cxo[:], ubuf[:], AF.Silu, bias=convb_sb[:, cc:cc + 1]),
            waits=[tu, t_cb, conv_state["cxo"]])
        conv_state["u"] = t_si
        last = None
        for q in range(4):
            last = pr.dma("scalar", lambda h, cc=cc, q=q: h.dma_start(
                out=convq_dram[q, cc * P:(cc + 1) * P, :],
                in_=cxo[:, q * 512:(q + 1) * 512]), waits=[t_si])
        conv_state["cxo"] = last
        conv_state["qd_last"] = last

    w1r = w1.rearrange("e (k p) i -> e p k i", p=P)
    w3r = w3.rearrange("e (k p) i -> e p k i", p=P)
    sw1r = sw1.rearrange("(k p) i -> p k i", p=P)
    sw3r = sw3.rearrange("(k p) i -> p k i", p=P)

    tok_wblk = [None, None]
    tok_hps = [None, None]
    tok_gps = [None, None]
    tok_sh = [None, None]
    tok_a1 = [None, None]
    tok_ast = [None] * 4
    tok_meb = None
    t_meb_frees = [t_rs2b]
    it_i = [0]
    ast_i = [0]
    blk_i = [0]

    ph_slot = [(pdub[:], 0), (pbig[:], 0)]
    pg_slot = [(pdub[:], 512), (pbig[:], 512)]

    for ei in range(9):
        routed = ei < E_RT
        row0 = ei * I_RT if routed else E_RT * I_RT
        if routed:
            t_mer = pr.dma("sync", lambda h, ei=ei: h.dma_start(
                out=apx(gwf[0:1, :], [[P, NTILE], [1, P]]),
                in_=bass.AP(m_dram.tensor, ei * P,
                            [[1, 1], [E_RT * P, NTILE], [1, P]])),
                waits=[t_mout, tok_meb])
            kk = []
            for j in range(4):
                kk.append(pr.t(lambda h, j=j: h.matmul(
                    pbig[:, j * 512:(j + 1) * 512], sb_ones[:],
                    gwf[0:1, j * 512:(j + 1) * 512], start=True, stop=True),
                    waits=[t_mer, tok_hps[1], tok_gps[1]] + t_meb_frees))
            tok_meb = pr.v(lambda h: h.tensor_copy(me_b[:], pbig[:]), waits=[kk[-1]])
            t_meb_frees = [tok_meb]
        if routed:
            nclist = [(0, 512), (512, 512), (1024, 384)]
        else:
            nclist = [(c, min(512, I_SH - c)) for c in range(0, I_SH, 512)]
        for (c0, C) in nclist:
            pa = blk_i[0] % 2
            blk_i[0] += 1
            if blk_i[0] - 1 < NCCH:
                emit_conv_chunk(blk_i[0] - 1)
            if routed:
                src1 = w1r[ei, :, :, c0:c0 + C]
                src3 = w3r[ei, :, :, c0:c0 + C]
            else:
                src1 = sw1r[:, :, c0:c0 + C]
                src3 = sw3r[:, :, c0:c0 + C]
            first2 = blk_i[0] <= 2
            t_s1 = pr.dma("gpsimd", lambda h, pa=pa, src1=src1, C=C: h.dma_start(
                out=apx(w1blk[pa][:], [[512, NK], [1, C]]), in_=src1),
                waits=[tok_wblk[pa]] + (barrier23 if first2 else []))
            t_s3 = pr.dma("gpsimd", lambda h, pa=pa, src3=src3, C=C: h.dma_start(
                out=apx(w3blk[pa][:], [[512, NK], [1, C]]), in_=src3),
                waits=[tok_wblk[pa]] + (barrier23 if first2 else []))
            last_mm = None
            for ic in range(C // P):
                for tn in range(4):
                    sp = it_i[0] % 2
                    it_i[0] += 1
                    pht, phc = ph_slot[sp]
                    pgt, pgc = pg_slot[sp]
                    hmms = []
                    gmms = []
                    for k in range(NK):
                        hmms.append(pr.t(
                            lambda h, k=k, ic=ic, tn=tn, pa=pa, pht=pht, phc=phc:
                            h.matmul(pht[:, phc:phc + 512],
                                     w1blk[pa][:, k * 512 + ic * P:k * 512 + (ic + 1) * P],
                                     xnf[:, k * T + tn * 512:k * T + (tn + 1) * 512],
                                     start=(k == 0), stop=(k == NK - 1)),
                            waits=[t_s1, tok_hps[sp]]
                            + ([t_meb_frees[0]] if routed else [])))
                    for k in range(NK):
                        gmms.append(pr.t(
                            lambda h, k=k, ic=ic, tn=tn, pa=pa, pgt=pgt, pgc=pgc:
                            h.matmul(pgt[:, pgc:pgc + 512],
                                     w3blk[pa][:, k * 512 + ic * P:k * 512 + (ic + 1) * P],
                                     xnf[:, k * T + tn * 512:k * T + (tn + 1) * 512],
                                     start=(k == 0), stop=(k == NK - 1)),
                            waits=[t_s3, tok_gps[sp]]))
                    last_mm = gmms[-1]
                    t_sh = pr.s(lambda h, sp=sp, pht=pht, phc=phc: h.activation(
                        shtile[sp][:], pht[:, phc:phc + 512], AF.Silu),
                        waits=[hmms[-1], tok_sh[sp]])
                    tok_hps[sp] = t_sh
                    ai = ast_i[0] % 4
                    ast_i[0] += 1
                    if routed:
                        t_a1 = pr.v(lambda h, sp=sp, pgt=pgt, pgc=pgc: h.tensor_mul(
                            a1tile[sp][:], shtile[sp][:], pgt[:, pgc:pgc + 512]),
                            waits=[t_sh, gmms[-1], tok_a1[sp]])
                        tok_gps[sp] = t_a1
                        tok_sh[sp] = t_a1
                        t_a2 = pr.v(lambda h, sp=sp, ai=ai, tn=tn: h.tensor_mul(
                            astage[ai][:], a1tile[sp][:],
                            me_b[:, tn * 512:(tn + 1) * 512]),
                            waits=[t_a1, tok_meb, tok_ast[ai]])
                        tok_a1[sp] = t_a2
                    else:
                        t_a2 = pr.v(lambda h, sp=sp, ai=ai, pgt=pgt, pgc=pgc:
                                    h.tensor_mul(astage[ai][:], shtile[sp][:],
                                                 pgt[:, pgc:pgc + 512]),
                                    waits=[t_sh, gmms[-1], tok_ast[ai]])
                        tok_gps[sp] = t_a2
                        tok_sh[sp] = t_a2
                    r0 = row0 + c0 + ic * P
                    tok_ast[ai] = pr.dma(
                        "scalar", lambda h, ai=ai, r0=r0, tn=tn: h.dma_start(
                            out=actT_dram[r0:r0 + P, tn * 512:(tn + 1) * 512],
                            in_=astage[ai][:]), waits=[t_a2])
            tok_wblk[pa] = last_mm

    barrier3b = [tok_ast[i] for i in range(4)] + [tok_meb] + pr.fence_all()
    if debug:
        t_dbg_xnf = pr.dma("gpsimd", lambda h: h.dma_start(out=dbg_xnf, in_=xnf[:]),
                           waits=pr.fence())
        barrier3b.append(t_dbg_xnf)
    s3.close()
    s_xnf.close()

    s4 = ExitStack()
    SB4 = mkSB(s4)
    y_t = SB4("y_t", [P, NTILE * 1024])
    atst = SB4("atst", [P, 16 * 1024], BF16)
    w2st = [SB4("w2st0", [P, 16 * 1024], BF16), SB4("w2st1", [P, 16 * 1024], BF16)]

    w2r = w2.rearrange("e (c p) n -> e p c n", p=P)
    sw2r = sw2.rearrange("(c p) n -> p c n", p=P)

    def w2_runs(cc0, ncc):
        runs = []
        cc = cc0
        while cc < cc0 + ncc:
            if cc < 88:
                e = cc // 11
                cstart = cc % 11
                ln = min(11 - cstart, cc0 + ncc - cc)
                runs.append(("w2", e, cstart, ln, cc - cc0))
            else:
                cstart = cc - 88
                ln = cc0 + ncc - cc
                runs.append(("sw2", 0, cstart, ln, cc - cc0))
            cc += ln
        return runs

    tok_w2st = [barrier3b, barrier3b]
    tok_at = [None, None]
    tok_yps = [None, None, None]
    tok_yrow = {}
    kgs = [(kg * 16, min(16, NKST - kg * 16)) for kg in range(7)]
    actT_r = actT_dram.rearrange("(c p) t -> p c t", p=P)

    for dh in range(2):
        tok_ylast = {}
        for gi, (cc0, ncc) in enumerate(kgs):
            wpa = gi % 2
            stoks = []
            ci2tok = [None] * ncc
            for (kind, e, cstart, ln, boff) in w2_runs(cc0, ncc):
                if kind == "w2":
                    src = w2r[e, :, cstart:cstart + ln, dh * 1024:(dh + 1) * 1024]
                else:
                    src = sw2r[:, cstart:cstart + ln, dh * 1024:(dh + 1) * 1024]
                tk_run = pr.dma(
                    "gpsimd", lambda h, wpa=wpa, src=src, ln=ln, boff=boff:
                    h.dma_start(out=apx(w2st[wpa][:], [[1024, ln], [1, 1024]],
                                        extra_off=boff * 1024), in_=src),
                    waits=tok_w2st[wpa])
                stoks.append(tk_run)
                for cci in range(boff, boff + ln):
                    ci2tok[cci] = tk_run
            for thalf in range(2):
                athalf_toks = []
                for ah in range(2):
                    nc_h = min(8, ncc - ah * 8)
                    if nc_h <= 0:
                        athalf_toks.append(athalf_toks[0])
                        continue
                    athalf_toks.append(pr.dma(
                        "sync", lambda h, cc0=cc0, ah=ah, nc_h=nc_h, thalf=thalf:
                        h.dma_start(
                            out=apx(atst[:], [[1024, nc_h], [1, 1024]],
                                    extra_off=ah * 8 * 1024),
                            in_=actT_r[:, cc0 + ah * 8:cc0 + ah * 8 + nc_h,
                                       thalf * 1024:(thalf + 1) * 1024]),
                        waits=[tok_at[ah]]
                        + (barrier3b if (dh == 0 and gi == 0) else [])))
                for M in range(8):
                    Mg = thalf * 8 + M
                    psl = (gi * 16 + thalf * 8 + M) % 3
                    pps = [pdub[:, 0:1024], pbig[:, 0:1024], pbig[:, 1024:2048]][psl]
                    mms = []
                    for ci in range(ncc):
                        lhs_w = [athalf_toks[ci // 8]]
                        for hh2 in range(2):
                            mms.append(pr.t(
                                lambda h, ci=ci, M=M, wpa=wpa, pps=pps, hh2=hh2:
                                h.matmul(pps[:, hh2 * 512:(hh2 + 1) * 512],
                                         atst[:, ci * 1024 + M * P:ci * 1024 + (M + 1) * P],
                                         w2st[wpa][:, ci * 1024 + hh2 * 512:ci * 1024 + (hh2 + 1) * 512],
                                         start=(ci == 0), stop=(ci == ncc - 1)),
                                waits=lhs_w + [ci2tok[ci], tok_yps[psl]]))
                    if gi == 0:
                        t_y = pr.v(lambda h, Mg=Mg, pps=pps: h.tensor_copy(
                            y_t[:, Mg * 1024:(Mg + 1) * 1024], pps),
                            waits=[mms[-1], tok_yrow.get(Mg)])
                    else:
                        t_y = pr.v(lambda h, Mg=Mg, pps=pps: h.tensor_add(
                            y_t[:, Mg * 1024:(Mg + 1) * 1024],
                            y_t[:, Mg * 1024:(Mg + 1) * 1024], pps),
                            waits=[mms[-1], tok_ylast[Mg]])
                    tok_ylast[Mg] = t_y
                    tok_yps[psl] = t_y
                    if ncc > 8:
                        tok_at[0] = mms[2 * 8 - 1]
                        tok_at[1] = mms[-1]
                    else:
                        tok_at[0] = mms[-1]
            tok_w2st[wpa] = [mms[-1]]
        for Mg in range(NTILE):
            tok_yrow[Mg] = pr.dma(
                "scalar", lambda h, Mg=Mg, dh=dh: h.dma_start(
                    out=ffn_dram[Mg * P:(Mg + 1) * P, dh * 1024:(dh + 1) * 1024],
                    in_=y_t[:, Mg * 1024:(Mg + 1) * 1024]),
                waits=[tok_ylast[Mg]])

    barrier4 = [tok_yrow[Mg] for Mg in range(NTILE)] + pr.fence_all()
    s4.close()

    s5 = ExitStack()
    SB5 = mkSB(s5)
    g_fm = SB5("g_fm", [P, NK * T], BF16)
    esel_b = SB5("esel_b", [HEADS, HEADS * P], BF16)
    t_esel = pr.dma("gpsimd", lambda h: h.dma_start(out=esel_b[:], in_=esel),
                    waits=barrier4)

    barrier4b = barrier4 + [conv_state["qd_last"], t_adt, t_dskb, t_esel]

    s5b = ExitStack()
    SB5B = mkSB(s5b)
    cq = SB5B("cq", [P, NCCH * 512], BF16)
    S_buf = [SB5B("S0", [P, 2 * T]), SB5B("S1", [P, 2 * T])]
    prev_bf = SB5B("prev_bf", [P, 2 * T], BF16)
    zt = SB5B("zt", [P, T], BF16)
    dtt = [SB5B("dtt0", [P, 64]), SB5B("dtt1", [P, 64])]
    acr = [SB5B("acr0", [P, P]), SB5B("acr1", [P, P])]
    ssm = [SB5B("ssm0", [P, P]), SB5B("ssm1", [P, P])]
    ahi = [SB5B("ahi0", [HEADS, P], BF16), SB5B("ahi1", [HEADS, P], BF16)]
    alo = [SB5B("alo0", [HEADS, P], BF16), SB5B("alo1", [HEADS, P], BF16)]
    cbm = [SB5B("cbm0", [P, P], BF16), SB5B("cbm1", [P, P], BF16)]
    xs_tm = SB5B("xs_tm", [P, T], BF16)
    btm = SB5B("btm", [P, 2 * P], BF16)
    xtb = SB5B("xtb", [P, T], BF16)
    relb = [SB5B("relb0", [P, 512]), SB5B("relb1", [P, 512])]
    mpb = [SB5B("mpb0", [P, 512], BF16), SB5B("mpb1", [P, 512], BF16)]
    ybuf = SB5B("ybuf", [P, 1024])
    ybuf2 = SB5B("ybuf2", [P, 1024])
    yfin = SB5B("yfin", [P, T])
    tbuf = SB5B("tbuf", [P, 1024])
    gst = SB5B("gst", [P, 4])
    gtbf = SB5B("gtbf", [P, T], BF16)

    t_s0 = pr.v(lambda h: h.memset(S_buf[0][:], 0.0), waits=barrier4b)
    scur = 0
    tok_S = [t_s0, None]
    tok_Sread = [[], []]
    tok_tbuf = None
    tok_ybuf2 = None
    tok_zsb = None
    tok_cq = None
    tok_zt = t_s0
    tok_prev = None
    tok_pdub_free = t_s0
    tok_pbig_free = t_s0
    tok_pmix_free = t_s0
    tok_gtr_done = None
    tok_gq_prev = None
    tok_gbf_prev = None
    war_dtt = [None, None]
    war_acr = [[], []]
    war_ssm = [None, None]
    war_ahl = [None, None]
    war_xs = [None, None]
    war_xtb = None
    war_btm = None
    tok_bcps = [None, None]
    tok_relb = [None, None]
    tok_mpb = [None, None]

    for q in range(4):
        t_cq = pr.dma("sync", lambda h, q=q: h.dma_start(
            out=cq.ap().rearrange("p (c t) -> p c t", t=512),
            in_=convq_dram[q].rearrange("(c p) t -> p c t", p=P)),
            waits=[tok_cq] + (barrier4b if q == 0 else []))
        for ppq in range(4):
            pi = q * 4 + ppq
            t0 = pi * P
            pa2 = pi % 2
            pfX = pr.fence() if SAFE & 1 else []
            tok_zt = pr.dma("gpsimd", lambda h, t0=t0: h.dma_start(
                out=zt[:], in_=z_dram[t0:t0 + P, :]), waits=[tok_zt])

            ta = []
            for j in range(2):
                ta.append(pr.t(lambda h, j=j, t0=t0: h.transpose(
                    pmix[:, j * 32:(j + 1) * 32], dt3[:, j * T + t0:j * T + t0 + P],
                    sb_iden_f[0:32, 0:32]),
                    waits=[t_adt, tok_pmix_free] + pfX))
            t_dtt0 = pr.v(lambda h, pa2=pa2: h.tensor_copy(
                dtt[pa2][:], pmix[:, 0:64]), waits=[ta[-1], war_dtt[pa2]] + pfX)
            tb = []
            for j, cst in enumerate((sb_tri, sb_ut, sb_e63, sb_e127)):
                tb.append(pr.t(lambda h, j=j, cst=cst, pa2=pa2: h.matmul(
                    pmix[:, 128 + j * 32:128 + (j + 1) * 32], cst[:],
                    dtt[pa2][:, 0:32], start=True, stop=True),
                    waits=[t_dtt0]))
            war_dtt[pa2] = tb[-1]
            t_acr = pr.v(lambda h, pa2=pa2: h.tensor_copy(
                acr[pa2][:], pmix[:, 128:256]),
                waits=[tb[-1]] + war_acr[pa2] + pfX)
            war_acr[pa2] = []
            t_eacs = pr.s(lambda h, pa2=pa2: h.activation(
                ssm[pa2][:, 0:32], acr[pa2][:, 0:32], AF.Exp),
                waits=[t_acr, war_ssm[pa2]] + pfX)
            t_wdec = pr.s(lambda h, pa2=pa2: h.activation(
                ssm[pa2][:, 32:64], acr[pa2][:, 32:64], AF.Exp), waits=[t_acr])
            t_cdb1 = pr.s(lambda h, pa2=pa2: h.activation(
                ssm[pa2][:, 64:96], acr[pa2][:, 64:96], AF.Exp), waits=[t_acr])
            t_cdb2 = pr.s(lambda h, pa2=pa2: h.activation(
                ssm[pa2][:, 96:128], acr[pa2][:, 96:128], AF.Exp), waits=[t_acr])
            war_acr[pa2].append(t_cdb2)
            t_afm = pr.t(lambda h, pa2=pa2: h.transpose(
                pmix[0:32, 256:384], acr[pa2][:, 0:32], sb_iden_f[:]),
                waits=[t_acr, t_dtt0])
            war_acr[pa2].append(t_afm)
            t_ahi = pr.v(lambda h, pa2=pa2: h.tensor_copy(
                ahi[pa2][:], pmix[0:32, 256:384]),
                waits=[t_afm, war_ahl[pa2]] + pfX)
            t_alo = pr.v(lambda h, pa2=pa2: h.tensor_tensor(
                alo[pa2][:], pmix[0:32, 256:384], ahi[pa2][:],
                ALU.subtract), waits=[t_ahi])
            te1 = pr.t(lambda h, ppq=ppq: h.matmul(
                pmix[:, 384:512], cq[:, 16 * 512 + ppq * P:16 * 512 + (ppq + 1) * P],
                cq[:, 18 * 512 + ppq * P:18 * 512 + (ppq + 1) * P],
                start=True, stop=False), waits=[t_cq, t_alo] + pfX)
            te2 = pr.t(lambda h, ppq=ppq: h.matmul(
                pmix[:, 384:512], cq[:, 17 * 512 + ppq * P:17 * 512 + (ppq + 1) * P],
                cq[:, 19 * 512 + ppq * P:19 * 512 + (ppq + 1) * P],
                start=False, stop=True), waits=[te1])
            t_cbm = pr.v(lambda h, pa2=pa2: h.tensor_mul(
                cbm[pa2][:], pmix[:, 384:512], sb_tri[:]), waits=[te2])
            tok_pmix_free = t_cbm
            ev_toks = []
            for grp in range(3):
                kl = [grp * 8 + j for j in range(8)] if grp < 2 else [16, 17]
                tts = []
                for j, k in enumerate(kl):
                    tts.append(pr.t(lambda h, j=j, k=k, ppq=ppq: h.transpose(
                        pbf[:, j * P:(j + 1) * P],
                        cq[:, k * 512 + ppq * P:k * 512 + (ppq + 1) * P],
                        sb_iden_b[:]),
                        waits=[t_cq, tok_pbf_free]))
                if grp < 2:
                    tok_pbf_free = pr.v(lambda h, grp=grp: h.tensor_copy(
                        xs_tm[:, grp * 1024:(grp + 1) * 1024], pbf[:, 0:1024]),
                        waits=[tts[-1], war_xs[grp]] + pfX)
                    war_xs[grp] = None
                else:
                    tok_pbf_free = pr.v(lambda h: h.tensor_copy(
                        btm[:], pbf[:, 0:256]), waits=[tts[-1], war_btm] + pfX)
                ev_toks.append(tok_pbf_free)
            t_xs, t_xs2, t_btm = ev_toks
            t_xt = pr.v(lambda h, pa2=pa2: h.tensor_tensor(
                apx(xtb[:], [[64, 32], [1, 64]]),
                apx(xs_tm[:], [[64, 32], [1, 64]]),
                apx(dtt[pa2][:, 32:64], [[1, 32], [0, 64]]), ALU.mult),
                waits=[t_xs, t_xs2, t_dtt0, war_xtb] + pfX)
            tc0 = []
            for ncn in range(2):
                tc0.append(pr.s(lambda h, ncn=ncn, scur=scur: h.activation(
                    prev_bf[:, ncn * T:(ncn + 1) * T],
                    S_buf[scur][:, ncn * T:(ncn + 1) * T], AF.Copy),
                    waits=[tok_S[scur], tok_prev]))
            t_prev_w = tc0[-1]
            tok_Sread[scur].append(t_prev_w)
            t_ydiag = None
            t_sub_last = None
            for hb in range(8):
                hp2 = hb % 2
                bcps = pdub[:, hp2 * 512:(hp2 + 1) * 512]
                bms = []
                for hh in range(4):
                    hg = hb * 4 + hh
                    bms.append(pr.t(lambda h, hg=hg, hh=hh, bcps=bcps, pa2=pa2:
                                    h.matmul(bcps[:, hh * P:(hh + 1) * P],
                                             esel_b[:, hg * P:(hg + 1) * P],
                                             ahi[pa2][:], start=True, stop=False),
                                    waits=[t_alo, tok_bcps[hp2], t_esel,
                                           tok_pdub_free]
                                    + (pr.fence() if SAFE & 2 else [])))
                    bms.append(pr.t(lambda h, hg=hg, hh=hh, bcps=bcps, pa2=pa2:
                                    h.matmul(bcps[:, hh * P:(hh + 1) * P],
                                             esel_b[:, hg * P:(hg + 1) * P],
                                             alo[pa2][:], start=False, stop=True),
                                    waits=[]))
                war_ahl[pa2] = bms[-1]
                t_sub = pr.v(lambda h, hb=hb, hp2=hp2, bcps=bcps, pa2=pa2:
                             h.tensor_tensor(
                                 apx(relb[hp2][:], [[128, 4], [1, 128]]),
                                 apx(bcps, [[128, 4], [1, 128]]),
                                 apx(acr[pa2][:, hb * 4:hb * 4 + 4],
                                     [[1, 4], [0, 128]]),
                                 ALU.subtract),
                             waits=[bms[-1], tok_relb[hp2]])
                t_sub_last = t_sub
                tok_bcps[hp2] = t_sub
                t_rel = pr.s(lambda h, hp2=hp2: h.activation(
                    relb[hp2][:], relb[hp2][:], AF.Relu, scale=-1.0),
                    waits=[t_sub])
                t_exp = pr.s(lambda h, hp2=hp2: h.activation(
                    relb[hp2][:], relb[hp2][:], AF.Exp, scale=-1.0),
                    waits=[t_rel])
                t_mpx = pr.v(lambda h, hp2=hp2, pa2=pa2: h.tensor_tensor(
                    apx(mpb[hp2][:], [[128, 4], [1, 128]]),
                    apx(relb[hp2][:], [[128, 4], [1, 128]]),
                    apx(cbm[pa2][:], [[0, 4], [1, 128]]), ALU.mult),
                    waits=[t_exp, t_cbm, tok_mpb[hp2]])
                tok_relb[hp2] = t_mpx
                for hh in range(4):
                    hg = hb * 4 + hh
                    t_ydiag = pr.t(lambda h, hg=hg, hh=hh, hp2=hp2: h.matmul(
                        pbig[:, hg * HD:(hg + 1) * HD],
                        mpb[hp2][:, hh * P:(hh + 1) * P],
                        xtb[:, hg * HD:(hg + 1) * HD],
                        start=True, stop=True),
                        waits=[t_mpx, t_xt, tok_pbig_free])
                tok_mpb[hp2] = t_ydiag
            t_xtw = pr.v(lambda h, pa2=pa2: h.tensor_tensor(
                apx(xtb[:], [[64, 32], [1, 64]]),
                apx(xtb[:], [[64, 32], [1, 64]]),
                apx(ssm[pa2][:, 32:64], [[1, 32], [0, 64]]), ALU.mult),
                waits=[t_xt, t_ydiag, t_wdec])
            t_asm = []
            prev_cast_toks = []
            sA, sB = scur, 1 - scur
            for c in range(2):
                src_b = sA if c == 0 else sB
                dst_b = sB if c == 0 else sA
                Lc = slice(c * 64, (c + 1) * 64)
                cdb_off = 64 + c * 32
                for r in range(2):
                    yo = []
                    for jj in range(2):
                        for ncn in range(2):
                            yo.append(pr.t(
                                lambda h, c=c, r=r, jj=jj, ncn=ncn, ppq=ppq, Lc=Lc:
                                h.matmul(
                                    pdub[Lc, jj * 512:(jj + 1) * 512],
                                    cq[:, (18 + ncn) * 512 + ppq * P + c * 64:
                                        (18 + ncn) * 512 + ppq * P + c * 64 + 64],
                                    prev_bf[:, ncn * T + r * 1024 + jj * 512:
                                            ncn * T + r * 1024 + (jj + 1) * 512],
                                    start=(ncn == 0), stop=(ncn == 1)),
                                waits=[t_prev_w, t_cq, tok_pdub_free, t_sub_last]
                                + (pr.fence() if SAFE & 4 else [])))
                    tok_prev = yo[-1]
                    a1 = pr.v(lambda h, Lc=Lc, r=r, pa2=pa2: h.tensor_tensor(
                        apx(ybuf[Lc, :], [[64, 16], [1, 64]]),
                        apx(pdub[Lc, 0:1024], [[64, 16], [1, 64]]),
                        apx(ssm[pa2][Lc, r * 16:r * 16 + 16], [[1, 16], [0, 64]]),
                        ALU.mult), waits=[yo[-1], t_eacs])
                    war_ssm[pa2] = a1
                    a2 = pr.g(lambda h, Lc=Lc, r=r: h.tensor_tensor(
                        apx(ybuf2[Lc, :], [[64, 16], [1, 64]]),
                        apx(xs_tm[Lc, r * 1024:(r + 1) * 1024], [[64, 16], [1, 64]]),
                        apx(dskb[Lc, r * 16:r * 16 + 16], [[1, 16], [0, 64]]),
                        ALU.mult), waits=[t_xs, t_xs2, t_dskb, tok_ybuf2])
                    war_xs[r] = a2
                    a3 = pr.v(lambda h, Lc=Lc: h.tensor_add(
                        ybuf[Lc, :], ybuf[Lc, :], ybuf2[Lc, :]), waits=[a1, a2])
                    tok_ybuf2 = a3
                    a4 = pr.v(lambda h, Lc=Lc, r=r: h.tensor_add(
                        yfin[Lc, r * 1024:(r + 1) * 1024], ybuf[Lc, :],
                        pbig[Lc, r * 1024:(r + 1) * 1024]),
                        waits=[a3, t_ydiag, tok_gq_prev])
                    t_asm.append(a4)
                    tok_pdub_free = a4
                for ncn in range(2):
                    for jh in range(2):
                        t_st = None
                        for jj in (jh * 2, jh * 2 + 1):
                            t_st = pr.t(lambda h, c=c, ncn=ncn, jj=jj:
                                        h.matmul(
                                            pdub[:, (jj % 2) * 512:(jj % 2 + 1) * 512],
                                            btm[c * 64:(c + 1) * 64, ncn * P:(ncn + 1) * P],
                                            xtb[c * 64:(c + 1) * 64, jj * 512:(jj + 1) * 512],
                                            start=True, stop=True),
                                        waits=[t_btm, t_xtw, tok_pdub_free])
                        war_xtb = t_st
                        war_btm = t_st
                        t_tb = pr.g(lambda h, ncn=ncn, jh=jh, src_b=src_b,
                                    cdb_off=cdb_off, pa2=pa2:
                                    h.tensor_tensor(
                                        apx(tbuf[:], [[64, 16], [1, 64]]),
                                        apx(S_buf[src_b][:, ncn * T + jh * 1024:ncn * T + (jh + 1) * 1024],
                                            [[64, 16], [1, 64]]),
                                        apx(ssm[pa2][:, cdb_off + jh * 16:cdb_off + jh * 16 + 16],
                                            [[1, 16], [0, 64]]),
                                        ALU.mult),
                                    waits=[tok_S[src_b], t_cdb1, t_cdb2,
                                           tok_tbuf])
                        war_ssm[pa2] = t_tb
                        tok_Sread[src_b].append(t_tb)
                        t_up = pr.v(lambda h, ncn=ncn, jh=jh, dst_b=dst_b:
                                    h.tensor_add(
                                        S_buf[dst_b][:, ncn * T + jh * 1024:ncn * T + (jh + 1) * 1024],
                                        tbuf[:],
                                        pdub[:, 0:1024]),
                                    waits=[t_st, t_tb] + tok_Sread[dst_b])
                        tok_pdub_free = t_up
                        tok_tbuf = t_up
                        tok_S[dst_b] = t_up
                tok_Sread[dst_b] = []
                if c == 0:
                    tc1 = []
                    for ncn in range(2):
                        tc1.append(pr.s(lambda h, ncn=ncn, sB=sB: h.activation(
                            prev_bf[:, ncn * T:(ncn + 1) * T],
                            S_buf[sB][:, ncn * T:(ncn + 1) * T], AF.Copy),
                            waits=[tok_S[sB], tok_prev]))
                    t_prev_w = tc1[-1]
                    tok_Sread[sB].append(t_prev_w)
            dump_toks = []
            if debug and pi == DBGPI:
                dump_toks.append(pr.dma("scalar", lambda h: h.dma_start(
                    out=dbg_yfin, in_=yfin[:]), waits=t_asm))
                dump_toks.append(pr.dma("scalar", lambda h: h.dma_start(
                    out=dbg_acr, in_=acr[pa2][:]), waits=[t_acr] + t_asm))
                dump_toks.append(pr.dma("gpsimd", lambda h: h.dma_start(
                    out=dbg_xtw, in_=xtb[:]), waits=[t_xtw] + t_asm))
                dump_toks.append(pr.dma("scalar", lambda h: h.dma_start(
                    out=dbg_S, in_=S_buf[sA][:]), waits=[tok_S[sA]] + t_asm))
            t_zs = pr.s(lambda h: h.activation(zt[:], zt[:], AF.Silu),
                        waits=[tok_zt, tok_zsb])
            if debug and pi == DBGPI:
                dump_toks.append(pr.dma("gpsimd", lambda h: h.dma_start(
                    out=dbg_zs, in_=zt[:]), waits=[t_zs]))
            t_yg = pr.v(lambda h: h.tensor_mul(yfin[:], yfin[:], zt[:]),
                        waits=[t_zs] + t_asm + dump_toks)
            t_gq = pr.s(lambda h: h.activation(zt[:], yfin[:], AF.Square,
                                               accum_out=gst[:, 0:1]),
                        waits=[t_yg, tok_gbf_prev] + dump_toks)
            tok_zsb = t_gq
            tok_gq_prev = t_gq
            t_gl = pr.s(lambda h: h.activation(gst[:, 1:2], gst[:, 0:1], AF.Ln,
                                               bias=c_eps, scale=c_invd),
                        waits=[t_gq])
            t_gr = pr.s(lambda h: h.activation(gst[:, 2:3], gst[:, 1:2], AF.Exp,
                                               scale=-0.5), waits=[t_gl])
            t_gbf = pr.v(lambda h: h.tensor_scalar_mul(gtbf[:], yfin[:],
                                                       gst[:, 2:3]),
                         waits=[t_gr, t_yg])
            tok_gbf_prev = t_gbf
            if debug and pi == DBGPI:
                t_dgt = pr.dma("gpsimd", lambda h: h.dma_start(
                    out=dbg_gt, in_=gtbf[:]), waits=[t_gbf])
                t_dgs = pr.dma("scalar", lambda h: h.dma_start(
                    out=dbg_gst, in_=gst[:]), waits=[t_gr])
                t_dyg = pr.dma("scalar", lambda h: h.dma_start(
                    out=dbg_yg, in_=yfin[:]), waits=[t_yg])
                t_gbf = pr.v(lambda h: h.engine_nop(),
                             waits=[t_gbf, t_dgt, t_dgs, t_dyg])
            for grp in range(2):
                tts = []
                for j in range(8):
                    k = grp * 8 + j
                    tts.append(pr.t(lambda h, j=j, k=k: h.transpose(
                        pbf[:, j * P:(j + 1) * P], gtbf[:, k * P:(k + 1) * P],
                        sb_iden_b[:]), waits=[t_gbf, tok_pbf_free]))
                off = (grp * 8) * T + t0
                tok_pbf_free = pr.v(lambda h, off=off: h.tensor_copy(
                    apx(g_fm[:], [[T, 8], [1, P]], extra_off=off),
                    apx(pbf[:], [[P, 8], [1, P]])), waits=[tts[-1]])
            tok_gtr_done = tok_pbf_free
            tok_pbig_free = t_asm[-1]
            tok_zt = t_gq
        tok_cq = tok_gtr_done

    s5b.close()
    s5c = ExitStack()
    SB5C = mkSB(s5c)
    w32st = SB5C("w32st", [P, NK * 512])
    wfull = SB5C("wfull", [P, NK * 2048], BF16)
    ustage = [SB5C("ust0", [P, 512], BF16), SB5C("ust1", [P, 512], BF16)]

    w_out_r = w_out.rearrange("(k p) c -> p k c", p=P)
    t_w32_free2 = None
    tok_wf = []
    for b in range(4):
        t_wd = pr.dma("sync", lambda h, b=b: h.dma_start(
            out=apx(w32st[:], [[512, NK], [1, 512]]),
            in_=w_out_r[:, :, b * 512:(b + 1) * 512]),
            waits=[t_w32_free2, tok_gtr_done] + (pr.fence() if b == 0 else []))
        casts2 = []
        for k in range(NK):
            casts2.append(pr.v(lambda h, b=b, k=k: h.tensor_scalar_mul(
                wfull[:, k * 2048 + b * 512:k * 2048 + (b + 1) * 512],
                w32st[:, k * 512:(k + 1) * 512], sb_gnormw[:, k:k + 1]),
                waits=[t_wd, tok_consts]))
        t_w32_free2 = casts2[-1]
        tok_wf.append(casts2[-1])

    tok_ups = [tok_pdub_free, tok_pbig_free]
    tok_ust = [None, None]
    ui = [0]
    for uc in range(NK):
        for tn in range(4):
            sp = (uc * 4 + tn) % 2
            pps = [pdub[:, 0:512], pbig[:, 0:512]][sp]
            mms = []
            for k in range(NK):
                mms.append(pr.t(lambda h, uc=uc, tn=tn, k=k, pps=pps: h.matmul(
                    pps, wfull[:, k * 2048 + uc * P:k * 2048 + (uc + 1) * P],
                    g_fm[:, k * T + tn * 512:k * T + (tn + 1) * 512],
                    start=(k == 0), stop=(k == NK - 1)),
                    waits=tok_wf + [tok_ups[sp], tok_gtr_done]))
            usl = ui[0] % 2
            ui[0] += 1
            t_ue = pr.s(lambda h, usl=usl, pps=pps: h.activation(
                ustage[usl][:], pps, AF.Copy), waits=[mms[-1], tok_ust[usl]])
            tok_ups[sp] = t_ue
            tok_ust[usl] = pr.dma(
                "scalar", lambda h, usl=usl, uc=uc, tn=tn: h.dma_start(
                    out=u_dram[uc * P:(uc + 1) * P, tn * 512:(tn + 1) * 512],
                    in_=ustage[usl][:]), waits=[t_ue])

    barrier5 = [tok_ust[0], tok_ust[1]] + pr.fence_all()
    if debug:
        t_dbg_g = pr.dma("gpsimd", lambda h: h.dma_start(out=dbg_gfm, in_=g_fm[:]),
                         waits=[tok_gtr_done] + pr.fence())
        barrier5.append(t_dbg_g)
        t_dbg_dtv = pr.dma("scalar", lambda h: h.dma_start(out=dbg_dtv, in_=dt3[:]),
                           waits=[t_adt] + pr.fence())
        barrier5.append(t_dbg_dtv)
    s5c.close()
    s5.close()

    s6 = ExitStack()
    SB6 = mkSB(s6)
    ublock = SB6("ublock", [P, NK * 1024], BF16)
    wobf = SB6("wobf", [P, NK * 2048], BF16)
    xtile = SB6("xtile", [P, D])
    ftile = SB6("ftile", [P, D])
    ostg = [SB6("ostg%d" % i, [P, 512]) for i in range(4)]

    wo_r = wo_w.rearrange("(k p) c -> p k c", p=P)
    t_wo = pr.dma("gpsimd", lambda h: h.dma_start(
        out=apx(wobf[:], [[2048, NK], [1, 2048]]), in_=wo_r), waits=barrier5)

    tok_ub = None
    tok_xf = None
    tok_og = [None] * 4
    tok_vps = [None, None, None]
    oi2 = [0]
    t_lastout = None
    u_r = u_dram.rearrange("(k p) t -> p k t", p=P)
    for thalf in range(2):
        tok_ub = pr.dma("sync", lambda h, thalf=thalf: h.dma_start(
            out=apx(ublock[:], [[1024, NK], [1, 1024]]),
            in_=u_r[:, :, thalf * 1024:(thalf + 1) * 1024]),
            waits=[tok_ub] + barrier5 + ([tok_vps[0], tok_vps[1], tok_vps[2]]
                                         if thalf else []))
        for M in range(8):
            Mg = thalf * 8 + M
            tok_xf = pr.dma("sync", lambda h, Mg=Mg: h.dma_start(
                out=xtile[:], in_=x[Mg * P:(Mg + 1) * P, :]),
                waits=[tok_xf] + (barrier5 if Mg == 0 else []))
            tok_xf = pr.dma("sync", lambda h, Mg=Mg: h.dma_start(
                out=ftile[:], in_=ffn_dram[Mg * P:(Mg + 1) * P, :]),
                waits=[tok_xf])
            last_o2 = None
            for vn in range(4):
                sp = (M * 4 + vn) % 3
                pps = [pdub[:, 0:512], pbig[:, 0:512], pbig[:, 512:1024]][sp]
                mms = []
                for k in range(NK):
                    mms.append(pr.t(lambda h, M=M, vn=vn, k=k, pps=pps: h.matmul(
                        pps, ublock[:, k * 1024 + M * P:k * 1024 + (M + 1) * P],
                        wobf[:, k * 2048 + vn * 512:k * 2048 + (vn + 1) * 512],
                        start=(k == 0), stop=(k == NK - 1)),
                        waits=[t_wo, tok_ub, tok_vps[sp]]))
                oi = oi2[0] % 4
                oi2[0] += 1
                o1 = pr.v(lambda h, oi=oi, vn=vn, pps=pps: h.tensor_add(
                    ostg[oi][:], pps, xtile[:, vn * 512:(vn + 1) * 512]),
                    waits=[mms[-1], tok_xf, tok_og[oi]])
                o2 = pr.v(lambda h, oi=oi, vn=vn: h.tensor_add(
                    ostg[oi][:], ostg[oi][:], ftile[:, vn * 512:(vn + 1) * 512]),
                    waits=[o1])
                tok_vps[sp] = o2
                last_o2 = o2
                t_lastout = pr.dma(
                    "scalar", lambda h, oi=oi, Mg=Mg, vn=vn: h.dma_start(
                        out=out[Mg * P:(Mg + 1) * P, vn * 512:(vn + 1) * 512],
                        in_=ostg[oi][:]), waits=[o2])
                tok_og[oi] = t_lastout
            tok_xf = last_o2

    tout = t_lastout
    if debug:
        tout = pr.dma("scalar", lambda h: h.dma_start(out=dbg_z, in_=z_dram),
                      waits=[tout])
        tout = pr.dma("scalar", lambda h: h.dma_start(out=dbg_xbc, in_=xbc_dram),
                      waits=[tout])
        tout = pr.dma("scalar", lambda h: h.dma_start(out=dbg_ffn, in_=ffn_dram),
                      waits=[tout])
        tout = pr.dma("scalar", lambda h: h.dma_start(out=dbg_dt, in_=dt_fm[:]),
                      waits=[tout])
        tout = pr.dma("scalar", lambda h: h.dma_start(out=dbg_srs, in_=srs[:]),
                      waits=[tout])
        tout = pr.dma("scalar", lambda h: h.dma_start(out=dbg_stat, in_=stat[:]),
                      waits=[tout])
        tout = pr.dma("scalar", lambda h: h.dma_start(out=dbg_m, in_=m_dram),
                      waits=[tout])
        tout = pr.dma("gpsimd", lambda h: h.dma_start(
            out=dbg_cq, in_=convq_dram.rearrange("q c t -> (q c) t")), waits=[tout])
        tout = pr.dma("gpsimd", lambda h: h.dma_start(out=dbg_act, in_=actT_dram),
                      waits=[tout])

    pr._op("sync", lambda h: h.nop(), [tout], "qs", False)

    s6.close()
    with es, nc.Block() as block:
        pr.finalize(nc, block, sems)
    return nc


def make_constants():
    idx = np.arange(P)
    tri = ((idx[:, None] <= idx[None, :]) &
           (idx[:, None] // CHUNK == idx[None, :] // CHUNK)).astype(np.float32)
    ut = ((idx[:, None] > idx[None, :]) &
          (idx[:, None] // CHUNK == idx[None, :] // CHUNK)).astype(np.float32)
    e63a = np.zeros((P, P), np.float32)
    e63a[0:64, :] = 1.0
    e127a = np.zeros((P, P), np.float32)
    e127a[64:128, :] = 1.0
    esl = np.zeros((HEADS, HEADS * P), np.float32)
    for hh in range(HEADS):
        esl[hh, hh * P:(hh + 1) * P] = 1.0
    return {
        "iden_f": np.eye(P, dtype=np.float32),
        "tri_cum": tri, "ut_strict": ut, "e63": e63a, "e127": e127a,
        "ones1": np.ones((1, P), np.float32),
        "esel": esl,
    }


def make_in_map(core, inputs):
    m = {
        "x": np.ascontiguousarray(inputs["L0_attn_x"][core]),
        "w_in": inputs["L0_attn_w_in"],
        "conv_w": inputs["L0_attn_conv_w"],
        "conv_b2": np.ascontiguousarray(inputs["L0_attn_conv_b"].reshape(NCCH, P)),
        "dt_bias": np.ascontiguousarray(inputs["L0_attn_dt_bias"].reshape(HEADS, 1)),
        "a_log": np.ascontiguousarray(inputs["L0_attn_A_log"].reshape(HEADS, 1)),
        "d_skip": np.ascontiguousarray(inputs["L0_attn_D_skip"].reshape(1, HEADS)),
        "normw2": np.ascontiguousarray(inputs["L0_attn_norm_w"].reshape(NK, P).T),
        "gnormw2": np.ascontiguousarray(inputs["L0_attn_gnorm_w"].reshape(NK, P).T),
        "w_out": inputs["L0_attn_w_out"],
        "wo_w": inputs["L0_attn_wo_w"],
        "gate_w": inputs["L0_ffn_gate_w"],
        "w1": inputs["L0_ffn_w1"],
        "w3": inputs["L0_ffn_w3"],
        "w2": inputs["L0_ffn_w2"],
        "sw1": inputs["L0_ffn_sw1"],
        "sw3": inputs["L0_ffn_sw3"],
        "sw2": inputs["L0_ffn_sw2"],
    }
    m.update(make_constants())
    return m


def kernel(**inputs):
    inputs = {k: np.asarray(v) for k, v in inputs.items()}
    nc = build_program()
    in_maps = [make_in_map(b, inputs) for b in range(8)]
    res = run_bass_kernel_spmd(nc, in_maps, core_ids=list(range(8)))
    return np.stack([np.asarray(res.results[b]["out"]) for b in range(8)], axis=0)
